# version 11
# baseline (speedup 1.0000x reference)
import numpy as np
from contextlib import ExitStack

import concourse.bass as bass
import concourse.mybir as mybir
from concourse.bass_utils import run_bass_kernel_spmd

F32 = mybir.dt.float32
BF16 = mybir.dt.bfloat16
AF = mybir.ActivationFunctionType
ALU = mybir.AluOpType
AX = mybir.AxisListType

MAGIC = float(np.float32(1.5 * 2.0**23))
NCORES = 8
COEF = float(np.float32((1.0 / np.sqrt(128.0)) / 1.5))



_NAME_N = [0]


def _sb(nc, name, shape, dtype):
    _NAME_N[0] += 1
    return nc.sbuf_tensor(f"{name}_{_NAME_N[0]}", shape, dtype)


def _ps(nc, name, shape, dtype):
    _NAME_N[0] += 1
    return nc.psum_tensor(f"{name}_{_NAME_N[0]}", shape, dtype)


class Counter:
    __slots__ = ("n",)

    def __init__(self, n=0):
        self.n = n

    def add(self, k=1):
        self.n += k
        return self.n


class Env:
    def __init__(self, nc, stack):
        self.nc = nc
        e = stack.enter_context
        self.sem_dY = e(nc.semaphore("dY"))
        self.sem_dG = e(nc.semaphore("dG"))
        self.sem_dYp = [e(nc.semaphore("dYa")), e(nc.semaphore("dYb"))]
        self.sem_dGp = [e(nc.semaphore("dGa")), e(nc.semaphore("dGb"))]
        self.sem_dOp = [e(nc.semaphore("dOa")), e(nc.semaphore("dOb"))]
        self.sem_dSp = [e(nc.semaphore("dSa")), e(nc.semaphore("dSb"))]
        self.sem_cT = e(nc.semaphore("cT"))
        self.sem_cV = e(nc.semaphore("cV"))
        self.sem_cS = e(nc.semaphore("cS"))
        self.sem_cc = e(nc.semaphore("cc"))
        self.sem_rc = e(nc.semaphore("rc"))
        self.sem_j = e(nc.semaphore("jn"))
        self.dY = Counter()
        self.dG = Counter()
        self.dYp = [Counter(), Counter()]
        self.dGp = [Counter(), Counter()]
        self.dOp = [Counter(), Counter()]
        self.dSp = [Counter(), Counter()]
        self.cT = Counter()
        self.cV = Counter()
        self.cS = Counter()
        self.cc = Counter()
        self.rc = Counter()
        self.joins = 0

    def join(self):
        nc = self.nc
        self.joins += 1
        target = 5 * self.joins
        dYn, dGn = 16 * self.dY.n, 16 * self.dG.n
        dYp = [16 * c.n for c in self.dYp]
        dGp = [16 * c.n for c in self.dGp]
        dOp = [16 * c.n for c in self.dOp]
        dSp = [16 * c.n for c in self.dSp]
        with nc.Block() as blk:
            def mk(name):
                def body(eng):
                    if name == "sync":
                        eng.wait_ge(self.sem_dY, dYn)
                        eng.wait_ge(self.sem_dYp[0], dYp[0])
                        eng.wait_ge(self.sem_dYp[1], dYp[1])
                        eng.wait_ge(self.sem_dOp[0], dOp[0])
                        eng.wait_ge(self.sem_dOp[1], dOp[1])
                    if name == "gpsimd":
                        eng.wait_ge(self.sem_dG, dGn)
                        eng.wait_ge(self.sem_dGp[0], dGp[0])
                        eng.wait_ge(self.sem_dGp[1], dGp[1])
                        eng.wait_ge(self.sem_dSp[0], dSp[0])
                        eng.wait_ge(self.sem_dSp[1], dSp[1])
                    eng.nop().then_inc(self.sem_j, 1)
                    eng.wait_ge(self.sem_j, target)
                return body
            blk.tensor(mk("tensor"))
            blk.vector(mk("vector"))
            blk.scalar(mk("scalar"))
            blk.gpsimd(mk("gpsimd"))
            blk.sync(mk("sync"))


def rt(env, v, producer):
    producer.then_inc(env.sem_rc, 1)
    v.wait_ge(env.sem_rc, env.rc.add(1))
    return producer


def safe_recip(env, v, out_ap, in_ap, producer):
    rt(env, v, producer)
    return v.reciprocal(out_ap, in_ap)




def emit_setup(nc, env, st, dram, sb):
    e = st.enter_context
    sb["b1_bf"] = e(_sb(nc, "b1_bf", [1, 4096], BF16))
    sb["b2_bf"] = e(_sb(nc, "b2_bf", [1, 1024], BF16))
    sb["halfbo_bf"] = e(_sb(nc, "halfbo_bf", [1, 1024], BF16))
    sb["bqkv_bc"] = e(_sb(nc, "bqkv_bc", [128, 384], F32))
    sb["mrow_lo"] = e(_sb(nc, "mrow_lo", [1, 128], BF16))
    sb["mrow_hi"] = e(_sb(nc, "mrow_hi", [1, 128], BF16))
    sb["ones_f32r"] = e(_sb(nc, "ones_f32r", [1, 128], F32))
    sb["ones_f32c"] = e(_sb(nc, "ones_f32c", [128, 1], F32))
    sb["ident"] = e(_sb(nc, "ident", [128, 128], BF16))
    sb["masks"] = e(_sb(nc, "masks", [128, 4 * 512], BF16))
    sb["wqkvT"] = e(_sb(nc, "wqkvT_sb", [128, 8 * 384], BF16))
    sb["wsc"] = e(_sb(nc, "wsc", [128, 4], F32))
    sb["parks"] = e(_sb(nc, "parks", [128, 96], F32))
    sb["crows"] = e(_sb(nc, "crows", [1, 4096], BF16))
    sb["lnred"] = e(_sb(nc, "lnred", [128, 64], F32))

    open_xqT(nc, sb)

    grp = []

    def gsb(name, shape, dt):
        cm = _sb(nc, name, shape, dt)
        t = cm.__enter__()
        grp.append(cm)
        return t

    def gps(name, shape, dt):
        cm = _ps(nc, name, shape, dt)
        t = cm.__enter__()
        grp.append(cm)
        return t

    with ExitStack() as stl:
        w1t = gsb("w1sh", [128, 4096], F32)
        w2t = gsb("w2sh", [128, 4096], F32)
        tern1 = gsb("tern1", [128, 4096], BF16)
        tern2 = gsb("tern2", [128, 4096], BF16)
        wabs = gsb("wabs", [128, 2], F32)
        wtmp = gsb("wtmp", [128, 2048], F32)
        arbuf = gsb("arbuf", [1, 8], F32)
        iot = gsb("iot", [128, 512], F32)
        bq_ps = gps("bq_ps", [128, 384], F32)
        ar_ps = gps("ar_ps", [128, 8], F32)
        sb["_wt"] = dict(grp=grp, w1t=w1t, w2t=w2t, tern1=tern1, tern2=tern2,
                         wtmp=wtmp, arbuf=arbuf, ar_ps=ar_ps)

        dY0 = env.dY.n
        dG0 = env.dG.n
        with nc.Block() as blk:
            @blk.sync
            def _(y):
                y.dma_start(out=sb["bqkv_bc"][0:1, :], in_=dram["bqkv"][:, :]).then_inc(env.sem_dY, 16)

            @blk.gpsimd
            def _(g):
                for j in range(8):
                    g.dma_start(out=w1t[:, j * 512:(j + 1) * 512],
                                in_=dram["w1Tsh"][j * 128:(j + 1) * 128, :]).then_inc(env.sem_dG, 16)
                for j in range(32):
                    g.dma_start(out=w2t[:, j * 128:(j + 1) * 128],
                                in_=dram["w2Tsh"][j * 128:(j + 1) * 128, :]).then_inc(env.sem_dG, 16)
                for j in range(8):
                    g.dma_start(out=sb["wqkvT"][:, j * 384:(j + 1) * 384],
                                in_=dram["wqkvT"][j * 128:(j + 1) * 128, :]).then_inc(env.sem_dG, 16)
                g.dma_start(out=sb["b1_bf"][:, :], in_=dram["b1r"][:, :]).then_inc(env.sem_dG, 16)
                g.dma_start(out=sb["b2_bf"][:, :], in_=dram["b2r"][:, :]).then_inc(env.sem_dG, 16)
                g.dma_start(out=sb["halfbo_bf"][:, :], in_=dram["bo_h"][:, :]).then_inc(env.sem_dG, 16)
                g.iota(iot[:, :], [[1, 512]], base=0, channel_multiplier=-1,
                       allow_small_or_imprecise_dtypes=True).then_inc(env.sem_cV, 1)
        env.dY.add(1)
        env.dG.add(51)
        cV_iota = env.cV.add(1)

        dY1 = env.dY.n
        dG1 = env.dG.n
        with nc.Block() as blk:
            @blk.vector
            def _(v):
                v.wait_ge(env.sem_cV, cV_iota)
                for o in range(4):
                    v.tensor_scalar(out=sb["masks"][:, o * 512:(o + 1) * 512],
                                    in0=iot[:, :], scalar1=float(o * 128),
                                    scalar2=None, op0=ALU.is_ge)
                v.tensor_scalar(out=sb["ident"][:, :], in0=iot[:, 0:128],
                                scalar1=0.0, scalar2=None, op0=ALU.is_equal)
                v.memset(sb["mrow_lo"][0:1, 0:64], 1.0)
                v.memset(sb["mrow_lo"][0:1, 64:128], 0.0)
                v.memset(sb["mrow_hi"][0:1, 0:64], 0.0)
                v.memset(sb["mrow_hi"][0:1, 64:128], 1.0)
                v.memset(sb["ones_f32r"][:, :], 1.0)
                v.memset(sb["ones_f32c"][:, :], 1.0)
                v.wait_ge(env.sem_dG, 16 * dG1)
                v.wait_ge(env.sem_dY, 16 * dY1)
                v.nop().then_inc(env.sem_cV, 1)

            @blk.scalar
            def _(s):
                s.wait_ge(env.sem_dG, 16 * dG1)
                s.activation(tern1[:, :], w1t[:, :], AF.Abs, bias=0.0, scale=1.0,
                             accum_out=wabs[:, 0:1]).then_inc(env.sem_cS, 1)
                s.activation(tern2[:, :], w2t[:, :], AF.Abs, bias=0.0, scale=1.0,
                             accum_out=wabs[:, 1:2]).then_inc(env.sem_cS, 1)
        cV_pre = env.cV.add(1)
        cS_abs = env.cS.add(2)

        with nc.Block() as blk:
            @blk.tensor
            def _(t):
                t.wait_ge(env.sem_cV, cV_pre)
                t.matmul(bq_ps[:, :], sb["ones_f32r"][0:1, :],
                         sb["bqkv_bc"][0:1, :], start=True, stop=True).then_inc(env.sem_cT, 1)
                t.wait_ge(env.sem_cS, cS_abs)
                t.matmul(ar_ps[0:1, 0:2], sb["ones_f32c"][:, :], wabs[:, :],
                         start=True, stop=True).then_inc(env.sem_cT, 1)
        cT_bc = env.cT.add(2)

        with nc.Block() as blk:
            @blk.vector
            def _(v):
                v.wait_ge(env.sem_cT, cT_bc - 1)
                v.tensor_copy(sb["bqkv_bc"][:, :], bq_ps[:, :])
                v.wait_ge(env.sem_cT, cT_bc)
                v.tensor_copy(arbuf[:, 0:2], ar_ps[0:1, 0:2])
                v.memset(arbuf[:, 2:8], 0.0).then_inc(env.sem_cV, 1)
        cV_ar = env.cV.add(1)

        dG2 = env.dG.n
        with nc.Block() as blk:
            @blk.gpsimd
            def _(g):
                g.wait_ge(env.sem_cV, cV_ar)
                g.dma_start(out=dram["arw_in"][:, :], in_=arbuf[:, :]).then_inc(env.sem_dG, 16)
                g.wait_ge(env.sem_dG, 16 * (dG2 + 1))
                g.collective_compute(
                    "AllReduce", ALU.add, replica_groups=[list(range(NCORES))],
                    ins=[dram["arw_in"][:, :]], outs=[dram["arw_out"][:, :]],
                ).then_inc(env.sem_cc, 1)
                g.wait_ge(env.sem_cc, env.cc.n + 1)
                g.dma_start(out=arbuf[:, :], in_=dram["arw_out"][:, :]).then_inc(env.sem_dG, 16)
        env.dG.add(2)
        env.cc.add(1)
        sb["_wt"]["dG3"] = env.dG.n


def emit_setup_tail(nc, env, dram, sb):
    wt = sb["_wt"]
    w1t, w2t, tern1, tern2 = wt["w1t"], wt["w2t"], wt["tern1"], wt["tern2"]
    wtmp, arbuf, ar_ps, dG3 = wt["wtmp"], wt["arbuf"], wt["ar_ps"], wt["dG3"]
    with nc.Block() as blk:
        @blk.vector
        def _(v):
            v.wait_ge(env.sem_dG, 16 * dG3)
            pr = v.tensor_scalar(out=arbuf[:, 0:2], in0=arbuf[:, 0:2],
                                 scalar1=1.0 / (4096.0 * 1024.0), scalar2=1e-5,
                                 op0=ALU.mult, op1=ALU.max)
            rr = safe_recip(env, v, arbuf[:, 2:4], arbuf[:, 0:2], pr)
            rt(env, v, rr)
            v.tensor_copy(sb["wsc"][0:1, 0:2], arbuf[:, 2:4])
            v.tensor_copy(sb["wsc"][0:1, 2:4], arbuf[:, 0:2]).then_inc(env.sem_cV, 1)
    cV_w0 = env.cV.add(1)
    with nc.Block() as blk:
        @blk.tensor
        def _(t):
            t.wait_ge(env.sem_cV, cV_w0)
            t.matmul(ar_ps[:, 0:4], sb["ones_f32r"][0:1, :], sb["wsc"][0:1, :],
                     start=True, stop=True).then_inc(env.sem_cT, 1)
    cT_w = env.cT.add(1)
    with nc.Block() as blk:
        @blk.vector
        def _(v):
            v.wait_ge(env.sem_cT, cT_w)
            rt(env, v, v.tensor_copy(sb["wsc"][:, :], ar_ps[:, 0:4]))
            for (wt_, tt, col) in ((w1t, tern1, 0), (w2t, tern2, 1)):
                for hf in range(2):
                    hs = slice(hf * 2048, (hf + 1) * 2048)
                    rt(env, v, v.tensor_scalar(out=wtmp[:, :], in0=wt_[:, hs],
                                               scalar1=sb["wsc"][:, col:col + 1],
                                               scalar2=None, op0=ALU.mult))
                    rt(env, v, v.tensor_scalar(out=wtmp[:, :], in0=wtmp[:, :],
                                               scalar1=1.0, scalar2=-1.0,
                                               op0=ALU.min, op1=ALU.max))
                    rt(env, v, v.tensor_scalar(out=tt[:, hs], in0=wtmp[:, :],
                                               scalar1=MAGIC, scalar2=MAGIC,
                                               op0=ALU.add, op1=ALU.subtract))
            v.nop().then_inc(env.sem_cV, 1)
    cV_tern = env.cV.add(1)

    with nc.Block() as blk:
        @blk.gpsimd
        def _(g):
            g.wait_ge(env.sem_cV, cV_tern)
            for j in range(8):
                g.dma_start(out=dram["agw1_in"][j * 128:(j + 1) * 128, :],
                            in_=tern1[:, j * 512:(j + 1) * 512]).then_inc(env.sem_dG, 16)
            for j in range(32):
                g.dma_start(out=dram["agw2_in"][j * 128:(j + 1) * 128, :],
                            in_=tern2[:, j * 128:(j + 1) * 128]).then_inc(env.sem_dG, 16)
            g.wait_ge(env.sem_dG, 16 * (dG3 + 40))
            g.collective_compute(
                "AllGather", ALU.bypass, replica_groups=[list(range(NCORES))],
                ins=[dram["agw1_in"][:, :]], outs=[dram["agw1_out"][:, :, :]],
            ).then_inc(env.sem_cc, 1)
            g.collective_compute(
                "AllGather", ALU.bypass, replica_groups=[list(range(NCORES))],
                ins=[dram["agw2_in"][:, :]], outs=[dram["agw2_out"][:, :, :]],
            ).then_inc(env.sem_cc, 1)
    env.dG.add(40)
    env.agw_ready_cc = env.cc.add(2)


def close_wtern(sb):
    for cm in reversed(sb.pop("_wt")["grp"]):
        cm.__exit__(None, None, None)


def open_xqT(nc, sb):
    cm = _sb(nc, "xqT_s", [128, 16 * 1024], BF16)
    sb["xqT"] = cm.__enter__()
    sb["_xqT_cm"] = cm


def close_xqT(sb):
    sb.pop("_xqT_cm").__exit__(None, None, None)
    sb.pop("xqT")


def emit_xchain(nc, env, st_unused, dram, sb, src_name, park_off,
                mid_emit=None, stats=False):
    P = park_off
    src = dram[src_name]
    with ExitStack() as stl:
        e = stl.enter_context
        xa = e(_sb(nc, "xc_a", [128, 16 * 1024], F32))
        red = e(_sb(nc, "xc_red", [128, 64], F32))
        sq = e(_sb(nc, "xc_sq", [128, 1024], F32))
        wk = e(_sb(nc, "xc_wk", [128, 80], F32))
        zz = e(_sb(nc, "xc_z", [128, 1024], F32))
        xq = e(_sb(nc, "xc_xq", [128, 2 * 1024], BF16))
        cpark_bf = e(_sb(nc, "xc_cpb", [128, 16], BF16))
        crow_sb = e(_sb(nc, "xc_crsb", [16, 128], BF16))
        ps_c = e(_ps(nc, "xc_ps", [128, 128], F32))

        gsems = [env.sem_dYp[0], env.sem_dYp[1], env.sem_dOp[0], env.sem_dOp[1]]
        gctr = [env.dYp[0], env.dYp[1], env.dOp[0], env.dOp[1]]
        g0 = [c.n for c in gctr]
        cV0, cS0 = env.cV.n, env.cS.n
        def reduce_half(tr):
            cvb = env.cV.n
            with nc.Block() as blk:
                if tr.start == 0:
                    @blk.sync
                    def _(y):
                        for t in range(16):
                            y.dma_start(out=xa[:, t * 1024:(t + 1) * 1024],
                                        in_=src[t * 128:(t + 1) * 128, :]).then_inc(gsems[t // 4], 16)

                @blk.vector
                def _(v):
                    for t in tr:
                        v.wait_ge(gsems[t // 4], 16 * (g0[t // 4] + 4))
                        xt = xa[:, t * 1024:(t + 1) * 1024]
                        v.tensor_reduce(out=red[:, t:t + 1], in_=xt, axis=AX.X, op=ALU.add)
                        v.tensor_reduce(out=red[:, 16 + t:17 + t], in_=xt, axis=AX.X, op=ALU.max)
                        v.tensor_reduce(out=red[:, 32 + t:33 + t], in_=xt, axis=AX.X,
                                        op=ALU.min).then_inc(env.sem_cV, 1)

                @blk.scalar
                def _(s):
                    for i, t in enumerate(tr):
                        s.wait_ge(env.sem_cV, cvb + i + 1)
                        xt = xa[:, t * 1024:(t + 1) * 1024]
                        s.activation(sq[:, :], xt, AF.Square, bias=0.0, scale=1.0,
                                     accum_out=red[:, 48 + t:49 + t]).then_inc(env.sem_cS, 1)
            env.cV.add(len(tr))

        if stats:
            red = sb["lnred"]
            with nc.Block() as blk:
                @blk.sync
                def _(y):
                    for t in range(16):
                        y.dma_start(out=xa[:, t * 1024:(t + 1) * 1024],
                                    in_=src[t * 128:(t + 1) * 128, :]).then_inc(gsems[t // 4], 16)
            for c in gctr:
                c.add(4)
            cS_red = env.cS.n
        else:
            reduce_half(range(0, 8))
            if mid_emit is not None:
                mid_emit()
            reduce_half(range(8, 16))
            for c in gctr:
                c.add(4)
            cS_red = env.cS.add(16)

        with nc.Block() as blk:
            @blk.vector
            def _(v):
                v.wait_ge(env.sem_cS, cS_red)
                m = sb["parks"][:, P:P + 16]
                v.tensor_scalar(out=m, in0=red[:, 0:16], scalar1=1.0 / 1024.0,
                                scalar2=None, op0=ALU.mult)
                rt(env, v, v.tensor_scalar(out=zz[:, 0:16], in0=red[:, 48:64],
                                           scalar1=1.0 / 1024.0, scalar2=None, op0=ALU.mult))
                rt(env, v, v.tensor_tensor(zz[:, 16:32], m, m, op=ALU.mult))
                rt(env, v, v.tensor_tensor(zz[:, 0:16], zz[:, 0:16], zz[:, 16:32],
                                           op=ALU.subtract))
                pr = v.tensor_scalar(out=wk[:, 0:16], in0=zz[:, 0:16], scalar1=1e-5,
                                     scalar2=None, op0=ALU.add)
                rr = safe_recip(env, v, wk[:, 16:32], wk[:, 0:16], pr)
                rt(env, v, rr)
                rt(env, v, v.tensor_tensor(zz[:, 16:32], zz[:, 0:16], wk[:, 16:32],
                                           op=ALU.mult))
                pr = v.tensor_scalar(out=zz[:, 16:32], in0=zz[:, 16:32], scalar1=1e-5,
                                     scalar2=None, op0=ALU.add)
                rr = safe_recip(env, v, zz[:, 32:48], zz[:, 16:32], pr)
                rt(env, v, rr)
                rt(env, v, v.tensor_tensor(wk[:, 32:48], wk[:, 16:32], zz[:, 32:48],
                                           op=ALU.mult))
                v.tensor_tensor(zz[:, 0:16], red[:, 16:32], m, op=ALU.subtract)
                rt(env, v, v.tensor_tensor(zz[:, 16:32], m, red[:, 32:48], op=ALU.subtract))
                rt(env, v, v.tensor_tensor(wk[:, 48:64], zz[:, 0:16], zz[:, 16:32],
                                           op=ALU.max))
                pr = v.tensor_scalar(out=zz[:, 0:16], in0=wk[:, 48:64], scalar1=1e-5,
                                scalar2=None, op0=ALU.max)
                rr = safe_recip(env, v, zz[:, 16:32], zz[:, 0:16], pr)
                rt(env, v, rr)
                v.tensor_scalar(out=sb["parks"][:, P + 48:P + 64], in0=zz[:, 16:32],
                                scalar1=127.0, scalar2=None,
                                op0=ALU.mult).then_inc(env.sem_cV, 1)
        cV_b1 = env.cV.add(1)

        with nc.Block() as blk:
            @blk.scalar
            def _(s):
                s.wait_ge(env.sem_cV, cV_b1)
                s.activation(wk[:, 32:48], wk[:, 32:48], AF.Sqrt, bias=0.0,
                             scale=1.0).then_inc(env.sem_cS, 1)
        cS_g = env.cS.add(1)

        with nc.Block() as blk:
            @blk.vector
            def _(v):
                v.wait_ge(env.sem_cS, cS_g)
                rt(env, v, v.tensor_tensor(wk[:, 64:80], wk[:, 32:48], wk[:, 48:64],
                                           op=ALU.mult))
                rt(env, v, v.tensor_scalar(out=wk[:, 64:80], in0=wk[:, 64:80], scalar1=1e-5,
                                           scalar2=None, op0=ALU.max))
                rt(env, v, v.tensor_scalar(out=wk[:, 64:80], in0=wk[:, 64:80],
                                           scalar1=sb["wsc"][:, 2:3], scalar2=None,
                                           op0=ALU.mult))
                pr = v.tensor_scalar(out=sb["parks"][:, P + 32:P + 48], in0=wk[:, 64:80],
                                     scalar1=1.0 / 127.0, scalar2=None, op0=ALU.mult)
                rr = safe_recip(env, v, wk[:, 0:16], sb["parks"][:, P + 32:P + 48], pr)
                rt(env, v, rr)
                v.tensor_copy(cpark_bf[:, :], wk[:, 0:16]).then_inc(env.sem_cV, 1)
        cV_cp = env.cV.add(1)

        with nc.Block() as blk:
            @blk.tensor
            def _(t):
                t.wait_ge(env.sem_cV, cV_cp)
                t.matmul(ps_c[0:16, :], cpark_bf[:, :], sb["ident"][:, :],
                         start=True, stop=True).then_inc(env.sem_cT, 1)
        cT_cp = env.cT.add(1)

        with nc.Block() as blk:
            @blk.vector
            def _(v):
                v.wait_ge(env.sem_cT, cT_cp)
                v.tensor_copy(crow_sb[:, :], ps_c[0:16, :]).then_inc(env.sem_cV, 1)
        cV_cr = env.cV.add(1)
        dG_cr0 = env.dG.n
        with nc.Block() as blk:
            @blk.gpsimd
            def _(g):
                g.wait_ge(env.sem_cV, cV_cr)
                g.dma_start(out=sb["crows"][0:1, 0:2048],
                            in_=crow_sb[:, :]).then_inc(env.sem_dG, 16)
                g.wait_ge(env.sem_dG, 16 * (dG_cr0 + 1))
        env.dG.add(1)

        cV1 = env.cV.n
        cT1 = env.cT.n
        cS1 = env.cS.n
        ps_t = [e(_ps(nc, f"xc_pt{i}", [128, 1024], F32)) for i in range(2)]
        with nc.Block() as blk:
            @blk.vector
            def _(v):
                for t in range(16):
                    if stats:
                        v.wait_ge(gsems[t // 4], 16 * (g0[t // 4] + 4))
                    if t >= 2:
                        v.wait_ge(env.sem_cT, cT1 + t - 1)
                    xt = xa[:, t * 1024:(t + 1) * 1024]
                    xo = xq[:, (t % 2) * 1024:(t % 2 + 1) * 1024]
                    for hf in range(2):
                        v.tensor_scalar(out=zz[:, hf * 512:(hf + 1) * 512],
                                        in0=xt[:, hf * 512:(hf + 1) * 512],
                                        scalar1=sb["parks"][:, P + t:P + t + 1],
                                        scalar2=sb["parks"][:, P + 48 + t:P + 49 + t],
                                        op0=ALU.subtract, op1=ALU.mult)
                    for hf in range(2):
                        pr = v.tensor_scalar(out=xo[:, hf * 512:(hf + 1) * 512],
                                             in0=zz[:, hf * 512:(hf + 1) * 512],
                                             scalar1=MAGIC, scalar2=MAGIC,
                                             op0=ALU.add, op1=ALU.subtract)
                    pr.then_inc(env.sem_cV, 1)

            @blk.tensor
            def _(t_):
                for t in range(16):
                    t_.wait_ge(env.sem_cV, cV1 + t + 1)
                    if t >= 2:
                        t_.wait_ge(env.sem_cS, cS1 + t - 1)
                    mm = None
                    for fi in range(8):
                        mm = t_.matmul(ps_t[t % 2][:, fi * 128:(fi + 1) * 128],
                                       xq[:, (t % 2) * 1024 + fi * 128:
                                          (t % 2) * 1024 + (fi + 1) * 128],
                                       sb["ident"][:, :], start=True, stop=True)
                    mm.then_inc(env.sem_cT, 1)

            @blk.scalar
            def _(s):
                for t in range(16):
                    s.wait_ge(env.sem_cT, cT1 + t + 1)
                    cp = None
                    for fi in range(8):
                        cp = s.copy(sb["xqT"][:, fi * 2048 + t * 128:
                                              fi * 2048 + (t + 1) * 128],
                                    ps_t[t % 2][:, fi * 128:(fi + 1) * 128])
                    cp.then_inc(env.sem_cS, 1)
        env.cV.add(16)
        env.cT.add(16)
        env.cS.add(16)
        env.join()


def emit_ffn(nc, env, dram, sb, res_src_name, out_dram_name, park_off,
             ln_stats=False):
    P = park_off

    with ExitStack() as stA:
        ea = stA.enter_context
        w1T = ea(_sb(nc, "fA_w1T", [128, 8 * 4096], BF16))
        xqT = sb["xqT"]
        h_raw = ea(_sb(nc, "fA_hraw", [128, 4096], F32))
        h_gel = ea(_sb(nc, "fA_hgel", [128, 2 * 4096], F32))
        hq = ea(_sb(nc, "fA_hq", [128, 4096], BF16))
        hred = ea(_sb(nc, "fA_hred", [128, 48], F32))
        hz = ea(_sb(nc, "fA_hz", [128, 4096], F32))
        hw = ea(_sb(nc, "fA_hw", [128, 64], F32))
        cpark_bf = ea(_sb(nc, "fA_cpb", [128, 16], BF16))
        crow_sb = ea(_sb(nc, "fA_crsb", [16, 128], BF16))
        ps_h = [ea(_ps(nc, f"fA_ps{i}", [128, 2048], F32)) for i in range(2)]
        sb["hredA"] = hred
        sb["hgelA"] = h_gel

        dY0, dG0, cT0, cS0, cV0 = env.dY.n, env.dG.n, env.cT.n, env.cS.n, env.cV.n
        agw_cc = env.agw_ready_cc
        with nc.Block() as blk:
            @blk.sync
            def _(y):
                y.wait_ge(env.sem_cc, agw_cc)
                w1T_v = w1T[:, :].rearrange("p (f g) -> p f g", g=4096)
                for r in range(8):
                    y.dma_start(
                        out=w1T_v[:, :, r * 512:(r + 1) * 512],
                        in_=dram["agw1_out"][r].rearrange("(f p) j -> p f j", p=128),
                    ).then_inc(env.sem_dY, 16)

            @blk.tensor
            def _(t):
                t.wait_ge(env.sem_dY, 16 * (dY0 + 8))
                for ti in range(16):
                    for s2 in range(2):
                        k = 2 * ti + s2
                        if k >= 2:
                            t2, sx = divmod(k - 2, 2)
                            t.wait_ge(env.sem_cS, cS0 + 5 * t2 + (1 if sx == 0 else 3))
                        ps = ps_h[k % 2]
                        for fi in range(8):
                            for nb in range(4):
                                fo = s2 * 2048 + nb * 512
                                t.matmul(ps[:, nb * 512:(nb + 1) * 512],
                                         xqT[:, fi * 2048 + ti * 128: fi * 2048 + (ti + 1) * 128],
                                         w1T[:, fi * 4096 + fo: fi * 4096 + fo + 512],
                                         start=(fi == 0), stop=False)
                        crow = sb["crows"][0:1, ti * 128:(ti + 1) * 128]
                        mm = None
                        for nb in range(4):
                            fo = s2 * 2048 + nb * 512
                            mm = t.matmul(ps[:, nb * 512:(nb + 1) * 512], crow,
                                          sb["b1_bf"][0:1, fo:fo + 512],
                                          start=False, stop=True)
                        mm.then_inc(env.sem_cT, 1)

            @blk.scalar
            def _(s):
                for ti in range(16):
                    if ti >= 2:
                        s.wait_ge(env.sem_cV, cV0 + ti - 1)
                    for s2 in range(2):
                        k = 2 * ti + s2
                        s.wait_ge(env.sem_cT, cT0 + k + 1)
                        ps = ps_h[k % 2]
                        hrs = h_raw[:, s2 * 2048:(s2 + 1) * 2048]
                        s.copy(hrs, ps[:, :]).then_inc(env.sem_cS, 1)
                        hgs = h_gel[:, (ti % 2) * 4096 + s2 * 2048:(ti % 2) * 4096 + (s2 + 1) * 2048]
                        s.activation(hgs, hrs, AF.Gelu, bias=0.0,
                                     scale=sb["parks"][:, P + 32 + ti:P + 33 + ti],
                                     ).then_inc(env.sem_cS, 1)
                    hg = h_gel[:, (ti % 2) * 4096:(ti % 2 + 1) * 4096]
                    s.activation(h_raw[:, :], hg, AF.Square, bias=0.0, scale=1.0,
                                 accum_out=hred[:, ti:ti + 1]).then_inc(env.sem_cS, 1)

            @blk.vector
            def _(v):
                for ti in range(16):
                    v.wait_ge(env.sem_cS, cS0 + 5 * ti + 4)
                    hg = h_gel[:, (ti % 2) * 4096:(ti % 2 + 1) * 4096]
                    rt(env, v, v.tensor_reduce(out=hred[:, 16 + ti:17 + ti], in_=hg,
                                               axis=AX.X, op=ALU.max,
                                               apply_absolute_value=True))
                    pr = v.tensor_scalar(out=hw[:, 0:1], in0=hred[:, 16 + ti:17 + ti],
                                         scalar1=1e-5, scalar2=None, op0=ALU.max)
                    rr = safe_recip(env, v, hw[:, 1:2], hw[:, 0:1], pr)
                    rt(env, v, rr)
                    rt(env, v, v.tensor_scalar(out=hw[:, 2:3], in0=hw[:, 1:2], scalar1=127.0,
                                               scalar2=None, op0=ALU.mult))
                    if ti >= 1:
                        v.wait_ge(env.sem_dG, 16 * (dG0 + ti))
                    v.tensor_scalar(out=hz[:, 0:2048], in0=hg[:, 0:2048],
                                    scalar1=hw[:, 2:3], scalar2=None, op0=ALU.mult)
                    v.tensor_scalar(out=hz[:, 2048:4096], in0=hg[:, 2048:4096],
                                    scalar1=hw[:, 2:3], scalar2=None, op0=ALU.mult)
                    v.tensor_scalar(out=hq[:, 0:2048], in0=hz[:, 0:2048],
                                    scalar1=MAGIC, scalar2=MAGIC, op0=ALU.add,
                                    op1=ALU.subtract)
                    v.tensor_scalar(out=hq[:, 2048:4096], in0=hz[:, 2048:4096],
                                    scalar1=MAGIC, scalar2=MAGIC, op0=ALU.add,
                                    op1=ALU.subtract).then_inc(env.sem_cV, 1)

            @blk.gpsimd
            def _(g):
                for ti in range(16):
                    g.wait_ge(env.sem_cV, cV0 + ti + 1)
                    g.dma_start(out=dram["hq_sc"][ti * 128:(ti + 1) * 128, :],
                                in_=hq[:, :]).then_inc(env.sem_dG, 16)
        env.dY.add(8)
        env.cT.add(32)
        env.cS.add(80)
        env.cV.add(16)
        env.dG.add(16)

        cV1 = env.cV.n
        cS_allsq = env.cS.n
        with nc.Block() as blk:
            @blk.vector
            def _(v):
                v.wait_ge(env.sem_cV, cV1)
                v.wait_ge(env.sem_cS, cS_allsq)
                pr = v.tensor_scalar(out=hw[:, 0:16], in0=hred[:, 0:16],
                                     scalar1=1.0 / 4096.0, scalar2=1e-5,
                                     op0=ALU.mult, op1=ALU.add)
                safe_recip(env, v, hw[:, 16:32], hw[:, 0:16], pr).then_inc(env.sem_cV, 1)
        cV_r = env.cV.add(1)

        with nc.Block() as blk:
            @blk.scalar
            def _(s):
                s.wait_ge(env.sem_cV, cV_r)
                s.activation(hw[:, 16:32], hw[:, 16:32], AF.Sqrt, bias=0.0,
                             scale=1.0).then_inc(env.sem_cS, 1)
        cS_r = env.cS.add(1)

        with nc.Block() as blk:
            @blk.vector
            def _(v):
                v.wait_ge(env.sem_cS, cS_r)
                rt(env, v, v.tensor_tensor(hw[:, 32:48], hw[:, 16:32], hred[:, 16:32],
                                           op=ALU.mult))
                rt(env, v, v.tensor_scalar(out=hw[:, 32:48], in0=hw[:, 32:48], scalar1=1e-5,
                                           scalar2=None, op0=ALU.max))
                rt(env, v, v.tensor_scalar(out=hw[:, 32:48], in0=hw[:, 32:48],
                                           scalar1=sb["wsc"][:, 3:4], scalar2=None,
                                           op0=ALU.mult))
                rt(env, v, v.tensor_scalar(out=hw[:, 32:48], in0=hw[:, 32:48],
                                           scalar1=1.0 / 127.0, scalar2=None,
                                           op0=ALU.mult))
                v.tensor_scalar(out=sb["parks"][:, P + 64:P + 80], in0=hw[:, 32:48],
                                scalar1=0.5, scalar2=None, op0=ALU.mult)
                rr = safe_recip(env, v, hw[:, 0:16], hw[:, 32:48],
                                v.tensor_scalar(out=hw[:, 48:49], in0=hw[:, 0:1],
                                                scalar1=0.0, scalar2=None, op0=ALU.add))
                rt(env, v, rr)
                v.tensor_copy(cpark_bf[:, :], hw[:, 0:16]).then_inc(env.sem_cV, 1)
        cV_cp = env.cV.add(1)

        with nc.Block() as blk:
            @blk.tensor
            def _(t):
                t.wait_ge(env.sem_cV, cV_cp)
                t.matmul(ps_h[0][0:16, 0:128], cpark_bf[:, :], sb["ident"][:, :],
                         start=True, stop=True).then_inc(env.sem_cT, 1)
        cT_cp = env.cT.add(1)

        with nc.Block() as blk:
            @blk.vector
            def _(v):
                v.wait_ge(env.sem_cT, cT_cp)
                v.tensor_copy(crow_sb[:, :], ps_h[0][0:16, 0:128]).then_inc(env.sem_cV, 1)
        cV_cr = env.cV.add(1)
        dG_cr0 = env.dG.n
        with nc.Block() as blk:
            @blk.gpsimd
            def _(g):
                g.wait_ge(env.sem_cV, cV_cr)
                g.dma_start(out=sb["crows"][0:1, 2048:4096],
                            in_=crow_sb[:, :]).then_inc(env.sem_dG, 16)
                g.wait_ge(env.sem_dG, 16 * (dG_cr0 + 1))
        env.dG.add(1)
        env.join()
    close_xqT(sb)

    with ExitStack() as stB:
        eb = stB.enter_context
        w2T = eb(_sb(nc, "fB_w2T", [128, 32 * 1024], BF16))
        hqT = eb(_sb(nc, "fB_hqT", [128, 2 * 16384], BF16))
        resb = eb(_sb(nc, "fB_res", [128, 2 * 1024], F32))
        outb = eb(_sb(nc, "fB_out", [128, 2 * 1024], F32))
        if ln_stats:
            sqs = eb(_sb(nc, "fB_sqs", [128, 1024], F32))
        ps_y = [eb(_ps(nc, f"fB_ps{i}", [128, 1024], F32)) for i in range(2)]

        dY0, cT0, cV0 = env.dY.n, env.cT.n, env.cV.n
        cS_sq0 = env.cS.n
        dYp0 = [env.dYp[0].n, env.dYp[1].n]
        dGp0 = [env.dGp[0].n, env.dGp[1].n]
        dSp0 = [env.dSp[0].n, env.dSp[1].n]
        res_src = dram[res_src_name]
        out_d = dram[out_dram_name]

        with nc.Block() as blk:
            @blk.sync
            def _(y):
                w2T_v = w2T[:, :].rearrange("p (f g) -> p f g", g=1024)
                for r in range(8):
                    srcv = dram["agw2_out"][r].rearrange("(f p) j -> p f j", p=128)
                    for q in range(4):
                        y.dma_start(
                            out=w2T_v[:, q * 8:(q + 1) * 8, r * 128:(r + 1) * 128],
                            in_=srcv[:, q * 8:(q + 1) * 8, :],
                        ).then_inc(env.sem_dY, 16)
                for tg in range(4):
                    if tg >= 2:
                        y.wait_ge(env.sem_cT, cT0 + 4 * (tg - 1))
                    for fi in range(32):
                        y.dma_start_transpose(
                            hqT[:, (tg % 2) * 16384 + fi * 512:(tg % 2) * 16384 + (fi + 1) * 512],
                            dram["hq_sc"][tg * 512:(tg + 1) * 512, fi * 128:(fi + 1) * 128],
                        ).then_inc(env.sem_dYp[tg % 2], 16)
            @blk.gpsimd
            def _(g):
                for t128 in range(18):
                    if t128 >= 2:
                        g.wait_ge(env.sem_cV, cV0 + t128 - 1)
                    if t128 < 16:
                        g.dma_start(out=resb[:, (t128 % 2) * 1024:(t128 % 2 + 1) * 1024],
                                    in_=res_src[t128 * 128:(t128 + 1) * 128, :],
                                    ).then_inc(env.sem_dGp[t128 % 2], 16)
                    if t128 >= 2:
                        ts_ = t128 - 2
                        g.dma_start(out=out_d[ts_ * 128:(ts_ + 1) * 128, :],
                                    in_=outb[:, (ts_ % 2) * 1024:(ts_ % 2 + 1) * 1024],
                                    ).then_inc(env.sem_dSp[ts_ % 2], 16)

            @blk.tensor
            def _(t):
                for tg in range(4):
                    t.wait_ge(env.sem_dY, 16 * (dY0 + 32))
                    t.wait_ge(env.sem_dYp[tg % 2], 16 * (dYp0[tg % 2] + 32 * (tg // 2 + 1)))
                    for tt in range(4):
                        t128 = tg * 4 + tt
                        if t128 >= 2:
                            t.wait_ge(env.sem_cV, cV0 + t128 - 1)
                        ps = ps_y[t128 % 2]
                        for fi in range(32):
                            lhs = hqT[:, (tg % 2) * 16384 + fi * 512 + tt * 128:
                                      (tg % 2) * 16384 + fi * 512 + (tt + 1) * 128]
                            t.matmul(ps[:, 0:512], lhs, w2T[:, fi * 1024:fi * 1024 + 512],
                                     start=(fi == 0), stop=False)
                            t.matmul(ps[:, 512:1024], lhs,
                                     w2T[:, fi * 1024 + 512:(fi + 1) * 1024],
                                     start=(fi == 0), stop=False)
                        crow = sb["crows"][0:1, 2048 + t128 * 128:2048 + (t128 + 1) * 128]
                        t.matmul(ps[:, 0:512], crow, sb["b2_bf"][0:1, 0:512],
                                 start=False, stop=True)
                        t.matmul(ps[:, 512:1024], crow, sb["b2_bf"][0:1, 512:1024],
                                 start=False, stop=True).then_inc(env.sem_cT, 1)

            @blk.vector
            def _(v):
                def red1(t):
                    v.tensor_reduce(out=sb["lnred"][:, t:t + 1],
                                    in_=outb[:, (t % 2) * 1024:(t % 2 + 1) * 1024],
                                    axis=AX.X, op=ALU.add)
                for t128 in range(16):
                    v.wait_ge(env.sem_cT, cT0 + t128 + 1)
                    v.wait_ge(env.sem_dGp[t128 % 2], 16 * (dGp0[t128 % 2] + t128 // 2 + 1))
                    if t128 >= 2:
                        v.wait_ge(env.sem_dSp[t128 % 2],
                                  16 * (dSp0[t128 % 2] + (t128 - 2) // 2 + 1))
                        if ln_stats:
                            v.wait_ge(env.sem_cS, cS_sq0 + t128 - 1)
                    ps = ps_y[t128 % 2]
                    v.scalar_tensor_tensor(
                        out=outb[:, (t128 % 2) * 1024:(t128 % 2 + 1) * 1024],
                        in0=ps[:, :],
                        scalar=sb["parks"][:, P + 64 + t128:P + 65 + t128],
                        in1=resb[:, (t128 % 2) * 1024:(t128 % 2 + 1) * 1024],
                        op0=ALU.mult, op1=ALU.add).then_inc(env.sem_cV, 1)
                    if ln_stats and t128 >= 1:
                        red1(t128 - 1)
                if ln_stats:
                    red1(15)

            if ln_stats:
                @blk.scalar
                def _(s):
                    for t in range(16):
                        s.wait_ge(env.sem_cV, cV0 + t + 1)
                        s.activation(sqs[:, :], outb[:, (t % 2) * 1024:(t % 2 + 1) * 1024],
                                     AF.Square, bias=0.0, scale=1.0,
                                     accum_out=sb["lnred"][:, 48 + t:49 + t],
                                     ).then_inc(env.sem_cS, 1)
        env.dY.add(32)
        env.dYp[0].add(64)
        env.dYp[1].add(64)
        env.dSp[0].add(8)
        env.dSp[1].add(8)
        env.dGp[0].add(8)
        env.dGp[1].add(8)
        env.cT.add(16)
        env.cV.add(16)
        if ln_stats:
            env.cS.add(16)
        env.join()


def emit_ln_to_bf(nc, env, dram, sb, src_name, dst_name):
    src, dst = dram[src_name], dram[dst_name]
    with ExitStack() as stl:
        e = stl.enter_context
        xa = e(_sb(nc, "ln_a", [128, 16 * 1024], F32))
        red = e(_sb(nc, "ln_red", [128, 48], F32))
        sq = e(_sb(nc, "ln_sq", [128, 1024], F32))
        ob = e(_sb(nc, "ln_o", [128, 2 * 1024], BF16))

        gsems = [env.sem_dYp[0], env.sem_dYp[1], env.sem_dOp[0], env.sem_dOp[1]]
        gctr = [env.dYp[0], env.dYp[1], env.dOp[0], env.dOp[1]]
        g0 = [c.n for c in gctr]
        with nc.Block() as blk:
            @blk.sync
            def _(y):
                for t in range(16):
                    y.dma_start(out=xa[:, t * 1024:(t + 1) * 1024],
                                in_=src[t * 128:(t + 1) * 128, :]).then_inc(gsems[t // 4], 16)
        for c in gctr:
            c.add(4)

        with nc.Block() as blk:
            @blk.vector
            def _(v):
                v.tensor_scalar(out=red[:, 0:16], in0=sb["lnred"][:, 0:16],
                                scalar1=1.0 / 1024.0, scalar2=None, op0=ALU.mult)
                rt(env, v, v.tensor_scalar(out=red[:, 16:32], in0=sb["lnred"][:, 48:64],
                                           scalar1=1.0 / 1024.0, scalar2=None,
                                           op0=ALU.mult))
                rt(env, v, v.tensor_tensor(red[:, 32:48], red[:, 0:16], red[:, 0:16],
                                           op=ALU.mult))
                rt(env, v, v.tensor_tensor(red[:, 16:32], red[:, 16:32], red[:, 32:48],
                                           op=ALU.subtract))
                v.tensor_scalar(out=red[:, 16:32], in0=red[:, 16:32], scalar1=1e-5,
                                scalar2=None, op0=ALU.add).then_inc(env.sem_cV, 1)
        cV_v = env.cV.add(1)

        with nc.Block() as blk:
            @blk.scalar
            def _(s):
                s.wait_ge(env.sem_cV, cV_v)
                s.activation(red[:, 16:32], red[:, 16:32], AF.Sqrt, bias=0.0,
                             scale=1.0).then_inc(env.sem_cS, 1)
        cS_s = env.cS.add(1)

        with nc.Block() as blk:
            @blk.vector
            def _(v):
                v.wait_ge(env.sem_cS, cS_s)
                v.reciprocal(red[:, 32:48], red[:, 16:32]).then_inc(env.sem_cV, 1)
        env.cV.add(1)

        cV1 = env.cV.n
        cT1 = env.cT.n
        cS1 = env.cS.n
        ps_t = [e(_ps(nc, f"ln_pt{i}", [128, 1024], F32)) for i in range(2)]
        with nc.Block() as blk:
            @blk.vector
            def _(v):
                for t in range(16):
                    v.wait_ge(gsems[t // 4], 16 * (g0[t // 4] + 4))
                    if t >= 2:
                        v.wait_ge(env.sem_cT, cT1 + t - 1)
                    v.tensor_scalar(out=ob[:, (t % 2) * 1024:(t % 2 + 1) * 1024],
                                    in0=xa[:, t * 1024:(t + 1) * 1024],
                                    scalar1=red[:, t:t + 1],
                                    scalar2=red[:, 32 + t:33 + t],
                                    op0=ALU.subtract, op1=ALU.mult).then_inc(env.sem_cV, 1)

            @blk.tensor
            def _(t_):
                for t in range(16):
                    t_.wait_ge(env.sem_cV, cV1 + t + 1)
                    if t >= 2:
                        t_.wait_ge(env.sem_cS, cS1 + t - 1)
                    mm = None
                    for fi in range(8):
                        mm = t_.matmul(ps_t[t % 2][:, fi * 128:(fi + 1) * 128],
                                       ob[:, (t % 2) * 1024 + fi * 128:
                                          (t % 2) * 1024 + (fi + 1) * 128],
                                       sb["ident"][:, :], start=True, stop=True)
                    mm.then_inc(env.sem_cT, 1)

            @blk.scalar
            def _(s):
                for t in range(16):
                    s.wait_ge(env.sem_cT, cT1 + t + 1)
                    cp = None
                    for fi in range(8):
                        cp = s.copy(sb["xqT"][:, fi * 2048 + t * 128:
                                              fi * 2048 + (t + 1) * 128],
                                    ps_t[t % 2][:, fi * 128:(fi + 1) * 128])
                    cp.then_inc(env.sem_cS, 1)
        env.cV.add(16)
        env.cT.add(16)
        env.cS.add(16)
        env.ln_evac_cS = cS1
        env.ln_tp_cT = cT1 + 16


def emit_agx(nc, env, dram, sb):
    dY0 = env.dY.n
    base = env.ln_evac_cS
    with nc.Block() as blk:
        @blk.sync
        def _(y):
            y.wait_ge(env.sem_cS, base + 8)
            for fi in range(8):
                y.dma_start(out=dram["agx_inA"][fi * 128:(fi + 1) * 128, :],
                            in_=sb["xqT"][:, fi * 2048:fi * 2048 + 1024],
                            ).then_inc(env.sem_dY, 16)
            y.wait_ge(env.sem_cS, base + 16)
            for fi in range(8):
                y.dma_start(out=dram["agx_inB"][fi * 128:(fi + 1) * 128, :],
                            in_=sb["xqT"][:, fi * 2048 + 1024:(fi + 1) * 2048],
                            ).then_inc(env.sem_dY, 16)

        @blk.gpsimd
        def _(g):
            g.wait_ge(env.sem_dY, 16 * (dY0 + 8))
            g.collective_compute(
                "AllGather", ALU.bypass, replica_groups=[list(range(NCORES))],
                ins=[dram["agx_inA"][:, :]], outs=[dram["agx_outA"][:, :, :]],
            ).then_inc(env.sem_cc, 1)
            g.wait_ge(env.sem_dY, 16 * (dY0 + 16))
            g.collective_compute(
                "AllGather", ALU.bypass, replica_groups=[list(range(NCORES))],
                ins=[dram["agx_inB"][:, :]], outs=[dram["agx_outB"][:, :, :]],
            ).then_inc(env.sem_cc, 1)
    env.dY.add(16)
    env.agxA_cc = env.cc.add(1)
    env.agxB_cc = env.cc.add(1)


def qkv_tile_of(k):
    if k < 64:
        return (k // 8) * 16 + (k % 8)
    return ((k - 64) // 8) * 16 + 8 + ((k - 64) % 8)


def emit_qkv(nc, env, st_att, dram, sb):
    e = st_att.enter_context
    sb["V"] = e(_sb(nc, "Vbig", [128, 128 * 130], BF16))
    sb["woT"] = e(_sb(nc, "woT_sb", [128, 8 * 1024], BF16))
    with ExitStack() as stl:
        el = stl.enter_context
        xbuf = el(_sb(nc, "qk_x", [128, 4 * 8192], BF16))
        sqd = el(_sb(nc, "qk_sqd", [128, 64], F32))
        nrm4 = el(_sb(nc, "qk_nrm4", [128, 64], F32))
        qkb = el(_sb(nc, "qk_qkb", [128, 2 * 256], BF16))
        ps = [el(_ps(nc, f"qk_ps{i}", [128, 384], F32)) for i in range(4)]

        lnS = env.ln_evac_cS + 16
        dY0, dG0, cT0, cV0, cS0 = env.dY.n, env.dG.n, env.cT.n, env.cV.n, env.cS.n

        with nc.Block() as blk:
            @blk.vector
            def _(v):
                v.wait_ge(env.sem_cS, lnS)
                vv = sb["V"][:, :].rearrange("p (t c) -> p t c", c=130)
                v.memset(vv[:, :, 64:65], 1.0)
                v.memset(vv[:, :, 129:130], 1.0)
                v.nop().then_inc(env.sem_cV, 1)

            @blk.gpsimd
            def _(g):
                g.wait_ge(env.sem_cS, lnS)
                for j in range(8):
                    g.dma_start(out=sb["woT"][:, j * 1024:(j + 1) * 1024],
                                in_=dram["woT"][j * 128:(j + 1) * 128, :]).then_inc(env.sem_dG, 16)
        env.cV.add(1)
        env.dG.add(8)
        cV0 = env.cV.n
        dG0 = env.dG.n

        NK = 128
        dYp0 = [env.dYp[0].n, env.dYp[1].n]
        dOp0 = [env.dOp[0].n, env.dOp[1].n]
        dGp0 = [env.dGp[0].n, env.dGp[1].n]
        with nc.Block() as blk:
            @blk.sync
            def _(y):
                y.wait_ge(env.sem_cS, lnS)
                y.wait_ge(env.sem_cc, env.agxA_cc)
                for r in range(8):
                    if r >= 2:
                        y.wait_ge(env.sem_cT, cT0 + 8 * (r - 1))
                    for fi in range(8):
                        y.dma_start(out=xbuf[:, (r % 2) * 8192 + fi * 1024:
                                             (r % 2) * 8192 + (fi + 1) * 1024],
                                    in_=dram["agx_outA"][r, fi * 128:(fi + 1) * 128, :],
                                    ).then_inc(env.sem_dYp[r % 2], 16)
                y.wait_ge(env.sem_cc, env.agxB_cc)
                for r in range(8):
                    if r >= 2:
                        y.wait_ge(env.sem_cT, cT0 + 64 + 8 * (r - 1))
                    for fi in range(8):
                        y.dma_start(out=xbuf[:, 16384 + (r % 2) * 8192 + fi * 1024:
                                             16384 + (r % 2) * 8192 + (fi + 1) * 1024],
                                    in_=dram["agx_outB"][r, fi * 128:(fi + 1) * 128, :],
                                    ).then_inc(env.sem_dOp[r % 2], 16)

            @blk.tensor
            def _(t):
                for k in range(NK):
                    if k < 64:
                        r, tc = k // 8, k % 8
                        if tc == 0:
                            t.wait_ge(env.sem_dYp[r % 2],
                                      16 * (dYp0[r % 2] + 8 * (r // 2) + 8))
                        col = (r % 2) * 8192 + tc * 128
                    else:
                        r, tc = (k - 64) // 8, (k - 64) % 8
                        if tc == 0:
                            t.wait_ge(env.sem_dOp[r % 2],
                                      16 * (dOp0[r % 2] + 8 * (r // 2) + 8))
                        col = 16384 + (r % 2) * 8192 + tc * 128
                    if k >= 4:
                        t.wait_ge(env.sem_cV, cV0 + (k - 4) + 1)
                    p = ps[k % 4]
                    for fi in range(8):
                        t.matmul(p[:, :],
                                 xbuf[:, col + fi * 1024:col + fi * 1024 + 128],
                                 sb["wqkvT"][:, fi * 384:(fi + 1) * 384],
                                 start=(fi == 0), stop=False)
                    t.matmul(p[:, :], sb["ones_f32r"][0:1, :],
                             sb["bqkv_bc"][0:1, :], start=False,
                             stop=True).then_inc(env.sem_cT, 1)

            @blk.scalar
            def _(s):
                def lnexp(kk):
                    sl = (kk % 4) * 16
                    s.activation(nrm4[:, sl + 4:sl + 8], nrm4[:, sl:sl + 4],
                                 AF.Ln, bias=0.0, scale=1.0)
                    s.activation(nrm4[:, sl + 8:sl + 12], nrm4[:, sl + 4:sl + 8],
                                 AF.Exp, bias=0.0,
                                 scale=-0.5).then_inc(env.sem_cS, 1)
                for k in range(NK):
                    s.wait_ge(env.sem_cT, cT0 + k + 1)
                    p = ps[k % 4]
                    sl = (k % 4) * 16
                    for h in range(4):
                        s.activation(sqd[:, :], p[:, h * 64:(h + 1) * 64],
                                     AF.Square, bias=0.0, scale=1.0,
                                     accum_out=nrm4[:, sl + h:sl + h + 1])
                    if k >= 1:
                        lnexp(k - 1)
                lnexp(NK - 1)

            @blk.vector
            def _(v):
                for k in range(NK):
                    v.wait_ge(env.sem_cS, cS0 + k + 1)
                    if k >= 2:
                        v.wait_ge(env.sem_dGp[k % 2], 16 * (dGp0[k % 2] + 2 * (k // 2)))
                    p = ps[k % 4]
                    sl = (k % 4) * 16
                    qb = qkb[:, (k % 2) * 256:(k % 2) * 256 + 256]
                    for h in range(4):
                        v.tensor_scalar(out=qb[:, h * 64:(h + 1) * 64],
                                        in0=p[:, h * 64:(h + 1) * 64],
                                        scalar1=nrm4[:, sl + 8 + h:sl + 9 + h],
                                        scalar2=None, op0=ALU.mult)
                    base = qkv_tile_of(k) * 130
                    v.tensor_copy(sb["V"][:, base:base + 64], p[:, 256:320])
                    v.tensor_copy(sb["V"][:, base + 65:base + 129],
                                  p[:, 320:384]).then_inc(env.sem_cV, 1)

            @blk.gpsimd
            def _(g):
                for k in range(NK):
                    g.wait_ge(env.sem_cV, cV0 + k + 1)
                    qb = qkb[:, (k % 2) * 256:(k % 2) * 256 + 256]
                    gt = qkv_tile_of(k)
                    g.dma_start(out=dram["qh_sc"][gt * 128:(gt + 1) * 128, :],
                                in_=qb[:, 0:128]).then_inc(env.sem_dGp[k % 2], 16)
                    g.dma_start(out=dram["kh_sc"][gt * 128:(gt + 1) * 128, :],
                                in_=qb[:, 128:256]).then_inc(env.sem_dGp[k % 2], 16)
        env.dYp[0].add(32)
        env.dYp[1].add(32)
        env.dOp[0].add(32)
        env.dOp[1].add(32)
        env.cT.add(NK)
        env.cV.add(NK)
        env.cS.add(NK)
        env.dGp[0].add(NK)
        env.dGp[1].add(NK)
        env.join()


def emit_attention(nc, env, st_att, dram, sb):
    stl = ExitStack()
    e = stl.enter_context
    qT = e(_sb(nc, "at_qT", [128, 2 * 4096], BF16))
    kT = e(_sb(nc, "at_kT", [128, 2 * 4096], BF16))
    pbf = e(_sb(nc, "at_p", [128, 3 * 1024], BF16))
    lnr = e(_sb(nc, "at_ln", [1, 1024], F32))
    rbf = e(_sb(nc, "at_rb", [1, 1024], BF16))
    oev = e(_sb(nc, "at_oe", [128, 512], F32))
    outT = e(_sb(nc, "at_outT", [128, 4096], BF16))
    ps_s = [e(_ps(nc, f"at_ps{i}", [128, 1024], F32)) for i in range(2)]
    ps_pv = e(_ps(nc, "at_pv", [128, 1024], F32))
    ps_bc = e(_ps(nc, "at_bc", [128, 512], F32))

    waves = []
    for b in range(4):
        for qc in range(8):
            for kt in range(4 * qc + 4):
                waves.append((b, qc, kt))
    NW = len(waves)
    widx = {w: i for i, w in enumerate(waves)}

    dY0, dG0 = env.dY.n, env.dG.n
    cT = Counter(env.cT.n)
    cV = Counter(env.cV.n)
    cS = Counter(env.cS.n)

    score_done = {}
    pv_done = {}
    bcast_done = {}
    exp_done = {}
    mask_done = {}
    tail1_done = {}
    evac_done = {}

    LAG = 2
    BDELAY = 2
    tev = []
    pending = []
    pvn = 0

    def _push_pv(wp):
        tev.append(("pv", wp))
        if wp[2] == 4 * wp[1] + 3:
            bd = 0 if wp[1] == 7 else BDELAY
            pending.append((pvn + 1 + bd, (wp[0], wp[1])))

    for i, w in enumerate(waves):
        tev.append(("score", w))
        if i >= LAG:
            _push_pv(waves[i - LAG])
            pvn += 1
            while pending and pending[0][0] <= pvn:
                tev.append(("bcast", pending.pop(0)[1]))
    for j in range(LAG, 0, -1):
        _push_pv(waves[-j])
        pvn += 1
        while pending and pending[0][0] <= pvn:
            tev.append(("bcast", pending.pop(0)[1]))
    while pending:
        tev.append(("bcast", pending.pop(0)[1]))

    for kind, x in tev:
        if kind == "score":
            score_done[x] = cT.add(1)
        elif kind == "pv":
            pv_done[x] = cT.add(1)
        else:
            bcast_done[x] = cT.add(1)

    tailS_done = {}
    for w in waves:
        exp_done[w] = cS.add(1)
        if w[2] == 4 * w[1] + 3:
            tailS_done[(w[0], w[1])] = cS.add(1)

    for w in waves:
        b, qc, kt = w
        if kt >= 4 * qc:
            mask_done[w] = cV.add(1)
        if kt == 4 * qc + 3:
            tail1_done[(b, qc)] = cV.add(1)
            evac_done[(b, qc)] = cV.add(1)

    def p_ready(w):
        b, qc, kt = w
        if kt >= 4 * qc:
            return (env.sem_cV, mask_done[w])
        return (env.sem_cS, exp_done[w])

    ag_dma_done = {b: 16 * (dG0 + b + 1) for b in range(4)}
    cc0 = env.cc.n
    ag_cc = {b: cc0 + b + 1 for b in range(4)}

    dYp0 = [env.dYp[0].n, env.dYp[1].n]
    trans_done = {b: (env.sem_dYp[b % 2], 16 * (dYp0[b % 2] + 2 * (b // 2 + 1)))
                  for b in range(4)}

    with nc.Block() as blk:
        @blk.sync
        def _(y):
            for b in range(4):
                if b >= 2:
                    last = (b - 2, 7, 31)
                    y.wait_ge(env.sem_cT, score_done[last])
                y.dma_start_transpose(qT[:, (b % 2) * 4096:(b % 2 + 1) * 4096],
                                      dram["qh_sc"][b * 4096:(b + 1) * 4096, :],
                                      ).then_inc(env.sem_dYp[b % 2], 16)
                y.dma_start_transpose(kT[:, (b % 2) * 4096:(b % 2 + 1) * 4096],
                                      dram["kh_sc"][b * 4096:(b + 1) * 4096, :],
                                      ).then_inc(env.sem_dYp[b % 2], 16)

        @blk.tensor
        def _(t):
            def emit_score(w):
                b, qc, kt = w
                i = widx[w]
                if i == 0 or waves[i - 1][0] != b:
                    t.wait_ge(*trans_done[b])
                if i >= 2:
                    t.wait_ge(env.sem_cS, exp_done[waves[i - 2]])
                sl = i % 2
                koff = (b % 2) * 4096 + kt * 128
                qoff = (b % 2) * 4096 + qc * 512
                t.matmul(ps_s[sl][:, 0:512], kT[0:64, koff:koff + 128],
                         qT[0:64, qoff:qoff + 512], start=True, stop=True)
                t.matmul(ps_s[sl][:, 512:1024], kT[64:128, koff:koff + 128],
                         qT[64:128, qoff:qoff + 512], start=True,
                         stop=True).then_inc(env.sem_cT, 1)

            def emit_pv(w):
                b, qc, kt = w
                i = widx[w]
                sem, val = p_ready(w)
                t.wait_ge(sem, val)
                if kt == 0 and (b, qc) != (0, 0):
                    pb, pqc = (b, qc - 1) if qc > 0 else (b - 1, 7)
                    t.wait_ge(env.sem_cV, tail1_done[(pb, pqc)])
                    t.wait_ge(env.sem_cS, tailS_done[(pb, pqc)])
                sl = i % 3
                gt = b * 32 + kt
                st_ = (kt == 0)
                sp = (kt == 4 * qc + 3)
                t.matmul(ps_pv[0:65, 0:512], sb["V"][:, gt * 130:gt * 130 + 65],
                         pbf[:, sl * 1024:sl * 1024 + 512], start=st_, stop=sp)
                t.matmul(ps_pv[0:65, 512:1024], sb["V"][:, gt * 130 + 65:gt * 130 + 130],
                         pbf[:, sl * 1024 + 512:sl * 1024 + 1024], start=st_,
                         stop=sp).then_inc(env.sem_cT, 1)

            def emit_bcast(b, qc):
                t.wait_ge(env.sem_cS, tailS_done[(b, qc)])
                t.wait_ge(env.sem_cV, tail1_done[(b, qc)])
                t.matmul(ps_bc[:, :], sb["mrow_lo"][0:1, :], rbf[0:1, 0:512],
                         start=True, stop=False)
                t.matmul(ps_bc[:, :], sb["mrow_hi"][0:1, :], rbf[0:1, 512:1024],
                         start=False, stop=True).then_inc(env.sem_cT, 1)

            for kind, x in tev:
                if kind == "score":
                    emit_score(x)
                elif kind == "pv":
                    emit_pv(x)
                else:
                    emit_bcast(*x)

        @blk.scalar
        def _(s):
            for i, w in enumerate(waves):
                s.wait_ge(env.sem_cT, score_done[w])
                if i >= 3:
                    s.wait_ge(env.sem_cT, pv_done[waves[i - 3]])
                s.activation(pbf[:, (i % 3) * 1024:(i % 3 + 1) * 1024],
                             ps_s[i % 2][:, :], AF.Exp, bias=0.0,
                             scale=COEF).then_inc(env.sem_cS, 1)
                if w[2] == 4 * w[1] + 3:
                    s.wait_ge(env.sem_cT, pv_done[w])
                    s.activation(lnr[0:1, :], ps_pv[64:65, :], AF.Ln,
                                 bias=0.0, scale=1.0)
                    s.activation(rbf[0:1, :], lnr[0:1, :], AF.Exp, bias=0.0,
                                 scale=-1.0).then_inc(env.sem_cS, 1)

        @blk.vector
        def _(v):
            for i, w in enumerate(waves):
                b, qc, kt = w
                sl = i % 3
                if kt >= 4 * qc:
                    o = kt - 4 * qc
                    v.wait_ge(env.sem_cS, exp_done[w])
                    v.tensor_tensor(pbf[:, sl * 1024:sl * 1024 + 512],
                                    pbf[:, sl * 1024:sl * 1024 + 512],
                                    sb["masks"][:, o * 512:(o + 1) * 512], op=ALU.mult)
                    v.tensor_tensor(pbf[:, sl * 1024 + 512:(sl + 1) * 1024],
                                    pbf[:, sl * 1024 + 512:(sl + 1) * 1024],
                                    sb["masks"][:, o * 512:(o + 1) * 512],
                                    op=ALU.mult).then_inc(env.sem_cV, 1)
                if kt == 4 * qc + 3:
                    v.wait_ge(env.sem_cT, pv_done[w])
                    if qc == 0 and b >= 1:
                        v.wait_ge(env.sem_dG, ag_dma_done[b - 1])
                    v.tensor_copy(oev[0:64, :], ps_pv[0:64, 0:512])
                    v.tensor_copy(oev[64:128, :],
                                  ps_pv[0:64, 512:1024]).then_inc(env.sem_cV, 1)
                    v.wait_ge(env.sem_cT, bcast_done[(b, qc)])
                    v.tensor_tensor(outT[:, qc * 512:(qc + 1) * 512], oev[:, :],
                                    ps_bc[:, :], op=ALU.mult).then_inc(env.sem_cV, 1)

        @blk.gpsimd
        def _(g):
            for b in range(4):
                g.wait_ge(env.sem_cV, evac_done[(b, 7)])
                g.dma_start(out=dram["ago_in"][b, :, :], in_=outT[:, :],
                            ).then_inc(env.sem_dG, 16)
                g.wait_ge(env.sem_dG, ag_dma_done[b])
                g.collective_compute(
                    "AllGather", ALU.bypass, replica_groups=[list(range(NCORES))],
                    ins=[dram["ago_in"][b, :, :]], outs=[dram["ago_out"][b, :, :, :]],
                ).then_inc(env.sem_cc, 1)

    env.dYp[0].add(4)
    env.dYp[1].add(4)
    env.dG.add(4)
    env.cT.n = cT.n
    env.cS.n = cS.n
    env.cV.n = cV.n
    env.cc.add(4)
    env.ago_ready_cc = env.cc.n
    env.join()
    stl.close()


def emit_wo(nc, env, st_att, dram, sb):
    with ExitStack() as stl:
        e = stl.enter_context
        om = e(_sb(nc, "wo_om", [128, 8 * 2048], BF16))
        x1b = e(_sb(nc, "wo_x1", [128, 2 * 1024], F32))
        x2b = e(_sb(nc, "wo_x2", [128, 2 * 1024], F32))
        sqs = e(_sb(nc, "wo_sqs", [128, 1024], F32))
        ps_y = [e(_ps(nc, f"wo_ps{i}", [128, 1024], F32)) for i in range(2)]

        dY0, dG0, cT0, cV0 = env.dY.n, env.dG.n, env.cT.n, env.cV.n
        cS_sq0 = env.cS.n
        dYp0 = [env.dYp[0].n, env.dYp[1].n]
        dGp0 = [env.dGp[0].n, env.dGp[1].n]
        ago_cc = env.ago_ready_cc
        with nc.Block() as blk:
            @blk.gpsimd
            def _(g):
                g.wait_ge(env.sem_cc, ago_cc)
                with g.register("rk") as rk, g.register("r2") as r2, \
                        g.register("rh") as rh, g.register("ro") as ro:
                    g.cc_rank_ld(rk, replica_groups=[list(range(NCORES))])
                    g.reg_div(r2, rk, 2)
                    g.reg_mul(r2, r2, 8 * 128 * 4096)
                    g.reg_mod(rh, rk, 2)
                    g.reg_mul(rh, rh, 2048)
                    g.reg_add(r2, r2, rh)
                    for r in range(8):
                        g.reg_mov(ro, r * 128 * 4096)
                        g.reg_add(ro, ro, r2)
                        g.dma_start(
                            out=om[:, r * 2048:(r + 1) * 2048],
                            in_=bass.AP(dram["ago_out"], ro, [[4096, 128], [1, 2048]]),
                        ).then_inc(env.sem_dG, 16)
                for t128 in range(16):
                    g.wait_ge(env.sem_cV, cV0 + t128 + 1)
                    g.dma_start(out=dram["x2_sp"][t128 * 128:(t128 + 1) * 128, :],
                                in_=x2b[:, (t128 % 2) * 1024:(t128 % 2 + 1) * 1024],
                                ).then_inc(env.sem_dGp[t128 % 2], 16)

            @blk.sync
            def _(y):
                for t128 in range(16):
                    if t128 >= 2:
                        y.wait_ge(env.sem_cV, cV0 + t128 - 1)
                    y.dma_start(out=x1b[:, (t128 % 2) * 1024:(t128 % 2 + 1) * 1024],
                                in_=dram["x1_sp"][t128 * 128:(t128 + 1) * 128, :],
                                ).then_inc(env.sem_dYp[t128 % 2], 16)

            @blk.tensor
            def _(t):
                t.wait_ge(env.sem_dG, 16 * (dG0 + 8))
                for t128 in range(16):
                    if t128 >= 2:
                        t.wait_ge(env.sem_cV, cV0 + t128 - 1)
                    ps = ps_y[t128 % 2]
                    for r in range(8):
                        lhs = om[:, r * 2048 + t128 * 128: r * 2048 + (t128 + 1) * 128]
                        t.matmul(ps[:, 0:512], lhs, sb["woT"][:, r * 1024:r * 1024 + 512],
                                 start=(r == 0), stop=False)
                        t.matmul(ps[:, 512:1024], lhs,
                                 sb["woT"][:, r * 1024 + 512:(r + 1) * 1024],
                                 start=(r == 0), stop=False)
                    t.matmul(ps[:, 0:512], sb["ones_row"][0:1, :],
                             sb["halfbo_bf"][0:1, 0:512], start=False, stop=True)
                    t.matmul(ps[:, 512:1024], sb["ones_row"][0:1, :],
                             sb["halfbo_bf"][0:1, 512:1024], start=False,
                             stop=True).then_inc(env.sem_cT, 1)

            @blk.vector
            def _(v):
                def red3(t):
                    xs = x2b[:, (t % 2) * 1024:(t % 2 + 1) * 1024]
                    v.tensor_reduce(out=sb["lnred"][:, t:t + 1], in_=xs,
                                    axis=AX.X, op=ALU.add)
                    v.tensor_reduce(out=sb["lnred"][:, 16 + t:17 + t], in_=xs,
                                    axis=AX.X, op=ALU.max)
                    v.tensor_reduce(out=sb["lnred"][:, 32 + t:33 + t], in_=xs,
                                    axis=AX.X, op=ALU.min)
                for t128 in range(16):
                    v.wait_ge(env.sem_cT, cT0 + t128 + 1)
                    v.wait_ge(env.sem_dYp[t128 % 2], 16 * (dYp0[t128 % 2] + t128 // 2 + 1))
                    if t128 >= 2:
                        v.wait_ge(env.sem_dGp[t128 % 2],
                                  16 * (dGp0[t128 % 2] + (t128 - 2) // 2 + 1))
                        v.wait_ge(env.sem_cS, cS_sq0 + t128 - 1)
                    v.scalar_tensor_tensor(
                        out=x2b[:, (t128 % 2) * 1024:(t128 % 2 + 1) * 1024],
                        in0=ps_y[t128 % 2][:, :], scalar=0.5,
                        in1=x1b[:, (t128 % 2) * 1024:(t128 % 2 + 1) * 1024],
                        op0=ALU.mult, op1=ALU.add).then_inc(env.sem_cV, 1)
                    if t128 >= 1:
                        red3(t128 - 1)
                red3(15)

            @blk.scalar
            def _(s):
                for t in range(16):
                    s.wait_ge(env.sem_cV, cV0 + t + 1)
                    s.activation(sqs[:, :], x2b[:, (t % 2) * 1024:(t % 2 + 1) * 1024],
                                 AF.Square, bias=0.0, scale=1.0,
                                 accum_out=sb["lnred"][:, 48 + t:49 + t],
                                 ).then_inc(env.sem_cS, 1)
        env.dYp[0].add(8)
        env.dYp[1].add(8)
        env.dG.add(8)
        env.dGp[0].add(8)
        env.dGp[1].add(8)
        env.cT.add(16)
        env.cV.add(16)
        env.cS.add(16)
        env.join()


def build_program(stop_after=None, debug_dump=None):
    nc = bass.Bass("TRN2", target_bir_lowering=False, detect_race_conditions=False)
    dram = {}
    dram["x0"] = nc.dram_tensor("x0", [2048, 1024], F32, kind="ExternalInput")
    dram["wqkvT"] = nc.dram_tensor("wqkvT", [1024, 384], F32, kind="ExternalInput")
    dram["bqkv"] = nc.dram_tensor("bqkv", [1, 384], F32, kind="ExternalInput")
    dram["woT"] = nc.dram_tensor("woT", [1024, 1024], F32, kind="ExternalInput")
    dram["bo_h"] = nc.dram_tensor("bo_h", [1, 1024], F32, kind="ExternalInput")
    dram["w1Tsh"] = nc.dram_tensor("w1Tsh", [1024, 512], F32, kind="ExternalInput")
    dram["b1r"] = nc.dram_tensor("b1r", [1, 4096], F32, kind="ExternalInput")
    dram["w2Tsh"] = nc.dram_tensor("w2Tsh", [4096, 128], F32, kind="ExternalInput")
    dram["b2r"] = nc.dram_tensor("b2r", [1, 1024], F32, kind="ExternalInput")
    dram["out"] = nc.dram_tensor("out", [2048, 1024], F32, kind="ExternalOutput")

    dram["arw_in"] = nc.dram_tensor("arw_in", [1, 8], F32)
    dram["arw_out"] = nc.dram_tensor("arw_out", [1, 8], F32)
    dram["agw1_in"] = nc.dram_tensor("agw1_in", [1024, 512], BF16)
    dram["agw1_out"] = nc.dram_tensor("agw1_out", [8, 1024, 512], BF16, addr_space="Shared")
    dram["agw2_in"] = nc.dram_tensor("agw2_in", [4096, 128], BF16)
    dram["agw2_out"] = nc.dram_tensor("agw2_out", [8, 4096, 128], BF16, addr_space="Shared")
    dram["xq_sc"] = nc.dram_tensor("xq_sc", [2048, 1024], BF16)
    dram["hq_sc"] = nc.dram_tensor("hq_sc", [2048, 4096], BF16)
    dram["agx_inA"] = nc.dram_tensor("agx_inA", [1024, 1024], BF16)
    dram["agx_inB"] = nc.dram_tensor("agx_inB", [1024, 1024], BF16)
    dram["agx_outA"] = nc.dram_tensor("agx_outA", [8, 1024, 1024], BF16, addr_space="Shared")
    dram["agx_outB"] = nc.dram_tensor("agx_outB", [8, 1024, 1024], BF16, addr_space="Shared")
    dram["qh_sc"] = nc.dram_tensor("qh_sc", [16384, 128], BF16)
    dram["kh_sc"] = nc.dram_tensor("kh_sc", [16384, 128], BF16)
    dram["ago_in"] = nc.dram_tensor("ago_in", [4, 128, 4096], BF16)
    dram["ago_out"] = nc.dram_tensor("ago_out", [4, 8, 128, 4096], BF16, addr_space="Shared")
    dram["x1_sp"] = nc.dram_tensor("x1_sp", [2048, 1024], F32)
    dram["x2_sp"] = nc.dram_tensor("x2_sp", [2048, 1024], F32)

    with ExitStack() as st:
        env = Env(nc, st)
        sb = {}
        sb["ones_row"] = st.enter_context(_sb(nc, "ones_row", [1, 128], BF16))
        emit_setup(nc, env, st, dram, sb)
        with nc.Block() as blk:
            @blk.vector
            def _(v):
                v.memset(sb["ones_row"][:, :], 1.0).then_inc(env.sem_cV, 1)
        env.cV.add(1)

        def _fin():
            if debug_dump is not None:
                env.join()
                with nc.Block() as blk:
                    @blk.gpsimd
                    def _(g):
                        if debug_dump.startswith("@"):
                            src_s = sb[debug_dump[1:]]
                            w_ = src_s.shape[1]
                            if w_ <= 1024:
                                g.dma_start(out=dram["out"][0:128, 0:w_],
                                            in_=src_s[:, :]).then_inc(env.sem_dG, 16)
                            else:
                                nrow = min(2048, (128 * w_) // 1024)
                                g.dma_start(
                                    out=dram["out"][0:nrow, :],
                                    in_=src_s[:, 0:(nrow * 1024) // 128],
                                    ).then_inc(env.sem_dG, 16)
                        else:
                            nd = dram[debug_dump]
                            tot = 1
                            for d_ in nd.shape:
                                tot *= d_
                            nr = min(2048, tot // 1024)
                            g.dma_start(out=dram["out"][0:nr, :],
                                        in_=bass.AP(nd, 0, [[1024, nr], [1, 1024]]),
                                        ).then_inc(env.sem_dG, 16)
                env.dG.add(1)
            with nc.Block() as blk:
                @blk.sync
                def _(y):
                    y.wait_ge(env.sem_dY, 16 * env.dY.n)

                @blk.gpsimd
                def _(g):
                    g.wait_ge(env.sem_dG, 16 * env.dG.n)
                    g.wait_ge(env.sem_cc, env.cc.n)

        emit_xchain(nc, env, st, dram, sb, "x0", park_off=0,
                    mid_emit=lambda: emit_setup_tail(nc, env, dram, sb))
        close_wtern(sb)
        if stop_after == "xchain1":
            _fin(); close_xqT(sb); return nc
        emit_ffn(nc, env, dram, sb, "x0", "x1_sp", park_off=0, ln_stats=True)
        if stop_after == "ffn1":
            _fin(); return nc
        open_xqT(nc, sb)
        emit_ln_to_bf(nc, env, dram, sb, "x1_sp", "xq_sc")
        if stop_after == "ln2":
            _fin(); close_xqT(sb); return nc
        emit_agx(nc, env, dram, sb)
        if stop_after == "agx":
            _fin(); close_xqT(sb); return nc
        stop_now = False
        with ExitStack() as st_att:
            emit_qkv(nc, env, st_att, dram, sb)
            if stop_after == "qkv":
                _fin(); stop_now = True
            if not stop_now:
                emit_attention(nc, env, st_att, dram, sb)
                if stop_after == "attn":
                    _fin(); stop_now = True
            if not stop_now:
                emit_wo(nc, env, st_att, dram, sb)
                env.join()
                if stop_after == "wo":
                    _fin(); stop_now = True
        close_xqT(sb)
        if stop_now:
            return nc
        open_xqT(nc, sb)
        emit_xchain(nc, env, st, dram, sb, "x2_sp", park_off=0, stats=True)
        if stop_after == "xchain3":
            _fin(); close_xqT(sb); return nc
        emit_ffn(nc, env, dram, sb, "x2_sp", "out", park_off=0)

        with nc.Block() as blk:
            @blk.sync
            def _(y):
                y.wait_ge(env.sem_dY, 16 * env.dY.n)

            @blk.gpsimd
            def _(g):
                g.wait_ge(env.sem_dG, 16 * env.dG.n)
    return nc


_CACHE = {}


def kernel(L0_attn_x, L0_attn_Wq, L0_attn_bq, L0_attn_Wk, L0_attn_bk,
           L0_attn_Wv, L0_attn_bv, L0_attn_Wo, L0_attn_bo,
           L0_ffn_w1, L0_ffn_b1, L0_ffn_w2, L0_ffn_b2):
    x = np.asarray(L0_attn_x, dtype=np.float32)
    Wq = np.asarray(L0_attn_Wq, np.float32)
    Wk = np.asarray(L0_attn_Wk, np.float32)
    Wv = np.asarray(L0_attn_Wv, np.float32)
    Wo = np.asarray(L0_attn_Wo, np.float32)
    bq = np.asarray(L0_attn_bq, np.float32)
    bk = np.asarray(L0_attn_bk, np.float32)
    bv = np.asarray(L0_attn_bv, np.float32)
    bo = np.asarray(L0_attn_bo, np.float32)
    w1 = np.asarray(L0_ffn_w1, np.float32)
    b1 = np.asarray(L0_ffn_b1, np.float32)
    w2 = np.asarray(L0_ffn_w2, np.float32)
    b2 = np.asarray(L0_ffn_b2, np.float32)

    if "nc" not in _CACHE:
        _CACHE["nc"] = build_program()
    nc = _CACHE["nc"]
    in_maps = make_in_maps(x, Wq, bq, Wk, bk, Wv, bv, Wo, bo, w1, b1, w2, b2)
    res = run_bass_kernel_spmd(nc, in_maps, core_ids=list(range(NCORES)))
    out = np.empty((4, 4096, 1024), np.float32)
    for c in range(NCORES):
        out[c // 2, (c % 2) * 2048:((c % 2) + 1) * 2048, :] = res.results[c]["out"]
    return out


def make_in_maps(x, Wq, bq, Wk, bk, Wv, bv, Wo, bo, w1, b1, w2, b2):
    w1T = np.ascontiguousarray(w1.T)
    w2T = np.ascontiguousarray(w2.T)
    woT = np.ascontiguousarray(Wo.T)
    in_maps = []
    for c in range(NCORES):
        sbq, half = c // 2, c % 2
        hsl = slice(128 * c, 128 * (c + 1))
        in_maps.append({
            "x0": np.ascontiguousarray(x[sbq, half * 2048:(half + 1) * 2048, :]),
            "wqkvT": np.ascontiguousarray(
                np.concatenate([Wq[hsl], Wk[hsl], Wv[hsl]], axis=0).T),
            "bqkv": np.concatenate([bq[hsl], bk[hsl], bv[hsl]])[None, :].copy(),
            "woT": woT,
            "bo_h": bo[None, :].copy(),
            "w1Tsh": np.ascontiguousarray(w1T[:, 512 * c:512 * (c + 1)]),
            "b1r": b1[None, :].copy(),
            "w2Tsh": np.ascontiguousarray(w2T[:, 128 * c:128 * (c + 1)]),
            "b2r": b2[None, :].copy(),
        })
    return in_maps


# version 12 (final)
# speedup vs baseline: 1.0070x; 1.0070x over previous
import numpy as np
from contextlib import ExitStack

import concourse.bass as bass
import concourse.mybir as mybir
from concourse.bass_utils import run_bass_kernel_spmd

F32 = mybir.dt.float32
BF16 = mybir.dt.bfloat16
AF = mybir.ActivationFunctionType
ALU = mybir.AluOpType
AX = mybir.AxisListType

MAGIC = float(np.float32(1.5 * 2.0**23))
NCORES = 8
COEF = float(np.float32((1.0 / np.sqrt(128.0)) / 1.5))



_NAME_N = [0]


def _sb(nc, name, shape, dtype):
    _NAME_N[0] += 1
    return nc.sbuf_tensor(f"{name}_{_NAME_N[0]}", shape, dtype)


def _ps(nc, name, shape, dtype):
    _NAME_N[0] += 1
    return nc.psum_tensor(f"{name}_{_NAME_N[0]}", shape, dtype)


class Counter:
    __slots__ = ("n",)

    def __init__(self, n=0):
        self.n = n

    def add(self, k=1):
        self.n += k
        return self.n


class Env:
    def __init__(self, nc, stack):
        self.nc = nc
        e = stack.enter_context
        self.sem_dY = e(nc.semaphore("dY"))
        self.sem_dG = e(nc.semaphore("dG"))
        self.sem_dYp = [e(nc.semaphore("dYa")), e(nc.semaphore("dYb"))]
        self.sem_dGp = [e(nc.semaphore("dGa")), e(nc.semaphore("dGb"))]
        self.sem_dOp = [e(nc.semaphore("dOa")), e(nc.semaphore("dOb"))]
        self.sem_dSp = [e(nc.semaphore("dSa")), e(nc.semaphore("dSb"))]
        self.sem_cT = e(nc.semaphore("cT"))
        self.sem_cV = e(nc.semaphore("cV"))
        self.sem_cS = e(nc.semaphore("cS"))
        self.sem_cc = e(nc.semaphore("cc"))
        self.sem_rc = e(nc.semaphore("rc"))
        self.sem_j = e(nc.semaphore("jn"))
        self.dY = Counter()
        self.dG = Counter()
        self.dYp = [Counter(), Counter()]
        self.dGp = [Counter(), Counter()]
        self.dOp = [Counter(), Counter()]
        self.dSp = [Counter(), Counter()]
        self.cT = Counter()
        self.cV = Counter()
        self.cS = Counter()
        self.cc = Counter()
        self.rc = Counter()
        self.joins = 0

    def join(self):
        nc = self.nc
        self.joins += 1
        target = 5 * self.joins
        dYn, dGn = 16 * self.dY.n, 16 * self.dG.n
        dYp = [16 * c.n for c in self.dYp]
        dGp = [16 * c.n for c in self.dGp]
        dOp = [16 * c.n for c in self.dOp]
        dSp = [16 * c.n for c in self.dSp]
        with nc.Block() as blk:
            def mk(name):
                def body(eng):
                    if name == "sync":
                        eng.wait_ge(self.sem_dY, dYn)
                        eng.wait_ge(self.sem_dYp[0], dYp[0])
                        eng.wait_ge(self.sem_dYp[1], dYp[1])
                        eng.wait_ge(self.sem_dOp[0], dOp[0])
                        eng.wait_ge(self.sem_dOp[1], dOp[1])
                    if name == "gpsimd":
                        eng.wait_ge(self.sem_dG, dGn)
                        eng.wait_ge(self.sem_dGp[0], dGp[0])
                        eng.wait_ge(self.sem_dGp[1], dGp[1])
                        eng.wait_ge(self.sem_dSp[0], dSp[0])
                        eng.wait_ge(self.sem_dSp[1], dSp[1])
                    eng.nop().then_inc(self.sem_j, 1)
                    eng.wait_ge(self.sem_j, target)
                return body
            blk.tensor(mk("tensor"))
            blk.vector(mk("vector"))
            blk.scalar(mk("scalar"))
            blk.gpsimd(mk("gpsimd"))
            blk.sync(mk("sync"))


def rt(env, v, producer):
    producer.then_inc(env.sem_rc, 1)
    v.wait_ge(env.sem_rc, env.rc.add(1))
    return producer


def safe_recip(env, v, out_ap, in_ap, producer):
    rt(env, v, producer)
    return v.reciprocal(out_ap, in_ap)




def emit_setup(nc, env, st, dram, sb):
    e = st.enter_context
    sb["b1_bf"] = e(_sb(nc, "b1_bf", [1, 4096], BF16))
    sb["b2_bf"] = e(_sb(nc, "b2_bf", [1, 1024], BF16))
    sb["halfbo_bf"] = e(_sb(nc, "halfbo_bf", [1, 1024], BF16))
    sb["bqkv_bc"] = e(_sb(nc, "bqkv_bc", [128, 384], F32))
    sb["mrow_lo"] = e(_sb(nc, "mrow_lo", [1, 128], BF16))
    sb["mrow_hi"] = e(_sb(nc, "mrow_hi", [1, 128], BF16))
    sb["ones_f32r"] = e(_sb(nc, "ones_f32r", [1, 128], F32))
    sb["ones_f32c"] = e(_sb(nc, "ones_f32c", [128, 1], F32))
    sb["ident"] = e(_sb(nc, "ident", [128, 128], BF16))
    sb["masks"] = e(_sb(nc, "masks", [128, 4 * 512], BF16))
    sb["wqkvT"] = e(_sb(nc, "wqkvT_sb", [128, 8 * 384], BF16))
    sb["wsc"] = e(_sb(nc, "wsc", [128, 4], F32))
    sb["parks"] = e(_sb(nc, "parks", [128, 96], F32))
    sb["crows"] = e(_sb(nc, "crows", [1, 4096], BF16))
    sb["lnred"] = e(_sb(nc, "lnred", [128, 64], F32))

    open_xqT(nc, sb)

    grp = []

    def gsb(name, shape, dt):
        cm = _sb(nc, name, shape, dt)
        t = cm.__enter__()
        grp.append(cm)
        return t

    def gps(name, shape, dt):
        cm = _ps(nc, name, shape, dt)
        t = cm.__enter__()
        grp.append(cm)
        return t

    with ExitStack() as stl:
        w1t = gsb("w1sh", [128, 4096], F32)
        w2t = gsb("w2sh", [128, 4096], F32)
        tern1 = gsb("tern1", [128, 4096], BF16)
        tern2 = gsb("tern2", [128, 4096], BF16)
        wabs = gsb("wabs", [128, 2], F32)
        wtmp = gsb("wtmp", [128, 2048], F32)
        arbuf = gsb("arbuf", [1, 8], F32)
        iot = gsb("iot", [128, 512], F32)
        bq_ps = gps("bq_ps", [128, 384], F32)
        ar_ps = gps("ar_ps", [128, 8], F32)
        sb["_wt"] = dict(grp=grp, w1t=w1t, w2t=w2t, tern1=tern1, tern2=tern2,
                         wtmp=wtmp, arbuf=arbuf, ar_ps=ar_ps)

        dY0 = env.dY.n
        dG0 = env.dG.n
        with nc.Block() as blk:
            @blk.sync
            def _(y):
                y.dma_start(out=sb["bqkv_bc"][0:1, :], in_=dram["bqkv"][:, :]).then_inc(env.sem_dY, 16)

            @blk.gpsimd
            def _(g):
                for j in range(8):
                    g.dma_start(out=w1t[:, j * 512:(j + 1) * 512],
                                in_=dram["w1Tsh"][j * 128:(j + 1) * 128, :]).then_inc(env.sem_dG, 16)
                for j in range(32):
                    g.dma_start(out=w2t[:, j * 128:(j + 1) * 128],
                                in_=dram["w2Tsh"][j * 128:(j + 1) * 128, :]).then_inc(env.sem_dG, 16)
                for j in range(8):
                    g.dma_start(out=sb["wqkvT"][:, j * 384:(j + 1) * 384],
                                in_=dram["wqkvT"][j * 128:(j + 1) * 128, :]).then_inc(env.sem_dG, 16)
                g.dma_start(out=sb["b1_bf"][:, :], in_=dram["b1r"][:, :]).then_inc(env.sem_dG, 16)
                g.dma_start(out=sb["b2_bf"][:, :], in_=dram["b2r"][:, :]).then_inc(env.sem_dG, 16)
                g.dma_start(out=sb["halfbo_bf"][:, :], in_=dram["bo_h"][:, :]).then_inc(env.sem_dG, 16)
                g.iota(iot[:, :], [[1, 512]], base=0, channel_multiplier=-1,
                       allow_small_or_imprecise_dtypes=True).then_inc(env.sem_cV, 1)
        env.dY.add(1)
        env.dG.add(51)
        cV_iota = env.cV.add(1)

        dY1 = env.dY.n
        dG1 = env.dG.n
        with nc.Block() as blk:
            @blk.vector
            def _(v):
                v.wait_ge(env.sem_cV, cV_iota)
                for o in range(4):
                    v.tensor_scalar(out=sb["masks"][:, o * 512:(o + 1) * 512],
                                    in0=iot[:, :], scalar1=float(o * 128),
                                    scalar2=None, op0=ALU.is_ge)
                v.tensor_scalar(out=sb["ident"][:, :], in0=iot[:, 0:128],
                                scalar1=0.0, scalar2=None, op0=ALU.is_equal)
                v.memset(sb["mrow_lo"][0:1, 0:64], 1.0)
                v.memset(sb["mrow_lo"][0:1, 64:128], 0.0)
                v.memset(sb["mrow_hi"][0:1, 0:64], 0.0)
                v.memset(sb["mrow_hi"][0:1, 64:128], 1.0)
                v.memset(sb["ones_f32r"][:, :], 1.0)
                v.memset(sb["ones_f32c"][:, :], 1.0)
                v.wait_ge(env.sem_dG, 16 * dG1)
                v.wait_ge(env.sem_dY, 16 * dY1)
                v.nop().then_inc(env.sem_cV, 1)

            @blk.scalar
            def _(s):
                s.wait_ge(env.sem_dG, 16 * dG1)
                s.activation(tern1[:, :], w1t[:, :], AF.Abs, bias=0.0, scale=1.0,
                             accum_out=wabs[:, 0:1]).then_inc(env.sem_cS, 1)
                s.activation(tern2[:, :], w2t[:, :], AF.Abs, bias=0.0, scale=1.0,
                             accum_out=wabs[:, 1:2]).then_inc(env.sem_cS, 1)
        cV_pre = env.cV.add(1)
        cS_abs = env.cS.add(2)

        with nc.Block() as blk:
            @blk.tensor
            def _(t):
                t.wait_ge(env.sem_cV, cV_pre)
                t.matmul(bq_ps[:, :], sb["ones_f32r"][0:1, :],
                         sb["bqkv_bc"][0:1, :], start=True, stop=True).then_inc(env.sem_cT, 1)
                t.wait_ge(env.sem_cS, cS_abs)
                t.matmul(ar_ps[0:1, 0:2], sb["ones_f32c"][:, :], wabs[:, :],
                         start=True, stop=True).then_inc(env.sem_cT, 1)
        cT_bc = env.cT.add(2)

        with nc.Block() as blk:
            @blk.vector
            def _(v):
                v.wait_ge(env.sem_cT, cT_bc - 1)
                v.tensor_copy(sb["bqkv_bc"][:, :], bq_ps[:, :])
                v.wait_ge(env.sem_cT, cT_bc)
                v.tensor_copy(arbuf[:, 0:2], ar_ps[0:1, 0:2])
                v.memset(arbuf[:, 2:8], 0.0).then_inc(env.sem_cV, 1)
        cV_ar = env.cV.add(1)

        dG2 = env.dG.n
        with nc.Block() as blk:
            @blk.gpsimd
            def _(g):
                g.wait_ge(env.sem_cV, cV_ar)
                g.dma_start(out=dram["arw_in"][:, :], in_=arbuf[:, :]).then_inc(env.sem_dG, 16)
                g.wait_ge(env.sem_dG, 16 * (dG2 + 1))
                g.collective_compute(
                    "AllReduce", ALU.add, replica_groups=[list(range(NCORES))],
                    ins=[dram["arw_in"][:, :]], outs=[dram["arw_out"][:, :]],
                ).then_inc(env.sem_cc, 1)
                g.wait_ge(env.sem_cc, env.cc.n + 1)
                g.dma_start(out=arbuf[:, :], in_=dram["arw_out"][:, :]).then_inc(env.sem_dG, 16)
        env.dG.add(2)
        env.cc.add(1)
        sb["_wt"]["dG3"] = env.dG.n


def emit_setup_tail(nc, env, dram, sb):
    wt = sb["_wt"]
    w1t, w2t, tern1, tern2 = wt["w1t"], wt["w2t"], wt["tern1"], wt["tern2"]
    wtmp, arbuf, ar_ps, dG3 = wt["wtmp"], wt["arbuf"], wt["ar_ps"], wt["dG3"]
    with nc.Block() as blk:
        @blk.vector
        def _(v):
            v.wait_ge(env.sem_dG, 16 * dG3)
            pr = v.tensor_scalar(out=arbuf[:, 0:2], in0=arbuf[:, 0:2],
                                 scalar1=1.0 / (4096.0 * 1024.0), scalar2=1e-5,
                                 op0=ALU.mult, op1=ALU.max)
            rr = safe_recip(env, v, arbuf[:, 2:4], arbuf[:, 0:2], pr)
            rt(env, v, rr)
            v.tensor_copy(sb["wsc"][0:1, 0:2], arbuf[:, 2:4])
            v.tensor_copy(sb["wsc"][0:1, 2:4], arbuf[:, 0:2]).then_inc(env.sem_cV, 1)
    cV_w0 = env.cV.add(1)
    with nc.Block() as blk:
        @blk.tensor
        def _(t):
            t.wait_ge(env.sem_cV, cV_w0)
            t.matmul(ar_ps[:, 0:4], sb["ones_f32r"][0:1, :], sb["wsc"][0:1, :],
                     start=True, stop=True).then_inc(env.sem_cT, 1)
    cT_w = env.cT.add(1)
    with nc.Block() as blk:
        @blk.vector
        def _(v):
            v.wait_ge(env.sem_cT, cT_w)
            rt(env, v, v.tensor_copy(sb["wsc"][:, :], ar_ps[:, 0:4]))
            for (wt_, tt, col) in ((w1t, tern1, 0), (w2t, tern2, 1)):
                for hf in range(2):
                    hs = slice(hf * 2048, (hf + 1) * 2048)
                    rt(env, v, v.tensor_scalar(out=wtmp[:, :], in0=wt_[:, hs],
                                               scalar1=sb["wsc"][:, col:col + 1],
                                               scalar2=None, op0=ALU.mult))
                    rt(env, v, v.tensor_scalar(out=wtmp[:, :], in0=wtmp[:, :],
                                               scalar1=1.0, scalar2=-1.0,
                                               op0=ALU.min, op1=ALU.max))
                    rt(env, v, v.tensor_scalar(out=tt[:, hs], in0=wtmp[:, :],
                                               scalar1=MAGIC, scalar2=MAGIC,
                                               op0=ALU.add, op1=ALU.subtract))
            v.nop().then_inc(env.sem_cV, 1)
    cV_tern = env.cV.add(1)

    with nc.Block() as blk:
        @blk.gpsimd
        def _(g):
            g.wait_ge(env.sem_cV, cV_tern)
            for j in range(8):
                g.dma_start(out=dram["agw1_in"][j * 128:(j + 1) * 128, :],
                            in_=tern1[:, j * 512:(j + 1) * 512]).then_inc(env.sem_dG, 16)
            for j in range(32):
                g.dma_start(out=dram["agw2_in"][j * 128:(j + 1) * 128, :],
                            in_=tern2[:, j * 128:(j + 1) * 128]).then_inc(env.sem_dG, 16)
            g.wait_ge(env.sem_dG, 16 * (dG3 + 40))
            g.collective_compute(
                "AllGather", ALU.bypass, replica_groups=[list(range(NCORES))],
                ins=[dram["agw1_in"][:, :]], outs=[dram["agw1_out"][:, :, :]],
            ).then_inc(env.sem_cc, 1)
            g.collective_compute(
                "AllGather", ALU.bypass, replica_groups=[list(range(NCORES))],
                ins=[dram["agw2_in"][:, :]], outs=[dram["agw2_out"][:, :, :]],
            ).then_inc(env.sem_cc, 1)
    env.dG.add(40)
    env.agw_ready_cc = env.cc.add(2)


def close_wtern(sb):
    for cm in reversed(sb.pop("_wt")["grp"]):
        cm.__exit__(None, None, None)


def open_xqT(nc, sb):
    cm = _sb(nc, "xqT_s", [128, 16 * 1024], BF16)
    sb["xqT"] = cm.__enter__()
    sb["_xqT_cm"] = cm


def close_xqT(sb):
    sb.pop("_xqT_cm").__exit__(None, None, None)
    sb.pop("xqT")


def emit_xchain(nc, env, st_unused, dram, sb, src_name, park_off,
                mid_emit=None, stats=False):
    P = park_off
    src = dram[src_name]
    with ExitStack() as stl:
        e = stl.enter_context
        xa = e(_sb(nc, "xc_a", [128, 16 * 1024], F32))
        red = e(_sb(nc, "xc_red", [128, 64], F32))
        sq = e(_sb(nc, "xc_sq", [128, 1024], F32))
        wk = e(_sb(nc, "xc_wk", [128, 80], F32))
        zz = e(_sb(nc, "xc_z", [128, 1024], F32))
        xq = e(_sb(nc, "xc_xq", [128, 2 * 1024], BF16))
        cpark_bf = e(_sb(nc, "xc_cpb", [128, 16], BF16))
        crow_sb = e(_sb(nc, "xc_crsb", [16, 128], BF16))
        ps_c = e(_ps(nc, "xc_ps", [128, 128], F32))

        gsems = [env.sem_dYp[0], env.sem_dYp[1], env.sem_dOp[0], env.sem_dOp[1]]
        gctr = [env.dYp[0], env.dYp[1], env.dOp[0], env.dOp[1]]
        g0 = [c.n for c in gctr]
        cV0, cS0 = env.cV.n, env.cS.n
        def reduce_half(tr):
            cvb = env.cV.n
            with nc.Block() as blk:
                if tr.start == 0:
                    @blk.sync
                    def _(y):
                        for t in range(16):
                            y.dma_start(out=xa[:, t * 1024:(t + 1) * 1024],
                                        in_=src[t * 128:(t + 1) * 128, :]).then_inc(gsems[t // 4], 16)

                @blk.vector
                def _(v):
                    for t in tr:
                        v.wait_ge(gsems[t // 4], 16 * (g0[t // 4] + 4))
                        xt = xa[:, t * 1024:(t + 1) * 1024]
                        v.tensor_reduce(out=red[:, t:t + 1], in_=xt, axis=AX.X, op=ALU.add)
                        v.tensor_reduce(out=red[:, 16 + t:17 + t], in_=xt, axis=AX.X, op=ALU.max)
                        v.tensor_reduce(out=red[:, 32 + t:33 + t], in_=xt, axis=AX.X,
                                        op=ALU.min).then_inc(env.sem_cV, 1)

                @blk.scalar
                def _(s):
                    for i, t in enumerate(tr):
                        s.wait_ge(env.sem_cV, cvb + i + 1)
                        xt = xa[:, t * 1024:(t + 1) * 1024]
                        s.activation(sq[:, :], xt, AF.Square, bias=0.0, scale=1.0,
                                     accum_out=red[:, 48 + t:49 + t]).then_inc(env.sem_cS, 1)
            env.cV.add(len(tr))

        if stats:
            red = sb["lnred"]
            with nc.Block() as blk:
                @blk.sync
                def _(y):
                    for t in range(16):
                        y.dma_start(out=xa[:, t * 1024:(t + 1) * 1024],
                                    in_=src[t * 128:(t + 1) * 128, :]).then_inc(gsems[t // 4], 16)
            for c in gctr:
                c.add(4)
            cS_red = env.cS.n
        else:
            reduce_half(range(0, 8))
            if mid_emit is not None:
                mid_emit()
            reduce_half(range(8, 16))
            for c in gctr:
                c.add(4)
            cS_red = env.cS.add(16)

        with nc.Block() as blk:
            @blk.vector
            def _(v):
                v.wait_ge(env.sem_cS, cS_red)
                m = sb["parks"][:, P:P + 16]
                v.tensor_scalar(out=m, in0=red[:, 0:16], scalar1=1.0 / 1024.0,
                                scalar2=None, op0=ALU.mult)
                rt(env, v, v.tensor_scalar(out=zz[:, 0:16], in0=red[:, 48:64],
                                           scalar1=1.0 / 1024.0, scalar2=None, op0=ALU.mult))
                rt(env, v, v.tensor_tensor(zz[:, 16:32], m, m, op=ALU.mult))
                rt(env, v, v.tensor_tensor(zz[:, 0:16], zz[:, 0:16], zz[:, 16:32],
                                           op=ALU.subtract))
                pr = v.tensor_scalar(out=wk[:, 0:16], in0=zz[:, 0:16], scalar1=1e-5,
                                     scalar2=None, op0=ALU.add)
                rr = safe_recip(env, v, wk[:, 16:32], wk[:, 0:16], pr)
                rt(env, v, rr)
                rt(env, v, v.tensor_tensor(zz[:, 16:32], zz[:, 0:16], wk[:, 16:32],
                                           op=ALU.mult))
                pr = v.tensor_scalar(out=zz[:, 16:32], in0=zz[:, 16:32], scalar1=1e-5,
                                     scalar2=None, op0=ALU.add)
                rr = safe_recip(env, v, zz[:, 32:48], zz[:, 16:32], pr)
                rt(env, v, rr)
                rt(env, v, v.tensor_tensor(wk[:, 32:48], wk[:, 16:32], zz[:, 32:48],
                                           op=ALU.mult))
                v.tensor_tensor(zz[:, 0:16], red[:, 16:32], m, op=ALU.subtract)
                rt(env, v, v.tensor_tensor(zz[:, 16:32], m, red[:, 32:48], op=ALU.subtract))
                rt(env, v, v.tensor_tensor(wk[:, 48:64], zz[:, 0:16], zz[:, 16:32],
                                           op=ALU.max))
                pr = v.tensor_scalar(out=zz[:, 0:16], in0=wk[:, 48:64], scalar1=1e-5,
                                scalar2=None, op0=ALU.max)
                rr = safe_recip(env, v, zz[:, 16:32], zz[:, 0:16], pr)
                rt(env, v, rr)
                v.tensor_scalar(out=sb["parks"][:, P + 48:P + 64], in0=zz[:, 16:32],
                                scalar1=127.0, scalar2=None,
                                op0=ALU.mult).then_inc(env.sem_cV, 1)
        cV_b1 = env.cV.add(1)

        with nc.Block() as blk:
            @blk.scalar
            def _(s):
                s.wait_ge(env.sem_cV, cV_b1)
                s.activation(wk[:, 32:48], wk[:, 32:48], AF.Sqrt, bias=0.0,
                             scale=1.0).then_inc(env.sem_cS, 1)
        cS_g = env.cS.add(1)

        with nc.Block() as blk:
            @blk.vector
            def _(v):
                v.wait_ge(env.sem_cS, cS_g)
                rt(env, v, v.tensor_tensor(wk[:, 64:80], wk[:, 32:48], wk[:, 48:64],
                                           op=ALU.mult))
                rt(env, v, v.tensor_scalar(out=wk[:, 64:80], in0=wk[:, 64:80], scalar1=1e-5,
                                           scalar2=None, op0=ALU.max))
                rt(env, v, v.tensor_scalar(out=wk[:, 64:80], in0=wk[:, 64:80],
                                           scalar1=sb["wsc"][:, 2:3], scalar2=None,
                                           op0=ALU.mult))
                pr = v.tensor_scalar(out=sb["parks"][:, P + 32:P + 48], in0=wk[:, 64:80],
                                     scalar1=1.0 / 127.0, scalar2=None, op0=ALU.mult)
                rr = safe_recip(env, v, wk[:, 0:16], sb["parks"][:, P + 32:P + 48], pr)
                rt(env, v, rr)
                v.tensor_copy(cpark_bf[:, :], wk[:, 0:16]).then_inc(env.sem_cV, 1)
        cV_cp = env.cV.add(1)

        with nc.Block() as blk:
            @blk.tensor
            def _(t):
                t.wait_ge(env.sem_cV, cV_cp)
                t.matmul(ps_c[0:16, :], cpark_bf[:, :], sb["ident"][:, :],
                         start=True, stop=True).then_inc(env.sem_cT, 1)
        cT_cp = env.cT.add(1)

        with nc.Block() as blk:
            @blk.vector
            def _(v):
                v.wait_ge(env.sem_cT, cT_cp)
                v.tensor_copy(crow_sb[:, :], ps_c[0:16, :]).then_inc(env.sem_cV, 1)
        cV_cr = env.cV.add(1)
        dG_cr0 = env.dG.n
        with nc.Block() as blk:
            @blk.gpsimd
            def _(g):
                g.wait_ge(env.sem_cV, cV_cr)
                g.dma_start(out=sb["crows"][0:1, 0:2048],
                            in_=crow_sb[:, :]).then_inc(env.sem_dG, 16)
                g.wait_ge(env.sem_dG, 16 * (dG_cr0 + 1))
        env.dG.add(1)

        cV1 = env.cV.n
        cT1 = env.cT.n
        cS1 = env.cS.n
        ps_t = [e(_ps(nc, f"xc_pt{i}", [128, 1024], F32)) for i in range(2)]
        with nc.Block() as blk:
            @blk.vector
            def _(v):
                for t in range(16):
                    if stats:
                        v.wait_ge(gsems[t // 4], 16 * (g0[t // 4] + 4))
                    if t >= 2:
                        v.wait_ge(env.sem_cT, cT1 + t - 1)
                    xt = xa[:, t * 1024:(t + 1) * 1024]
                    xo = xq[:, (t % 2) * 1024:(t % 2 + 1) * 1024]
                    for hf in range(2):
                        v.tensor_scalar(out=zz[:, hf * 512:(hf + 1) * 512],
                                        in0=xt[:, hf * 512:(hf + 1) * 512],
                                        scalar1=sb["parks"][:, P + t:P + t + 1],
                                        scalar2=sb["parks"][:, P + 48 + t:P + 49 + t],
                                        op0=ALU.subtract, op1=ALU.mult)
                    for hf in range(2):
                        pr = v.tensor_scalar(out=xo[:, hf * 512:(hf + 1) * 512],
                                             in0=zz[:, hf * 512:(hf + 1) * 512],
                                             scalar1=MAGIC, scalar2=MAGIC,
                                             op0=ALU.add, op1=ALU.subtract)
                    pr.then_inc(env.sem_cV, 1)

            @blk.tensor
            def _(t_):
                for t in range(16):
                    t_.wait_ge(env.sem_cV, cV1 + t + 1)
                    if t >= 2:
                        t_.wait_ge(env.sem_cS, cS1 + t - 1)
                    mm = None
                    for fi in range(8):
                        mm = t_.matmul(ps_t[t % 2][:, fi * 128:(fi + 1) * 128],
                                       xq[:, (t % 2) * 1024 + fi * 128:
                                          (t % 2) * 1024 + (fi + 1) * 128],
                                       sb["ident"][:, :], start=True, stop=True)
                    mm.then_inc(env.sem_cT, 1)

            @blk.scalar
            def _(s):
                for t in range(16):
                    s.wait_ge(env.sem_cT, cT1 + t + 1)
                    cp = None
                    for fi in range(8):
                        cp = s.copy(sb["xqT"][:, fi * 2048 + t * 128:
                                              fi * 2048 + (t + 1) * 128],
                                    ps_t[t % 2][:, fi * 128:(fi + 1) * 128])
                    cp.then_inc(env.sem_cS, 1)
        env.cV.add(16)
        env.cT.add(16)
        env.cS.add(16)
        env.join()


def emit_ffn(nc, env, dram, sb, res_src_name, out_dram_name, park_off,
             ln_stats=False):
    P = park_off

    with ExitStack() as stA:
        ea = stA.enter_context
        w1T = ea(_sb(nc, "fA_w1T", [128, 8 * 4096], BF16))
        xqT = sb["xqT"]
        h_raw = ea(_sb(nc, "fA_hraw", [128, 4096], F32))
        h_gel = ea(_sb(nc, "fA_hgel", [128, 2 * 4096], F32))
        hq = ea(_sb(nc, "fA_hq", [128, 4096], BF16))
        hred = ea(_sb(nc, "fA_hred", [128, 48], F32))
        hz = ea(_sb(nc, "fA_hz", [128, 4096], F32))
        hw = ea(_sb(nc, "fA_hw", [128, 64], F32))
        cpark_bf = ea(_sb(nc, "fA_cpb", [128, 16], BF16))
        crow_sb = ea(_sb(nc, "fA_crsb", [16, 128], BF16))
        ps_h = [ea(_ps(nc, f"fA_ps{i}", [128, 2048], F32)) for i in range(2)]
        sb["hredA"] = hred
        sb["hgelA"] = h_gel

        dY0, dG0, cT0, cS0, cV0 = env.dY.n, env.dG.n, env.cT.n, env.cS.n, env.cV.n
        agw_cc = env.agw_ready_cc
        with nc.Block() as blk:
            @blk.sync
            def _(y):
                y.wait_ge(env.sem_cc, agw_cc)
                w1T_v = w1T[:, :].rearrange("p (f g) -> p f g", g=4096)
                for r in range(8):
                    y.dma_start(
                        out=w1T_v[:, :, r * 512:(r + 1) * 512],
                        in_=dram["agw1_out"][r].rearrange("(f p) j -> p f j", p=128),
                    ).then_inc(env.sem_dY, 16)

            @blk.tensor
            def _(t):
                t.wait_ge(env.sem_dY, 16 * (dY0 + 8))
                for ti in range(16):
                    for s2 in range(2):
                        k = 2 * ti + s2
                        if k >= 2:
                            t2, sx = divmod(k - 2, 2)
                            t.wait_ge(env.sem_cS, cS0 + 5 * t2 + (1 if sx == 0 else 3))
                        ps = ps_h[k % 2]
                        for fi in range(8):
                            for nb in range(4):
                                fo = s2 * 2048 + nb * 512
                                t.matmul(ps[:, nb * 512:(nb + 1) * 512],
                                         xqT[:, fi * 2048 + ti * 128: fi * 2048 + (ti + 1) * 128],
                                         w1T[:, fi * 4096 + fo: fi * 4096 + fo + 512],
                                         start=(fi == 0), stop=False)
                        crow = sb["crows"][0:1, ti * 128:(ti + 1) * 128]
                        mm = None
                        for nb in range(4):
                            fo = s2 * 2048 + nb * 512
                            mm = t.matmul(ps[:, nb * 512:(nb + 1) * 512], crow,
                                          sb["b1_bf"][0:1, fo:fo + 512],
                                          start=False, stop=True)
                        mm.then_inc(env.sem_cT, 1)

            @blk.scalar
            def _(s):
                for ti in range(16):
                    if ti >= 2:
                        s.wait_ge(env.sem_cV, cV0 + ti - 1)
                    for s2 in range(2):
                        k = 2 * ti + s2
                        s.wait_ge(env.sem_cT, cT0 + k + 1)
                        ps = ps_h[k % 2]
                        hrs = h_raw[:, s2 * 2048:(s2 + 1) * 2048]
                        s.copy(hrs, ps[:, :]).then_inc(env.sem_cS, 1)
                        hgs = h_gel[:, (ti % 2) * 4096 + s2 * 2048:(ti % 2) * 4096 + (s2 + 1) * 2048]
                        s.activation(hgs, hrs, AF.Gelu, bias=0.0,
                                     scale=sb["parks"][:, P + 32 + ti:P + 33 + ti],
                                     ).then_inc(env.sem_cS, 1)
                    hg = h_gel[:, (ti % 2) * 4096:(ti % 2 + 1) * 4096]
                    s.activation(h_raw[:, :], hg, AF.Square, bias=0.0, scale=1.0,
                                 accum_out=hred[:, ti:ti + 1]).then_inc(env.sem_cS, 1)

            @blk.vector
            def _(v):
                for ti in range(16):
                    v.wait_ge(env.sem_cS, cS0 + 5 * ti + 4)
                    hg = h_gel[:, (ti % 2) * 4096:(ti % 2 + 1) * 4096]
                    rt(env, v, v.tensor_reduce(out=hred[:, 16 + ti:17 + ti], in_=hg,
                                               axis=AX.X, op=ALU.max,
                                               apply_absolute_value=True))
                    pr = v.tensor_scalar(out=hw[:, 0:1], in0=hred[:, 16 + ti:17 + ti],
                                         scalar1=1e-5, scalar2=None, op0=ALU.max)
                    rr = safe_recip(env, v, hw[:, 1:2], hw[:, 0:1], pr)
                    rt(env, v, rr)
                    rt(env, v, v.tensor_scalar(out=hw[:, 2:3], in0=hw[:, 1:2], scalar1=127.0,
                                               scalar2=None, op0=ALU.mult))
                    if ti >= 1:
                        v.wait_ge(env.sem_dG, 16 * (dG0 + ti))
                    v.tensor_scalar(out=hz[:, 0:2048], in0=hg[:, 0:2048],
                                    scalar1=hw[:, 2:3], scalar2=None, op0=ALU.mult)
                    v.tensor_scalar(out=hz[:, 2048:4096], in0=hg[:, 2048:4096],
                                    scalar1=hw[:, 2:3], scalar2=None, op0=ALU.mult)
                    v.tensor_scalar(out=hq[:, 0:2048], in0=hz[:, 0:2048],
                                    scalar1=MAGIC, scalar2=MAGIC, op0=ALU.add,
                                    op1=ALU.subtract)
                    v.tensor_scalar(out=hq[:, 2048:4096], in0=hz[:, 2048:4096],
                                    scalar1=MAGIC, scalar2=MAGIC, op0=ALU.add,
                                    op1=ALU.subtract).then_inc(env.sem_cV, 1)

            @blk.gpsimd
            def _(g):
                for ti in range(16):
                    g.wait_ge(env.sem_cV, cV0 + ti + 1)
                    g.dma_start(out=dram["hq_sc"][ti * 128:(ti + 1) * 128, :],
                                in_=hq[:, :]).then_inc(env.sem_dG, 16)
        env.dY.add(8)
        env.cT.add(32)
        env.cS.add(80)
        env.cV.add(16)
        env.dG.add(16)

        cV1 = env.cV.n
        cS_allsq = env.cS.n
        with nc.Block() as blk:
            @blk.vector
            def _(v):
                v.wait_ge(env.sem_cV, cV1)
                v.wait_ge(env.sem_cS, cS_allsq)
                pr = v.tensor_scalar(out=hw[:, 0:16], in0=hred[:, 0:16],
                                     scalar1=1.0 / 4096.0, scalar2=1e-5,
                                     op0=ALU.mult, op1=ALU.add)
                safe_recip(env, v, hw[:, 16:32], hw[:, 0:16], pr).then_inc(env.sem_cV, 1)
        cV_r = env.cV.add(1)

        with nc.Block() as blk:
            @blk.scalar
            def _(s):
                s.wait_ge(env.sem_cV, cV_r)
                s.activation(hw[:, 16:32], hw[:, 16:32], AF.Sqrt, bias=0.0,
                             scale=1.0).then_inc(env.sem_cS, 1)
        cS_r = env.cS.add(1)

        with nc.Block() as blk:
            @blk.vector
            def _(v):
                v.wait_ge(env.sem_cS, cS_r)
                rt(env, v, v.tensor_tensor(hw[:, 32:48], hw[:, 16:32], hred[:, 16:32],
                                           op=ALU.mult))
                rt(env, v, v.tensor_scalar(out=hw[:, 32:48], in0=hw[:, 32:48], scalar1=1e-5,
                                           scalar2=None, op0=ALU.max))
                rt(env, v, v.tensor_scalar(out=hw[:, 32:48], in0=hw[:, 32:48],
                                           scalar1=sb["wsc"][:, 3:4], scalar2=None,
                                           op0=ALU.mult))
                rt(env, v, v.tensor_scalar(out=hw[:, 32:48], in0=hw[:, 32:48],
                                           scalar1=1.0 / 127.0, scalar2=None,
                                           op0=ALU.mult))
                v.tensor_scalar(out=sb["parks"][:, P + 64:P + 80], in0=hw[:, 32:48],
                                scalar1=0.5, scalar2=None, op0=ALU.mult)
                rr = safe_recip(env, v, hw[:, 0:16], hw[:, 32:48],
                                v.tensor_scalar(out=hw[:, 48:49], in0=hw[:, 0:1],
                                                scalar1=0.0, scalar2=None, op0=ALU.add))
                rt(env, v, rr)
                v.tensor_copy(cpark_bf[:, :], hw[:, 0:16]).then_inc(env.sem_cV, 1)
        cV_cp = env.cV.add(1)

        with nc.Block() as blk:
            @blk.tensor
            def _(t):
                t.wait_ge(env.sem_cV, cV_cp)
                t.matmul(ps_h[0][0:16, 0:128], cpark_bf[:, :], sb["ident"][:, :],
                         start=True, stop=True).then_inc(env.sem_cT, 1)
        cT_cp = env.cT.add(1)

        with nc.Block() as blk:
            @blk.vector
            def _(v):
                v.wait_ge(env.sem_cT, cT_cp)
                v.tensor_copy(crow_sb[:, :], ps_h[0][0:16, 0:128]).then_inc(env.sem_cV, 1)
        cV_cr = env.cV.add(1)
        dG_cr0 = env.dG.n
        with nc.Block() as blk:
            @blk.gpsimd
            def _(g):
                g.wait_ge(env.sem_cV, cV_cr)
                g.dma_start(out=sb["crows"][0:1, 2048:4096],
                            in_=crow_sb[:, :]).then_inc(env.sem_dG, 16)
                g.wait_ge(env.sem_dG, 16 * (dG_cr0 + 1))
        env.dG.add(1)
        env.join()
    close_xqT(sb)

    with ExitStack() as stB:
        eb = stB.enter_context
        w2T = eb(_sb(nc, "fB_w2T", [128, 32 * 1024], BF16))
        hqT = eb(_sb(nc, "fB_hqT", [128, 2 * 16384], BF16))
        resb = eb(_sb(nc, "fB_res", [128, 2 * 1024], F32))
        outb = eb(_sb(nc, "fB_out", [128, 2 * 1024], F32))
        if ln_stats:
            sqs = eb(_sb(nc, "fB_sqs", [128, 1024], F32))
        ps_y = [eb(_ps(nc, f"fB_ps{i}", [128, 1024], F32)) for i in range(2)]

        dY0, cT0, cV0 = env.dY.n, env.cT.n, env.cV.n
        cS_sq0 = env.cS.n
        dYp0 = [env.dYp[0].n, env.dYp[1].n]
        dGp0 = [env.dGp[0].n, env.dGp[1].n]
        dSp0 = [env.dSp[0].n, env.dSp[1].n]
        res_src = dram[res_src_name]
        out_d = dram[out_dram_name]

        with nc.Block() as blk:
            @blk.sync
            def _(y):
                w2T_v = w2T[:, :].rearrange("p (f g) -> p f g", g=1024)
                for r in range(8):
                    srcv = dram["agw2_out"][r].rearrange("(f p) j -> p f j", p=128)
                    for q in range(4):
                        y.dma_start(
                            out=w2T_v[:, q * 8:(q + 1) * 8, r * 128:(r + 1) * 128],
                            in_=srcv[:, q * 8:(q + 1) * 8, :],
                        ).then_inc(env.sem_dY, 16)
                for tg in range(4):
                    if tg >= 2:
                        y.wait_ge(env.sem_cT, cT0 + 4 * (tg - 1))
                    for fi in range(32):
                        y.dma_start_transpose(
                            hqT[:, (tg % 2) * 16384 + fi * 512:(tg % 2) * 16384 + (fi + 1) * 512],
                            dram["hq_sc"][tg * 512:(tg + 1) * 512, fi * 128:(fi + 1) * 128],
                        ).then_inc(env.sem_dYp[tg % 2], 16)
            @blk.gpsimd
            def _(g):
                for t128 in range(18):
                    if t128 >= 2:
                        g.wait_ge(env.sem_cV, cV0 + t128 - 1)
                    if t128 < 16:
                        g.dma_start(out=resb[:, (t128 % 2) * 1024:(t128 % 2 + 1) * 1024],
                                    in_=res_src[t128 * 128:(t128 + 1) * 128, :],
                                    ).then_inc(env.sem_dGp[t128 % 2], 16)
                    if t128 >= 2:
                        ts_ = t128 - 2
                        g.dma_start(out=out_d[ts_ * 128:(ts_ + 1) * 128, :],
                                    in_=outb[:, (ts_ % 2) * 1024:(ts_ % 2 + 1) * 1024],
                                    ).then_inc(env.sem_dSp[ts_ % 2], 16)

            @blk.tensor
            def _(t):
                for tg in range(4):
                    t.wait_ge(env.sem_dY, 16 * (dY0 + 32))
                    t.wait_ge(env.sem_dYp[tg % 2], 16 * (dYp0[tg % 2] + 32 * (tg // 2 + 1)))
                    for tt in range(4):
                        t128 = tg * 4 + tt
                        if t128 >= 2:
                            t.wait_ge(env.sem_cV, cV0 + t128 - 1)
                        ps = ps_y[t128 % 2]
                        for fi in range(32):
                            lhs = hqT[:, (tg % 2) * 16384 + fi * 512 + tt * 128:
                                      (tg % 2) * 16384 + fi * 512 + (tt + 1) * 128]
                            t.matmul(ps[:, 0:512], lhs, w2T[:, fi * 1024:fi * 1024 + 512],
                                     start=(fi == 0), stop=False)
                            t.matmul(ps[:, 512:1024], lhs,
                                     w2T[:, fi * 1024 + 512:(fi + 1) * 1024],
                                     start=(fi == 0), stop=False)
                        crow = sb["crows"][0:1, 2048 + t128 * 128:2048 + (t128 + 1) * 128]
                        t.matmul(ps[:, 0:512], crow, sb["b2_bf"][0:1, 0:512],
                                 start=False, stop=True)
                        t.matmul(ps[:, 512:1024], crow, sb["b2_bf"][0:1, 512:1024],
                                 start=False, stop=True).then_inc(env.sem_cT, 1)

            @blk.vector
            def _(v):
                def red1(t):
                    v.tensor_reduce(out=sb["lnred"][:, t:t + 1],
                                    in_=outb[:, (t % 2) * 1024:(t % 2 + 1) * 1024],
                                    axis=AX.X, op=ALU.add)
                for t128 in range(16):
                    v.wait_ge(env.sem_cT, cT0 + t128 + 1)
                    v.wait_ge(env.sem_dGp[t128 % 2], 16 * (dGp0[t128 % 2] + t128 // 2 + 1))
                    if t128 >= 2:
                        v.wait_ge(env.sem_dSp[t128 % 2],
                                  16 * (dSp0[t128 % 2] + (t128 - 2) // 2 + 1))
                        if ln_stats:
                            v.wait_ge(env.sem_cS, cS_sq0 + t128 - 1)
                    ps = ps_y[t128 % 2]
                    v.scalar_tensor_tensor(
                        out=outb[:, (t128 % 2) * 1024:(t128 % 2 + 1) * 1024],
                        in0=ps[:, :],
                        scalar=sb["parks"][:, P + 64 + t128:P + 65 + t128],
                        in1=resb[:, (t128 % 2) * 1024:(t128 % 2 + 1) * 1024],
                        op0=ALU.mult, op1=ALU.add).then_inc(env.sem_cV, 1)
                    if ln_stats and t128 >= 1:
                        red1(t128 - 1)
                if ln_stats:
                    red1(15)

            if ln_stats:
                @blk.scalar
                def _(s):
                    for t in range(16):
                        s.wait_ge(env.sem_cV, cV0 + t + 1)
                        s.activation(sqs[:, :], outb[:, (t % 2) * 1024:(t % 2 + 1) * 1024],
                                     AF.Square, bias=0.0, scale=1.0,
                                     accum_out=sb["lnred"][:, 48 + t:49 + t],
                                     ).then_inc(env.sem_cS, 1)
        env.dY.add(32)
        env.dYp[0].add(64)
        env.dYp[1].add(64)
        env.dSp[0].add(8)
        env.dSp[1].add(8)
        env.dGp[0].add(8)
        env.dGp[1].add(8)
        env.cT.add(16)
        env.cV.add(16)
        if ln_stats:
            env.cS.add(16)
        env.join()


def emit_ln_to_bf(nc, env, dram, sb, src_name, dst_name):
    src, dst = dram[src_name], dram[dst_name]
    with ExitStack() as stl:
        e = stl.enter_context
        xa = e(_sb(nc, "ln_a", [128, 16 * 1024], F32))
        red = e(_sb(nc, "ln_red", [128, 48], F32))
        sq = e(_sb(nc, "ln_sq", [128, 1024], F32))
        ob = e(_sb(nc, "ln_o", [128, 2 * 1024], BF16))

        gsems = [env.sem_dYp[0], env.sem_dYp[1], env.sem_dOp[0], env.sem_dOp[1]]
        gctr = [env.dYp[0], env.dYp[1], env.dOp[0], env.dOp[1]]
        g0 = [c.n for c in gctr]
        with nc.Block() as blk:
            @blk.sync
            def _(y):
                for t in range(16):
                    y.dma_start(out=xa[:, t * 1024:(t + 1) * 1024],
                                in_=src[t * 128:(t + 1) * 128, :]).then_inc(gsems[t // 4], 16)
        for c in gctr:
            c.add(4)

        with nc.Block() as blk:
            @blk.vector
            def _(v):
                v.tensor_scalar(out=red[:, 0:16], in0=sb["lnred"][:, 0:16],
                                scalar1=1.0 / 1024.0, scalar2=None, op0=ALU.mult)
                rt(env, v, v.tensor_scalar(out=red[:, 16:32], in0=sb["lnred"][:, 48:64],
                                           scalar1=1.0 / 1024.0, scalar2=None,
                                           op0=ALU.mult))
                rt(env, v, v.tensor_tensor(red[:, 32:48], red[:, 0:16], red[:, 0:16],
                                           op=ALU.mult))
                rt(env, v, v.tensor_tensor(red[:, 16:32], red[:, 16:32], red[:, 32:48],
                                           op=ALU.subtract))
                v.tensor_scalar(out=red[:, 16:32], in0=red[:, 16:32], scalar1=1e-5,
                                scalar2=None, op0=ALU.add).then_inc(env.sem_cV, 1)
        cV_v = env.cV.add(1)

        with nc.Block() as blk:
            @blk.scalar
            def _(s):
                s.wait_ge(env.sem_cV, cV_v)
                s.activation(red[:, 16:32], red[:, 16:32], AF.Sqrt, bias=0.0,
                             scale=1.0).then_inc(env.sem_cS, 1)
        cS_s = env.cS.add(1)

        with nc.Block() as blk:
            @blk.vector
            def _(v):
                v.wait_ge(env.sem_cS, cS_s)
                v.reciprocal(red[:, 32:48], red[:, 16:32]).then_inc(env.sem_cV, 1)
        env.cV.add(1)

        cV1 = env.cV.n
        cT1 = env.cT.n
        cS1 = env.cS.n
        ps_t = [e(_ps(nc, f"ln_pt{i}", [128, 1024], F32)) for i in range(2)]
        with nc.Block() as blk:
            @blk.vector
            def _(v):
                for t in range(16):
                    v.wait_ge(gsems[t // 4], 16 * (g0[t // 4] + 4))
                    if t >= 2:
                        v.wait_ge(env.sem_cT, cT1 + t - 1)
                    v.tensor_scalar(out=ob[:, (t % 2) * 1024:(t % 2 + 1) * 1024],
                                    in0=xa[:, t * 1024:(t + 1) * 1024],
                                    scalar1=red[:, t:t + 1],
                                    scalar2=red[:, 32 + t:33 + t],
                                    op0=ALU.subtract, op1=ALU.mult).then_inc(env.sem_cV, 1)

            @blk.tensor
            def _(t_):
                for t in range(16):
                    t_.wait_ge(env.sem_cV, cV1 + t + 1)
                    if t >= 2:
                        t_.wait_ge(env.sem_cS, cS1 + t - 1)
                    mm = None
                    for fi in range(8):
                        mm = t_.matmul(ps_t[t % 2][:, fi * 128:(fi + 1) * 128],
                                       ob[:, (t % 2) * 1024 + fi * 128:
                                          (t % 2) * 1024 + (fi + 1) * 128],
                                       sb["ident"][:, :], start=True, stop=True)
                    mm.then_inc(env.sem_cT, 1)

            @blk.scalar
            def _(s):
                for t in range(16):
                    s.wait_ge(env.sem_cT, cT1 + t + 1)
                    cp = None
                    for fi in range(8):
                        cp = s.copy(sb["xqT"][:, fi * 2048 + t * 128:
                                              fi * 2048 + (t + 1) * 128],
                                    ps_t[t % 2][:, fi * 128:(fi + 1) * 128])
                    cp.then_inc(env.sem_cS, 1)
        env.cV.add(16)
        env.cT.add(16)
        env.cS.add(16)
        env.ln_evac_cS = cS1
        env.ln_tp_cT = cT1 + 16


def emit_agx(nc, env, dram, sb):
    dY0 = env.dY.n
    base = env.ln_evac_cS
    with nc.Block() as blk:
        @blk.sync
        def _(y):
            y.wait_ge(env.sem_cS, base + 8)
            for fi in range(8):
                y.dma_start(out=dram["agx_inA"][fi * 128:(fi + 1) * 128, :],
                            in_=sb["xqT"][:, fi * 2048:fi * 2048 + 1024],
                            ).then_inc(env.sem_dY, 16)
            y.wait_ge(env.sem_cS, base + 16)
            for fi in range(8):
                y.dma_start(out=dram["agx_inB"][fi * 128:(fi + 1) * 128, :],
                            in_=sb["xqT"][:, fi * 2048 + 1024:(fi + 1) * 2048],
                            ).then_inc(env.sem_dY, 16)

        @blk.gpsimd
        def _(g):
            g.wait_ge(env.sem_dY, 16 * (dY0 + 8))
            g.collective_compute(
                "AllGather", ALU.bypass, replica_groups=[list(range(NCORES))],
                ins=[dram["agx_inA"][:, :]], outs=[dram["agx_outA"][:, :, :]],
            ).then_inc(env.sem_cc, 1)
            g.wait_ge(env.sem_dY, 16 * (dY0 + 16))
            g.collective_compute(
                "AllGather", ALU.bypass, replica_groups=[list(range(NCORES))],
                ins=[dram["agx_inB"][:, :]], outs=[dram["agx_outB"][:, :, :]],
            ).then_inc(env.sem_cc, 1)
    env.dY.add(16)
    env.agxA_cc = env.cc.add(1)
    env.agxB_cc = env.cc.add(1)


def qkv_tile_of(k):
    if k < 64:
        return (k // 8) * 16 + (k % 8)
    return ((k - 64) // 8) * 16 + 8 + ((k - 64) % 8)


def emit_qkv(nc, env, st_att, dram, sb):
    e = st_att.enter_context
    sb["V"] = e(_sb(nc, "Vbig", [128, 128 * 130], BF16))
    sb["woT"] = e(_sb(nc, "woT_sb", [128, 8 * 1024], BF16))
    with ExitStack() as stl:
        el = stl.enter_context
        xbuf = el(_sb(nc, "qk_x", [128, 4 * 8192], BF16))
        sqd = el(_sb(nc, "qk_sqd", [128, 64], F32))
        nrm4 = el(_sb(nc, "qk_nrm4", [128, 96], F32))
        qkb = el(_sb(nc, "qk_qkb", [128, 2 * 256], BF16))
        ps = [el(_ps(nc, f"qk_ps{i}", [128, 384], F32)) for i in range(6)]

        lnS = env.ln_evac_cS + 16
        dY0, dG0, cT0, cV0, cS0 = env.dY.n, env.dG.n, env.cT.n, env.cV.n, env.cS.n

        with nc.Block() as blk:
            @blk.vector
            def _(v):
                v.wait_ge(env.sem_cS, lnS)
                vv = sb["V"][:, :].rearrange("p (t c) -> p t c", c=130)
                v.memset(vv[:, :, 64:65], 1.0)
                v.memset(vv[:, :, 129:130], 1.0)
                v.nop().then_inc(env.sem_cV, 1)

            @blk.gpsimd
            def _(g):
                g.wait_ge(env.sem_cS, lnS)
                for j in range(8):
                    g.dma_start(out=sb["woT"][:, j * 1024:(j + 1) * 1024],
                                in_=dram["woT"][j * 128:(j + 1) * 128, :]).then_inc(env.sem_dG, 16)
        env.cV.add(1)
        env.dG.add(8)
        cV0 = env.cV.n
        dG0 = env.dG.n

        NK = 128
        dYp0 = [env.dYp[0].n, env.dYp[1].n]
        dOp0 = [env.dOp[0].n, env.dOp[1].n]
        dGp0 = [env.dGp[0].n, env.dGp[1].n]
        with nc.Block() as blk:
            @blk.sync
            def _(y):
                y.wait_ge(env.sem_cS, lnS)
                y.wait_ge(env.sem_cc, env.agxA_cc)
                for r in range(8):
                    if r >= 2:
                        y.wait_ge(env.sem_cT, cT0 + 8 * (r - 1))
                    for fi in range(8):
                        y.dma_start(out=xbuf[:, (r % 2) * 8192 + fi * 1024:
                                             (r % 2) * 8192 + (fi + 1) * 1024],
                                    in_=dram["agx_outA"][r, fi * 128:(fi + 1) * 128, :],
                                    ).then_inc(env.sem_dYp[r % 2], 16)
                y.wait_ge(env.sem_cc, env.agxB_cc)
                for r in range(8):
                    if r >= 2:
                        y.wait_ge(env.sem_cT, cT0 + 64 + 8 * (r - 1))
                    for fi in range(8):
                        y.dma_start(out=xbuf[:, 16384 + (r % 2) * 8192 + fi * 1024:
                                             16384 + (r % 2) * 8192 + (fi + 1) * 1024],
                                    in_=dram["agx_outB"][r, fi * 128:(fi + 1) * 128, :],
                                    ).then_inc(env.sem_dOp[r % 2], 16)

            @blk.tensor
            def _(t):
                for k in range(NK):
                    if k < 64:
                        r, tc = k // 8, k % 8
                        if tc == 0:
                            t.wait_ge(env.sem_dYp[r % 2],
                                      16 * (dYp0[r % 2] + 8 * (r // 2) + 8))
                        col = (r % 2) * 8192 + tc * 128
                    else:
                        r, tc = (k - 64) // 8, (k - 64) % 8
                        if tc == 0:
                            t.wait_ge(env.sem_dOp[r % 2],
                                      16 * (dOp0[r % 2] + 8 * (r // 2) + 8))
                        col = 16384 + (r % 2) * 8192 + tc * 128
                    if k >= 6:
                        t.wait_ge(env.sem_cV, cV0 + (k - 6) + 1)
                    p = ps[k % 6]
                    for fi in range(8):
                        t.matmul(p[:, :],
                                 xbuf[:, col + fi * 1024:col + fi * 1024 + 128],
                                 sb["wqkvT"][:, fi * 384:(fi + 1) * 384],
                                 start=(fi == 0), stop=False)
                    t.matmul(p[:, :], sb["ones_f32r"][0:1, :],
                             sb["bqkv_bc"][0:1, :], start=False,
                             stop=True).then_inc(env.sem_cT, 1)

            @blk.scalar
            def _(s):
                def lnexp(kk):
                    sl = (kk % 6) * 16
                    s.activation(nrm4[:, sl + 4:sl + 8], nrm4[:, sl:sl + 4],
                                 AF.Ln, bias=0.0, scale=1.0)
                    s.activation(nrm4[:, sl + 8:sl + 12], nrm4[:, sl + 4:sl + 8],
                                 AF.Exp, bias=0.0,
                                 scale=-0.5).then_inc(env.sem_cS, 1)
                for k in range(NK):
                    s.wait_ge(env.sem_cT, cT0 + k + 1)
                    p = ps[k % 6]
                    sl = (k % 6) * 16
                    for h in range(4):
                        s.activation(sqd[:, :], p[:, h * 64:(h + 1) * 64],
                                     AF.Square, bias=0.0, scale=1.0,
                                     accum_out=nrm4[:, sl + h:sl + h + 1])
                    if k >= 1:
                        lnexp(k - 1)
                lnexp(NK - 1)

            @blk.vector
            def _(v):
                for k in range(NK):
                    v.wait_ge(env.sem_cS, cS0 + k + 1)
                    if k >= 2:
                        v.wait_ge(env.sem_dGp[k % 2], 16 * (dGp0[k % 2] + 2 * (k // 2)))
                    p = ps[k % 6]
                    sl = (k % 6) * 16
                    qb = qkb[:, (k % 2) * 256:(k % 2) * 256 + 256]
                    for h in range(4):
                        v.tensor_scalar(out=qb[:, h * 64:(h + 1) * 64],
                                        in0=p[:, h * 64:(h + 1) * 64],
                                        scalar1=nrm4[:, sl + 8 + h:sl + 9 + h],
                                        scalar2=None, op0=ALU.mult)
                    base = qkv_tile_of(k) * 130
                    v.tensor_copy(sb["V"][:, base:base + 64], p[:, 256:320])
                    v.tensor_copy(sb["V"][:, base + 65:base + 129],
                                  p[:, 320:384]).then_inc(env.sem_cV, 1)

            @blk.gpsimd
            def _(g):
                for k in range(NK):
                    g.wait_ge(env.sem_cV, cV0 + k + 1)
                    qb = qkb[:, (k % 2) * 256:(k % 2) * 256 + 256]
                    gt = qkv_tile_of(k)
                    g.dma_start(out=dram["qh_sc"][gt * 128:(gt + 1) * 128, :],
                                in_=qb[:, 0:128]).then_inc(env.sem_dGp[k % 2], 16)
                    g.dma_start(out=dram["kh_sc"][gt * 128:(gt + 1) * 128, :],
                                in_=qb[:, 128:256]).then_inc(env.sem_dGp[k % 2], 16)
        env.dYp[0].add(32)
        env.dYp[1].add(32)
        env.dOp[0].add(32)
        env.dOp[1].add(32)
        env.cT.add(NK)
        env.cV.add(NK)
        env.cS.add(NK)
        env.dGp[0].add(NK)
        env.dGp[1].add(NK)
        env.join()


def emit_attention(nc, env, st_att, dram, sb):
    stl = ExitStack()
    e = stl.enter_context
    qT = e(_sb(nc, "at_qT", [128, 2 * 4096], BF16))
    kT = e(_sb(nc, "at_kT", [128, 2 * 4096], BF16))
    pbf = e(_sb(nc, "at_p", [128, 3 * 1024], BF16))
    lnr = e(_sb(nc, "at_ln", [1, 1024], F32))
    rbf = e(_sb(nc, "at_rb", [1, 1024], BF16))
    oev = e(_sb(nc, "at_oe", [128, 512], F32))
    outT = e(_sb(nc, "at_outT", [128, 4096], BF16))
    ps_s = [e(_ps(nc, f"at_ps{i}", [128, 1024], F32)) for i in range(2)]
    ps_pv = e(_ps(nc, "at_pv", [128, 1024], F32))
    ps_bc = e(_ps(nc, "at_bc", [128, 512], F32))

    waves = []
    for b in range(4):
        for qc in range(8):
            for kt in range(4 * qc + 4):
                waves.append((b, qc, kt))
    NW = len(waves)
    widx = {w: i for i, w in enumerate(waves)}

    dY0, dG0 = env.dY.n, env.dG.n
    cT = Counter(env.cT.n)
    cV = Counter(env.cV.n)
    cS = Counter(env.cS.n)

    score_done = {}
    pv_done = {}
    bcast_done = {}
    exp_done = {}
    mask_done = {}
    tail1_done = {}
    evac_done = {}

    LAG = 2
    BDELAY = 3
    tev = []
    pending = []
    pvn = 0

    def _push_pv(wp):
        tev.append(("pv", wp))
        if wp[2] == 4 * wp[1] + 3:
            bd = 0 if wp[1] == 7 else BDELAY
            pending.append((pvn + 1 + bd, (wp[0], wp[1])))

    for i, w in enumerate(waves):
        tev.append(("score", w))
        if i >= LAG:
            _push_pv(waves[i - LAG])
            pvn += 1
            while pending and pending[0][0] <= pvn:
                tev.append(("bcast", pending.pop(0)[1]))
    for j in range(LAG, 0, -1):
        _push_pv(waves[-j])
        pvn += 1
        while pending and pending[0][0] <= pvn:
            tev.append(("bcast", pending.pop(0)[1]))
    while pending:
        tev.append(("bcast", pending.pop(0)[1]))

    for kind, x in tev:
        if kind == "score":
            score_done[x] = cT.add(1)
        elif kind == "pv":
            pv_done[x] = cT.add(1)
        else:
            bcast_done[x] = cT.add(1)

    tailS_done = {}
    for w in waves:
        exp_done[w] = cS.add(1)
        if w[2] == 4 * w[1] + 3:
            tailS_done[(w[0], w[1])] = cS.add(1)

    for w in waves:
        b, qc, kt = w
        if kt >= 4 * qc:
            mask_done[w] = cV.add(1)
        if kt == 4 * qc + 3:
            tail1_done[(b, qc)] = cV.add(1)
            evac_done[(b, qc)] = cV.add(1)

    def p_ready(w):
        b, qc, kt = w
        if kt >= 4 * qc:
            return (env.sem_cV, mask_done[w])
        return (env.sem_cS, exp_done[w])

    ag_dma_done = {b: 16 * (dG0 + b + 1) for b in range(4)}
    cc0 = env.cc.n
    ag_cc = {b: cc0 + b + 1 for b in range(4)}

    dYp0 = [env.dYp[0].n, env.dYp[1].n]
    trans_done = {b: (env.sem_dYp[b % 2], 16 * (dYp0[b % 2] + 2 * (b // 2 + 1)))
                  for b in range(4)}

    with nc.Block() as blk:
        @blk.sync
        def _(y):
            for b in range(4):
                if b >= 2:
                    last = (b - 2, 7, 31)
                    y.wait_ge(env.sem_cT, score_done[last])
                y.dma_start_transpose(qT[:, (b % 2) * 4096:(b % 2 + 1) * 4096],
                                      dram["qh_sc"][b * 4096:(b + 1) * 4096, :],
                                      ).then_inc(env.sem_dYp[b % 2], 16)
                y.dma_start_transpose(kT[:, (b % 2) * 4096:(b % 2 + 1) * 4096],
                                      dram["kh_sc"][b * 4096:(b + 1) * 4096, :],
                                      ).then_inc(env.sem_dYp[b % 2], 16)

        @blk.tensor
        def _(t):
            def emit_score(w):
                b, qc, kt = w
                i = widx[w]
                if i == 0 or waves[i - 1][0] != b:
                    t.wait_ge(*trans_done[b])
                if i >= 2:
                    t.wait_ge(env.sem_cS, exp_done[waves[i - 2]])
                sl = i % 2
                koff = (b % 2) * 4096 + kt * 128
                qoff = (b % 2) * 4096 + qc * 512
                t.matmul(ps_s[sl][:, 0:512], kT[0:64, koff:koff + 128],
                         qT[0:64, qoff:qoff + 512], start=True, stop=True)
                t.matmul(ps_s[sl][:, 512:1024], kT[64:128, koff:koff + 128],
                         qT[64:128, qoff:qoff + 512], start=True,
                         stop=True).then_inc(env.sem_cT, 1)

            def emit_pv(w):
                b, qc, kt = w
                i = widx[w]
                sem, val = p_ready(w)
                t.wait_ge(sem, val)
                if kt == 0 and (b, qc) != (0, 0):
                    pb, pqc = (b, qc - 1) if qc > 0 else (b - 1, 7)
                    t.wait_ge(env.sem_cV, tail1_done[(pb, pqc)])
                    t.wait_ge(env.sem_cS, tailS_done[(pb, pqc)])
                sl = i % 3
                gt = b * 32 + kt
                st_ = (kt == 0)
                sp = (kt == 4 * qc + 3)
                t.matmul(ps_pv[0:65, 0:512], sb["V"][:, gt * 130:gt * 130 + 65],
                         pbf[:, sl * 1024:sl * 1024 + 512], start=st_, stop=sp)
                t.matmul(ps_pv[0:65, 512:1024], sb["V"][:, gt * 130 + 65:gt * 130 + 130],
                         pbf[:, sl * 1024 + 512:sl * 1024 + 1024], start=st_,
                         stop=sp).then_inc(env.sem_cT, 1)

            def emit_bcast(b, qc):
                t.wait_ge(env.sem_cS, tailS_done[(b, qc)])
                t.wait_ge(env.sem_cV, tail1_done[(b, qc)])
                t.matmul(ps_bc[:, :], sb["mrow_lo"][0:1, :], rbf[0:1, 0:512],
                         start=True, stop=False)
                t.matmul(ps_bc[:, :], sb["mrow_hi"][0:1, :], rbf[0:1, 512:1024],
                         start=False, stop=True).then_inc(env.sem_cT, 1)

            for kind, x in tev:
                if kind == "score":
                    emit_score(x)
                elif kind == "pv":
                    emit_pv(x)
                else:
                    emit_bcast(*x)

        @blk.scalar
        def _(s):
            for i, w in enumerate(waves):
                s.wait_ge(env.sem_cT, score_done[w])
                if i >= 3:
                    s.wait_ge(env.sem_cT, pv_done[waves[i - 3]])
                s.activation(pbf[:, (i % 3) * 1024:(i % 3 + 1) * 1024],
                             ps_s[i % 2][:, :], AF.Exp, bias=0.0,
                             scale=COEF).then_inc(env.sem_cS, 1)
                if w[2] == 4 * w[1] + 3:
                    s.wait_ge(env.sem_cT, pv_done[w])
                    s.activation(lnr[0:1, :], ps_pv[64:65, :], AF.Ln,
                                 bias=0.0, scale=1.0)
                    s.activation(rbf[0:1, :], lnr[0:1, :], AF.Exp, bias=0.0,
                                 scale=-1.0).then_inc(env.sem_cS, 1)

        @blk.vector
        def _(v):
            for i, w in enumerate(waves):
                b, qc, kt = w
                sl = i % 3
                if kt >= 4 * qc:
                    o = kt - 4 * qc
                    v.wait_ge(env.sem_cS, exp_done[w])
                    v.tensor_tensor(pbf[:, sl * 1024:sl * 1024 + 512],
                                    pbf[:, sl * 1024:sl * 1024 + 512],
                                    sb["masks"][:, o * 512:(o + 1) * 512], op=ALU.mult)
                    v.tensor_tensor(pbf[:, sl * 1024 + 512:(sl + 1) * 1024],
                                    pbf[:, sl * 1024 + 512:(sl + 1) * 1024],
                                    sb["masks"][:, o * 512:(o + 1) * 512],
                                    op=ALU.mult).then_inc(env.sem_cV, 1)
                if kt == 4 * qc + 3:
                    v.wait_ge(env.sem_cT, pv_done[w])
                    if qc == 0 and b >= 1:
                        v.wait_ge(env.sem_dG, ag_dma_done[b - 1])
                    v.tensor_copy(oev[0:64, :], ps_pv[0:64, 0:512])
                    v.tensor_copy(oev[64:128, :],
                                  ps_pv[0:64, 512:1024]).then_inc(env.sem_cV, 1)
                    v.wait_ge(env.sem_cT, bcast_done[(b, qc)])
                    v.tensor_tensor(outT[:, qc * 512:(qc + 1) * 512], oev[:, :],
                                    ps_bc[:, :], op=ALU.mult).then_inc(env.sem_cV, 1)

        @blk.gpsimd
        def _(g):
            for b in range(4):
                g.wait_ge(env.sem_cV, evac_done[(b, 7)])
                g.dma_start(out=dram["ago_in"][b, :, :], in_=outT[:, :],
                            ).then_inc(env.sem_dG, 16)
                g.wait_ge(env.sem_dG, ag_dma_done[b])
                g.collective_compute(
                    "AllGather", ALU.bypass, replica_groups=[list(range(NCORES))],
                    ins=[dram["ago_in"][b, :, :]], outs=[dram["ago_out"][b, :, :, :]],
                ).then_inc(env.sem_cc, 1)

    env.dYp[0].add(4)
    env.dYp[1].add(4)
    env.dG.add(4)
    env.cT.n = cT.n
    env.cS.n = cS.n
    env.cV.n = cV.n
    env.cc.add(4)
    env.ago_ready_cc = env.cc.n
    env.join()
    stl.close()


def emit_wo(nc, env, st_att, dram, sb):
    with ExitStack() as stl:
        e = stl.enter_context
        om = e(_sb(nc, "wo_om", [128, 8 * 2048], BF16))
        x1b = e(_sb(nc, "wo_x1", [128, 2 * 1024], F32))
        x2b = e(_sb(nc, "wo_x2", [128, 2 * 1024], F32))
        sqs = e(_sb(nc, "wo_sqs", [128, 1024], F32))
        ps_y = [e(_ps(nc, f"wo_ps{i}", [128, 1024], F32)) for i in range(2)]

        dY0, dG0, cT0, cV0 = env.dY.n, env.dG.n, env.cT.n, env.cV.n
        cS_sq0 = env.cS.n
        dYp0 = [env.dYp[0].n, env.dYp[1].n]
        dGp0 = [env.dGp[0].n, env.dGp[1].n]
        ago_cc = env.ago_ready_cc
        with nc.Block() as blk:
            @blk.gpsimd
            def _(g):
                g.wait_ge(env.sem_cc, ago_cc)
                with g.register("rk") as rk, g.register("r2") as r2, \
                        g.register("rh") as rh, g.register("ro") as ro:
                    g.cc_rank_ld(rk, replica_groups=[list(range(NCORES))])
                    g.reg_div(r2, rk, 2)
                    g.reg_mul(r2, r2, 8 * 128 * 4096)
                    g.reg_mod(rh, rk, 2)
                    g.reg_mul(rh, rh, 2048)
                    g.reg_add(r2, r2, rh)
                    for r in range(8):
                        g.reg_mov(ro, r * 128 * 4096)
                        g.reg_add(ro, ro, r2)
                        g.dma_start(
                            out=om[:, r * 2048:(r + 1) * 2048],
                            in_=bass.AP(dram["ago_out"], ro, [[4096, 128], [1, 2048]]),
                        ).then_inc(env.sem_dG, 16)
                for t128 in range(16):
                    g.wait_ge(env.sem_cV, cV0 + t128 + 1)
                    g.dma_start(out=dram["x2_sp"][t128 * 128:(t128 + 1) * 128, :],
                                in_=x2b[:, (t128 % 2) * 1024:(t128 % 2 + 1) * 1024],
                                ).then_inc(env.sem_dGp[t128 % 2], 16)

            @blk.sync
            def _(y):
                for t128 in range(16):
                    if t128 >= 2:
                        y.wait_ge(env.sem_cV, cV0 + t128 - 1)
                    y.dma_start(out=x1b[:, (t128 % 2) * 1024:(t128 % 2 + 1) * 1024],
                                in_=dram["x1_sp"][t128 * 128:(t128 + 1) * 128, :],
                                ).then_inc(env.sem_dYp[t128 % 2], 16)

            @blk.tensor
            def _(t):
                t.wait_ge(env.sem_dG, 16 * (dG0 + 8))
                for t128 in range(16):
                    if t128 >= 2:
                        t.wait_ge(env.sem_cV, cV0 + t128 - 1)
                    ps = ps_y[t128 % 2]
                    for r in range(8):
                        lhs = om[:, r * 2048 + t128 * 128: r * 2048 + (t128 + 1) * 128]
                        t.matmul(ps[:, 0:512], lhs, sb["woT"][:, r * 1024:r * 1024 + 512],
                                 start=(r == 0), stop=False)
                        t.matmul(ps[:, 512:1024], lhs,
                                 sb["woT"][:, r * 1024 + 512:(r + 1) * 1024],
                                 start=(r == 0), stop=False)
                    t.matmul(ps[:, 0:512], sb["ones_row"][0:1, :],
                             sb["halfbo_bf"][0:1, 0:512], start=False, stop=True)
                    t.matmul(ps[:, 512:1024], sb["ones_row"][0:1, :],
                             sb["halfbo_bf"][0:1, 512:1024], start=False,
                             stop=True).then_inc(env.sem_cT, 1)

            @blk.vector
            def _(v):
                def red3(t):
                    xs = x2b[:, (t % 2) * 1024:(t % 2 + 1) * 1024]
                    v.tensor_reduce(out=sb["lnred"][:, t:t + 1], in_=xs,
                                    axis=AX.X, op=ALU.add)
                    v.tensor_reduce(out=sb["lnred"][:, 16 + t:17 + t], in_=xs,
                                    axis=AX.X, op=ALU.max)
                    v.tensor_reduce(out=sb["lnred"][:, 32 + t:33 + t], in_=xs,
                                    axis=AX.X, op=ALU.min)
                for t128 in range(16):
                    v.wait_ge(env.sem_cT, cT0 + t128 + 1)
                    v.wait_ge(env.sem_dYp[t128 % 2], 16 * (dYp0[t128 % 2] + t128 // 2 + 1))
                    if t128 >= 2:
                        v.wait_ge(env.sem_dGp[t128 % 2],
                                  16 * (dGp0[t128 % 2] + (t128 - 2) // 2 + 1))
                        v.wait_ge(env.sem_cS, cS_sq0 + t128 - 1)
                    v.scalar_tensor_tensor(
                        out=x2b[:, (t128 % 2) * 1024:(t128 % 2 + 1) * 1024],
                        in0=ps_y[t128 % 2][:, :], scalar=0.5,
                        in1=x1b[:, (t128 % 2) * 1024:(t128 % 2 + 1) * 1024],
                        op0=ALU.mult, op1=ALU.add).then_inc(env.sem_cV, 1)
                    if t128 >= 1:
                        red3(t128 - 1)
                red3(15)

            @blk.scalar
            def _(s):
                for t in range(16):
                    s.wait_ge(env.sem_cV, cV0 + t + 1)
                    s.activation(sqs[:, :], x2b[:, (t % 2) * 1024:(t % 2 + 1) * 1024],
                                 AF.Square, bias=0.0, scale=1.0,
                                 accum_out=sb["lnred"][:, 48 + t:49 + t],
                                 ).then_inc(env.sem_cS, 1)
        env.dYp[0].add(8)
        env.dYp[1].add(8)
        env.dG.add(8)
        env.dGp[0].add(8)
        env.dGp[1].add(8)
        env.cT.add(16)
        env.cV.add(16)
        env.cS.add(16)
        env.join()


def build_program(stop_after=None, debug_dump=None):
    nc = bass.Bass("TRN2", target_bir_lowering=False, detect_race_conditions=False)
    dram = {}
    dram["x0"] = nc.dram_tensor("x0", [2048, 1024], F32, kind="ExternalInput")
    dram["wqkvT"] = nc.dram_tensor("wqkvT", [1024, 384], F32, kind="ExternalInput")
    dram["bqkv"] = nc.dram_tensor("bqkv", [1, 384], F32, kind="ExternalInput")
    dram["woT"] = nc.dram_tensor("woT", [1024, 1024], F32, kind="ExternalInput")
    dram["bo_h"] = nc.dram_tensor("bo_h", [1, 1024], F32, kind="ExternalInput")
    dram["w1Tsh"] = nc.dram_tensor("w1Tsh", [1024, 512], F32, kind="ExternalInput")
    dram["b1r"] = nc.dram_tensor("b1r", [1, 4096], F32, kind="ExternalInput")
    dram["w2Tsh"] = nc.dram_tensor("w2Tsh", [4096, 128], F32, kind="ExternalInput")
    dram["b2r"] = nc.dram_tensor("b2r", [1, 1024], F32, kind="ExternalInput")
    dram["out"] = nc.dram_tensor("out", [2048, 1024], F32, kind="ExternalOutput")

    dram["arw_in"] = nc.dram_tensor("arw_in", [1, 8], F32)
    dram["arw_out"] = nc.dram_tensor("arw_out", [1, 8], F32)
    dram["agw1_in"] = nc.dram_tensor("agw1_in", [1024, 512], BF16)
    dram["agw1_out"] = nc.dram_tensor("agw1_out", [8, 1024, 512], BF16, addr_space="Shared")
    dram["agw2_in"] = nc.dram_tensor("agw2_in", [4096, 128], BF16)
    dram["agw2_out"] = nc.dram_tensor("agw2_out", [8, 4096, 128], BF16, addr_space="Shared")
    dram["xq_sc"] = nc.dram_tensor("xq_sc", [2048, 1024], BF16)
    dram["hq_sc"] = nc.dram_tensor("hq_sc", [2048, 4096], BF16)
    dram["agx_inA"] = nc.dram_tensor("agx_inA", [1024, 1024], BF16)
    dram["agx_inB"] = nc.dram_tensor("agx_inB", [1024, 1024], BF16)
    dram["agx_outA"] = nc.dram_tensor("agx_outA", [8, 1024, 1024], BF16, addr_space="Shared")
    dram["agx_outB"] = nc.dram_tensor("agx_outB", [8, 1024, 1024], BF16, addr_space="Shared")
    dram["qh_sc"] = nc.dram_tensor("qh_sc", [16384, 128], BF16)
    dram["kh_sc"] = nc.dram_tensor("kh_sc", [16384, 128], BF16)
    dram["ago_in"] = nc.dram_tensor("ago_in", [4, 128, 4096], BF16)
    dram["ago_out"] = nc.dram_tensor("ago_out", [4, 8, 128, 4096], BF16, addr_space="Shared")
    dram["x1_sp"] = nc.dram_tensor("x1_sp", [2048, 1024], F32)
    dram["x2_sp"] = nc.dram_tensor("x2_sp", [2048, 1024], F32)

    with ExitStack() as st:
        env = Env(nc, st)
        sb = {}
        sb["ones_row"] = st.enter_context(_sb(nc, "ones_row", [1, 128], BF16))
        emit_setup(nc, env, st, dram, sb)
        with nc.Block() as blk:
            @blk.vector
            def _(v):
                v.memset(sb["ones_row"][:, :], 1.0).then_inc(env.sem_cV, 1)
        env.cV.add(1)

        def _fin():
            if debug_dump is not None:
                env.join()
                with nc.Block() as blk:
                    @blk.gpsimd
                    def _(g):
                        if debug_dump.startswith("@"):
                            src_s = sb[debug_dump[1:]]
                            w_ = src_s.shape[1]
                            if w_ <= 1024:
                                g.dma_start(out=dram["out"][0:128, 0:w_],
                                            in_=src_s[:, :]).then_inc(env.sem_dG, 16)
                            else:
                                nrow = min(2048, (128 * w_) // 1024)
                                g.dma_start(
                                    out=dram["out"][0:nrow, :],
                                    in_=src_s[:, 0:(nrow * 1024) // 128],
                                    ).then_inc(env.sem_dG, 16)
                        else:
                            nd = dram[debug_dump]
                            tot = 1
                            for d_ in nd.shape:
                                tot *= d_
                            nr = min(2048, tot // 1024)
                            g.dma_start(out=dram["out"][0:nr, :],
                                        in_=bass.AP(nd, 0, [[1024, nr], [1, 1024]]),
                                        ).then_inc(env.sem_dG, 16)
                env.dG.add(1)
            with nc.Block() as blk:
                @blk.sync
                def _(y):
                    y.wait_ge(env.sem_dY, 16 * env.dY.n)

                @blk.gpsimd
                def _(g):
                    g.wait_ge(env.sem_dG, 16 * env.dG.n)
                    g.wait_ge(env.sem_cc, env.cc.n)

        emit_xchain(nc, env, st, dram, sb, "x0", park_off=0,
                    mid_emit=lambda: emit_setup_tail(nc, env, dram, sb))
        close_wtern(sb)
        if stop_after == "xchain1":
            _fin(); close_xqT(sb); return nc
        emit_ffn(nc, env, dram, sb, "x0", "x1_sp", park_off=0, ln_stats=True)
        if stop_after == "ffn1":
            _fin(); return nc
        open_xqT(nc, sb)
        emit_ln_to_bf(nc, env, dram, sb, "x1_sp", "xq_sc")
        if stop_after == "ln2":
            _fin(); close_xqT(sb); return nc
        emit_agx(nc, env, dram, sb)
        if stop_after == "agx":
            _fin(); close_xqT(sb); return nc
        stop_now = False
        with ExitStack() as st_att:
            emit_qkv(nc, env, st_att, dram, sb)
            if stop_after == "qkv":
                _fin(); stop_now = True
            if not stop_now:
                emit_attention(nc, env, st_att, dram, sb)
                if stop_after == "attn":
                    _fin(); stop_now = True
            if not stop_now:
                emit_wo(nc, env, st_att, dram, sb)
                env.join()
                if stop_after == "wo":
                    _fin(); stop_now = True
        close_xqT(sb)
        if stop_now:
            return nc
        open_xqT(nc, sb)
        emit_xchain(nc, env, st, dram, sb, "x2_sp", park_off=0, stats=True)
        if stop_after == "xchain3":
            _fin(); close_xqT(sb); return nc
        emit_ffn(nc, env, dram, sb, "x2_sp", "out", park_off=0)

        with nc.Block() as blk:
            @blk.sync
            def _(y):
                y.wait_ge(env.sem_dY, 16 * env.dY.n)

            @blk.gpsimd
            def _(g):
                g.wait_ge(env.sem_dG, 16 * env.dG.n)
    return nc


_CACHE = {}


def kernel(L0_attn_x, L0_attn_Wq, L0_attn_bq, L0_attn_Wk, L0_attn_bk,
           L0_attn_Wv, L0_attn_bv, L0_attn_Wo, L0_attn_bo,
           L0_ffn_w1, L0_ffn_b1, L0_ffn_w2, L0_ffn_b2):
    x = np.asarray(L0_attn_x, dtype=np.float32)
    Wq = np.asarray(L0_attn_Wq, np.float32)
    Wk = np.asarray(L0_attn_Wk, np.float32)
    Wv = np.asarray(L0_attn_Wv, np.float32)
    Wo = np.asarray(L0_attn_Wo, np.float32)
    bq = np.asarray(L0_attn_bq, np.float32)
    bk = np.asarray(L0_attn_bk, np.float32)
    bv = np.asarray(L0_attn_bv, np.float32)
    bo = np.asarray(L0_attn_bo, np.float32)
    w1 = np.asarray(L0_ffn_w1, np.float32)
    b1 = np.asarray(L0_ffn_b1, np.float32)
    w2 = np.asarray(L0_ffn_w2, np.float32)
    b2 = np.asarray(L0_ffn_b2, np.float32)

    if "nc" not in _CACHE:
        _CACHE["nc"] = build_program()
    nc = _CACHE["nc"]
    in_maps = make_in_maps(x, Wq, bq, Wk, bk, Wv, bv, Wo, bo, w1, b1, w2, b2)
    res = run_bass_kernel_spmd(nc, in_maps, core_ids=list(range(NCORES)))
    out = np.empty((4, 4096, 1024), np.float32)
    for c in range(NCORES):
        out[c // 2, (c % 2) * 2048:((c % 2) + 1) * 2048, :] = res.results[c]["out"]
    return out


def make_in_maps(x, Wq, bq, Wk, bk, Wv, bv, Wo, bo, w1, b1, w2, b2):
    w1T = np.ascontiguousarray(w1.T)
    w2T = np.ascontiguousarray(w2.T)
    woT = np.ascontiguousarray(Wo.T)
    in_maps = []
    for c in range(NCORES):
        sbq, half = c // 2, c % 2
        hsl = slice(128 * c, 128 * (c + 1))
        in_maps.append({
            "x0": np.ascontiguousarray(x[sbq, half * 2048:(half + 1) * 2048, :]),
            "wqkvT": np.ascontiguousarray(
                np.concatenate([Wq[hsl], Wk[hsl], Wv[hsl]], axis=0).T),
            "bqkv": np.concatenate([bq[hsl], bk[hsl], bv[hsl]])[None, :].copy(),
            "woT": woT,
            "bo_h": bo[None, :].copy(),
            "w1Tsh": np.ascontiguousarray(w1T[:, 512 * c:512 * (c + 1)]),
            "b1r": b1[None, :].copy(),
            "w2Tsh": np.ascontiguousarray(w2T[:, 128 * c:128 * (c + 1)]),
            "b2r": b2[None, :].copy(),
        })
    return in_maps


# version 13
# speedup vs baseline: 1.0186x; 1.0115x over previous
import numpy as np
from contextlib import ExitStack

import concourse.bass as bass
import concourse.mybir as mybir
from concourse.bass_utils import run_bass_kernel_spmd

F32 = mybir.dt.float32
BF16 = mybir.dt.bfloat16
AF = mybir.ActivationFunctionType
ALU = mybir.AluOpType
AX = mybir.AxisListType

MAGIC = float(np.float32(1.5 * 2.0**23))
NCORES = 8
COEF = float(np.float32((1.0 / np.sqrt(128.0)) / 1.5))



_NAME_N = [0]


def _sb(nc, name, shape, dtype):
    _NAME_N[0] += 1
    return nc.sbuf_tensor(f"{name}_{_NAME_N[0]}", shape, dtype)


def _ps(nc, name, shape, dtype):
    _NAME_N[0] += 1
    return nc.psum_tensor(f"{name}_{_NAME_N[0]}", shape, dtype)


class Counter:
    __slots__ = ("n",)

    def __init__(self, n=0):
        self.n = n

    def add(self, k=1):
        self.n += k
        return self.n


class Env:
    def __init__(self, nc, stack):
        self.nc = nc
        e = stack.enter_context
        self.sem_dY = e(nc.semaphore("dY"))
        self.sem_dG = e(nc.semaphore("dG"))
        self.sem_dYp = [e(nc.semaphore("dYa")), e(nc.semaphore("dYb"))]
        self.sem_dGp = [e(nc.semaphore("dGa")), e(nc.semaphore("dGb"))]
        self.sem_dOp = [e(nc.semaphore("dOa")), e(nc.semaphore("dOb"))]
        self.sem_dSp = [e(nc.semaphore("dSa")), e(nc.semaphore("dSb"))]
        self.sem_cT = e(nc.semaphore("cT"))
        self.sem_cV = e(nc.semaphore("cV"))
        self.sem_cS = e(nc.semaphore("cS"))
        self.sem_cc = e(nc.semaphore("cc"))
        self.sem_rc = e(nc.semaphore("rc"))
        self.sem_j = e(nc.semaphore("jn"))
        self.dY = Counter()
        self.dG = Counter()
        self.dYp = [Counter(), Counter()]
        self.dGp = [Counter(), Counter()]
        self.dOp = [Counter(), Counter()]
        self.dSp = [Counter(), Counter()]
        self.cT = Counter()
        self.cV = Counter()
        self.cS = Counter()
        self.cc = Counter()
        self.rc = Counter()
        self.joins = 0

    def join(self):
        nc = self.nc
        self.joins += 1
        target = 5 * self.joins
        dYn, dGn = 16 * self.dY.n, 16 * self.dG.n
        dYp = [16 * c.n for c in self.dYp]
        dGp = [16 * c.n for c in self.dGp]
        dOp = [16 * c.n for c in self.dOp]
        dSp = [16 * c.n for c in self.dSp]
        with nc.Block() as blk:
            def mk(name):
                def body(eng):
                    if name == "sync":
                        eng.wait_ge(self.sem_dY, dYn)
                        eng.wait_ge(self.sem_dYp[0], dYp[0])
                        eng.wait_ge(self.sem_dYp[1], dYp[1])
                        eng.wait_ge(self.sem_dOp[0], dOp[0])
                        eng.wait_ge(self.sem_dOp[1], dOp[1])
                    if name == "gpsimd":
                        eng.wait_ge(self.sem_dG, dGn)
                        eng.wait_ge(self.sem_dGp[0], dGp[0])
                        eng.wait_ge(self.sem_dGp[1], dGp[1])
                        eng.wait_ge(self.sem_dSp[0], dSp[0])
                        eng.wait_ge(self.sem_dSp[1], dSp[1])
                    eng.nop().then_inc(self.sem_j, 1)
                    eng.wait_ge(self.sem_j, target)
                return body
            blk.tensor(mk("tensor"))
            blk.vector(mk("vector"))
            blk.scalar(mk("scalar"))
            blk.gpsimd(mk("gpsimd"))
            blk.sync(mk("sync"))


def rt(env, v, producer):
    producer.then_inc(env.sem_rc, 1)
    v.wait_ge(env.sem_rc, env.rc.add(1))
    return producer


def safe_recip(env, v, out_ap, in_ap, producer):
    rt(env, v, producer)
    return v.reciprocal(out_ap, in_ap)




def emit_setup(nc, env, st, dram, sb):
    e = st.enter_context
    sb["b1_bf"] = e(_sb(nc, "b1_bf", [1, 4096], BF16))
    sb["b2_bf"] = e(_sb(nc, "b2_bf", [1, 1024], BF16))
    sb["halfbo_bf"] = e(_sb(nc, "halfbo_bf", [1, 1024], BF16))
    sb["bqkv_bc"] = e(_sb(nc, "bqkv_bc", [128, 384], F32))
    sb["mrow_lo"] = e(_sb(nc, "mrow_lo", [1, 128], BF16))
    sb["mrow_hi"] = e(_sb(nc, "mrow_hi", [1, 128], BF16))
    sb["ones_f32r"] = e(_sb(nc, "ones_f32r", [1, 128], F32))
    sb["ones_f32c"] = e(_sb(nc, "ones_f32c", [128, 1], F32))
    sb["ident"] = e(_sb(nc, "ident", [128, 128], BF16))
    sb["masks"] = e(_sb(nc, "masks", [128, 4 * 512], BF16))
    sb["wqkvT"] = e(_sb(nc, "wqkvT_sb", [128, 8 * 384], BF16))
    sb["wsc"] = e(_sb(nc, "wsc", [128, 4], F32))
    sb["parks"] = e(_sb(nc, "parks", [128, 96], F32))
    sb["crows"] = e(_sb(nc, "crows", [1, 4096], BF16))
    sb["lnred"] = e(_sb(nc, "lnred", [128, 64], F32))

    open_xqT(nc, sb)

    grp = []

    def gsb(name, shape, dt):
        cm = _sb(nc, name, shape, dt)
        t = cm.__enter__()
        grp.append(cm)
        return t

    def gps(name, shape, dt):
        cm = _ps(nc, name, shape, dt)
        t = cm.__enter__()
        grp.append(cm)
        return t

    with ExitStack() as stl:
        w1t = gsb("w1sh", [128, 4096], F32)
        w2t = gsb("w2sh", [128, 4096], F32)
        tern1 = gsb("tern1", [128, 4096], BF16)
        tern2 = gsb("tern2", [128, 4096], BF16)
        wabs = gsb("wabs", [128, 2], F32)
        wtmp = gsb("wtmp", [128, 2048], F32)
        arbuf = gsb("arbuf", [1, 8], F32)
        iot = gsb("iot", [128, 512], F32)
        bq_ps = gps("bq_ps", [128, 384], F32)
        ar_ps = gps("ar_ps", [128, 8], F32)
        sb["_wt"] = dict(grp=grp, w1t=w1t, w2t=w2t, tern1=tern1, tern2=tern2,
                         wtmp=wtmp, arbuf=arbuf, ar_ps=ar_ps)

        dY0 = env.dY.n
        dG0 = env.dG.n
        with nc.Block() as blk:
            @blk.sync
            def _(y):
                y.dma_start(out=sb["bqkv_bc"][0:1, :], in_=dram["bqkv"][:, :]).then_inc(env.sem_dY, 16)

            @blk.gpsimd
            def _(g):
                for j in range(8):
                    g.dma_start(out=w1t[:, j * 512:(j + 1) * 512],
                                in_=dram["w1Tsh"][j * 128:(j + 1) * 128, :]).then_inc(env.sem_dG, 16)
                for j in range(32):
                    g.dma_start(out=w2t[:, j * 128:(j + 1) * 128],
                                in_=dram["w2Tsh"][j * 128:(j + 1) * 128, :]).then_inc(env.sem_dG, 16)
                for j in range(8):
                    g.dma_start(out=sb["wqkvT"][:, j * 384:(j + 1) * 384],
                                in_=dram["wqkvT"][j * 128:(j + 1) * 128, :]).then_inc(env.sem_dG, 16)
                g.dma_start(out=sb["b1_bf"][:, :], in_=dram["b1r"][:, :]).then_inc(env.sem_dG, 16)
                g.dma_start(out=sb["b2_bf"][:, :], in_=dram["b2r"][:, :]).then_inc(env.sem_dG, 16)
                g.dma_start(out=sb["halfbo_bf"][:, :], in_=dram["bo_h"][:, :]).then_inc(env.sem_dG, 16)
                g.iota(iot[:, :], [[1, 512]], base=0, channel_multiplier=-1,
                       allow_small_or_imprecise_dtypes=True).then_inc(env.sem_cV, 1)
        env.dY.add(1)
        env.dG.add(51)
        cV_iota = env.cV.add(1)

        dY1 = env.dY.n
        dG1 = env.dG.n
        with nc.Block() as blk:
            @blk.vector
            def _(v):
                v.wait_ge(env.sem_cV, cV_iota)
                for o in range(4):
                    v.tensor_scalar(out=sb["masks"][:, o * 512:(o + 1) * 512],
                                    in0=iot[:, :], scalar1=float(o * 128),
                                    scalar2=None, op0=ALU.is_ge)
                v.tensor_scalar(out=sb["ident"][:, :], in0=iot[:, 0:128],
                                scalar1=0.0, scalar2=None, op0=ALU.is_equal)
                v.memset(sb["mrow_lo"][0:1, 0:64], 1.0)
                v.memset(sb["mrow_lo"][0:1, 64:128], 0.0)
                v.memset(sb["mrow_hi"][0:1, 0:64], 0.0)
                v.memset(sb["mrow_hi"][0:1, 64:128], 1.0)
                v.memset(sb["ones_f32r"][:, :], 1.0)
                v.memset(sb["ones_f32c"][:, :], 1.0)
                v.wait_ge(env.sem_dG, 16 * dG1)
                v.wait_ge(env.sem_dY, 16 * dY1)
                v.nop().then_inc(env.sem_cV, 1)

            @blk.scalar
            def _(s):
                s.wait_ge(env.sem_dG, 16 * dG1)
                s.activation(tern1[:, :], w1t[:, :], AF.Abs, bias=0.0, scale=1.0,
                             accum_out=wabs[:, 0:1]).then_inc(env.sem_cS, 1)
                s.activation(tern2[:, :], w2t[:, :], AF.Abs, bias=0.0, scale=1.0,
                             accum_out=wabs[:, 1:2]).then_inc(env.sem_cS, 1)
        cV_pre = env.cV.add(1)
        cS_abs = env.cS.add(2)

        with nc.Block() as blk:
            @blk.tensor
            def _(t):
                t.wait_ge(env.sem_cV, cV_pre)
                t.matmul(bq_ps[:, :], sb["ones_f32r"][0:1, :],
                         sb["bqkv_bc"][0:1, :], start=True, stop=True).then_inc(env.sem_cT, 1)
                t.wait_ge(env.sem_cS, cS_abs)
                t.matmul(ar_ps[0:1, 0:2], sb["ones_f32c"][:, :], wabs[:, :],
                         start=True, stop=True).then_inc(env.sem_cT, 1)
        cT_bc = env.cT.add(2)

        with nc.Block() as blk:
            @blk.vector
            def _(v):
                v.wait_ge(env.sem_cT, cT_bc - 1)
                v.tensor_copy(sb["bqkv_bc"][:, :], bq_ps[:, :])
                v.wait_ge(env.sem_cT, cT_bc)
                v.tensor_copy(arbuf[:, 0:2], ar_ps[0:1, 0:2])
                v.memset(arbuf[:, 2:8], 0.0).then_inc(env.sem_cV, 1)
        cV_ar = env.cV.add(1)

        dG2 = env.dG.n
        with nc.Block() as blk:
            @blk.gpsimd
            def _(g):
                g.wait_ge(env.sem_cV, cV_ar)
                g.dma_start(out=dram["arw_in"][:, :], in_=arbuf[:, :]).then_inc(env.sem_dG, 16)
                g.wait_ge(env.sem_dG, 16 * (dG2 + 1))
                g.collective_compute(
                    "AllReduce", ALU.add, replica_groups=[list(range(NCORES))],
                    ins=[dram["arw_in"][:, :]], outs=[dram["arw_out"][:, :]],
                ).then_inc(env.sem_cc, 1)
                g.wait_ge(env.sem_cc, env.cc.n + 1)
                g.dma_start(out=arbuf[:, :], in_=dram["arw_out"][:, :]).then_inc(env.sem_dG, 16)
        env.dG.add(2)
        env.cc.add(1)
        sb["_wt"]["dG3"] = env.dG.n


def emit_setup_tail(nc, env, dram, sb):
    wt = sb["_wt"]
    w1t, w2t, tern1, tern2 = wt["w1t"], wt["w2t"], wt["tern1"], wt["tern2"]
    wtmp, arbuf, ar_ps, dG3 = wt["wtmp"], wt["arbuf"], wt["ar_ps"], wt["dG3"]
    with nc.Block() as blk:
        @blk.vector
        def _(v):
            v.wait_ge(env.sem_dG, 16 * dG3)
            pr = v.tensor_scalar(out=arbuf[:, 0:2], in0=arbuf[:, 0:2],
                                 scalar1=1.0 / (4096.0 * 1024.0), scalar2=1e-5,
                                 op0=ALU.mult, op1=ALU.max)
            rr = safe_recip(env, v, arbuf[:, 2:4], arbuf[:, 0:2], pr)
            rt(env, v, rr)
            v.tensor_copy(sb["wsc"][0:1, 0:2], arbuf[:, 2:4])
            v.tensor_copy(sb["wsc"][0:1, 2:4], arbuf[:, 0:2]).then_inc(env.sem_cV, 1)
    cV_w0 = env.cV.add(1)
    with nc.Block() as blk:
        @blk.tensor
        def _(t):
            t.wait_ge(env.sem_cV, cV_w0)
            t.matmul(ar_ps[:, 0:4], sb["ones_f32r"][0:1, :], sb["wsc"][0:1, :],
                     start=True, stop=True).then_inc(env.sem_cT, 1)
    cT_w = env.cT.add(1)
    with nc.Block() as blk:
        @blk.vector
        def _(v):
            v.wait_ge(env.sem_cT, cT_w)
            rt(env, v, v.tensor_copy(sb["wsc"][:, :], ar_ps[:, 0:4]))
            for (wt_, tt, col) in ((w1t, tern1, 0), (w2t, tern2, 1)):
                for hf in range(2):
                    hs = slice(hf * 2048, (hf + 1) * 2048)
                    rt(env, v, v.tensor_scalar(out=wtmp[:, :], in0=wt_[:, hs],
                                               scalar1=sb["wsc"][:, col:col + 1],
                                               scalar2=None, op0=ALU.mult))
                    rt(env, v, v.tensor_scalar(out=wtmp[:, :], in0=wtmp[:, :],
                                               scalar1=1.0, scalar2=-1.0,
                                               op0=ALU.min, op1=ALU.max))
                    rt(env, v, v.tensor_scalar(out=tt[:, hs], in0=wtmp[:, :],
                                               scalar1=MAGIC, scalar2=MAGIC,
                                               op0=ALU.add, op1=ALU.subtract))
            v.nop().then_inc(env.sem_cV, 1)
    cV_tern = env.cV.add(1)

    with nc.Block() as blk:
        @blk.gpsimd
        def _(g):
            g.wait_ge(env.sem_cV, cV_tern)
            for j in range(8):
                g.dma_start(out=dram["agw1_in"][j * 128:(j + 1) * 128, :],
                            in_=tern1[:, j * 512:(j + 1) * 512]).then_inc(env.sem_dG, 16)
            for j in range(32):
                g.dma_start(out=dram["agw2_in"][j * 128:(j + 1) * 128, :],
                            in_=tern2[:, j * 128:(j + 1) * 128]).then_inc(env.sem_dG, 16)
            g.wait_ge(env.sem_dG, 16 * (dG3 + 40))
            g.collective_compute(
                "AllGather", ALU.bypass, replica_groups=[list(range(NCORES))],
                ins=[dram["agw1_in"][:, :]], outs=[dram["agw1_out"][:, :, :]],
            ).then_inc(env.sem_cc, 1)
            g.collective_compute(
                "AllGather", ALU.bypass, replica_groups=[list(range(NCORES))],
                ins=[dram["agw2_in"][:, :]], outs=[dram["agw2_out"][:, :, :]],
            ).then_inc(env.sem_cc, 1)
    env.dG.add(40)
    env.agw_ready_cc = env.cc.add(2)


def close_wtern(sb):
    for cm in reversed(sb.pop("_wt")["grp"]):
        cm.__exit__(None, None, None)


def open_xqT(nc, sb):
    cm = _sb(nc, "xqT_s", [128, 16 * 1024], BF16)
    sb["xqT"] = cm.__enter__()
    sb["_xqT_cm"] = cm


def close_xqT(sb):
    sb.pop("_xqT_cm").__exit__(None, None, None)
    sb.pop("xqT")


def emit_xchain(nc, env, st_unused, dram, sb, src_name, park_off,
                mid_emit=None, stats=False):
    P = park_off
    src = dram[src_name]
    with ExitStack() as stl:
        e = stl.enter_context
        xa = e(_sb(nc, "xc_a", [128, 16 * 1024], F32))
        red = e(_sb(nc, "xc_red", [128, 64], F32))
        sq = e(_sb(nc, "xc_sq", [128, 1024], F32))
        wk = e(_sb(nc, "xc_wk", [128, 80], F32))
        zz = e(_sb(nc, "xc_z", [128, 1024], F32))
        xq = e(_sb(nc, "xc_xq", [128, 2 * 1024], BF16))
        cpark_bf = e(_sb(nc, "xc_cpb", [128, 16], BF16))
        crow_sb = e(_sb(nc, "xc_crsb", [16, 128], BF16))
        ps_c = e(_ps(nc, "xc_ps", [128, 128], F32))

        gsems = [env.sem_dYp[0], env.sem_dYp[1], env.sem_dOp[0], env.sem_dOp[1]]
        gctr = [env.dYp[0], env.dYp[1], env.dOp[0], env.dOp[1]]
        g0 = [c.n for c in gctr]
        cV0, cS0 = env.cV.n, env.cS.n
        def reduce_half(tr):
            cvb = env.cV.n
            with nc.Block() as blk:
                if tr.start == 0:
                    @blk.sync
                    def _(y):
                        for t in range(16):
                            y.dma_start(out=xa[:, t * 1024:(t + 1) * 1024],
                                        in_=src[t * 128:(t + 1) * 128, :]).then_inc(gsems[t // 4], 16)

                @blk.vector
                def _(v):
                    for t in tr:
                        v.wait_ge(gsems[t // 4], 16 * (g0[t // 4] + 4))
                        xt = xa[:, t * 1024:(t + 1) * 1024]
                        v.tensor_reduce(out=red[:, t:t + 1], in_=xt, axis=AX.X, op=ALU.add)
                        v.tensor_reduce(out=red[:, 16 + t:17 + t], in_=xt, axis=AX.X, op=ALU.max)
                        v.tensor_reduce(out=red[:, 32 + t:33 + t], in_=xt, axis=AX.X,
                                        op=ALU.min).then_inc(env.sem_cV, 1)

                @blk.scalar
                def _(s):
                    for i, t in enumerate(tr):
                        s.wait_ge(env.sem_cV, cvb + i + 1)
                        xt = xa[:, t * 1024:(t + 1) * 1024]
                        s.activation(sq[:, :], xt, AF.Square, bias=0.0, scale=1.0,
                                     accum_out=red[:, 48 + t:49 + t]).then_inc(env.sem_cS, 1)
            env.cV.add(len(tr))

        if stats:
            red = sb["lnred"]
            with nc.Block() as blk:
                @blk.sync
                def _(y):
                    for t in range(16):
                        y.dma_start(out=xa[:, t * 1024:(t + 1) * 1024],
                                    in_=src[t * 128:(t + 1) * 128, :]).then_inc(gsems[t // 4], 16)
            for c in gctr:
                c.add(4)
            cS_red = env.cS.n
        else:
            reduce_half(range(0, 8))
            if mid_emit is not None:
                mid_emit()
            reduce_half(range(8, 16))
            for c in gctr:
                c.add(4)
            cS_red = env.cS.add(16)

        with nc.Block() as blk:
            @blk.vector
            def _(v):
                v.wait_ge(env.sem_cS, cS_red)
                m = sb["parks"][:, P:P + 16]
                v.tensor_scalar(out=m, in0=red[:, 0:16], scalar1=1.0 / 1024.0,
                                scalar2=None, op0=ALU.mult)
                rt(env, v, v.tensor_scalar(out=zz[:, 0:16], in0=red[:, 48:64],
                                           scalar1=1.0 / 1024.0, scalar2=None, op0=ALU.mult))
                rt(env, v, v.tensor_tensor(zz[:, 16:32], m, m, op=ALU.mult))
                rt(env, v, v.tensor_tensor(zz[:, 0:16], zz[:, 0:16], zz[:, 16:32],
                                           op=ALU.subtract))
                pr = v.tensor_scalar(out=wk[:, 0:16], in0=zz[:, 0:16], scalar1=1e-5,
                                     scalar2=None, op0=ALU.add)
                rr = safe_recip(env, v, wk[:, 16:32], wk[:, 0:16], pr)
                rt(env, v, rr)
                rt(env, v, v.tensor_tensor(zz[:, 16:32], zz[:, 0:16], wk[:, 16:32],
                                           op=ALU.mult))
                pr = v.tensor_scalar(out=zz[:, 16:32], in0=zz[:, 16:32], scalar1=1e-5,
                                     scalar2=None, op0=ALU.add)
                rr = safe_recip(env, v, zz[:, 32:48], zz[:, 16:32], pr)
                rt(env, v, rr)
                rt(env, v, v.tensor_tensor(wk[:, 32:48], wk[:, 16:32], zz[:, 32:48],
                                           op=ALU.mult))
                v.tensor_tensor(zz[:, 0:16], red[:, 16:32], m, op=ALU.subtract)
                rt(env, v, v.tensor_tensor(zz[:, 16:32], m, red[:, 32:48], op=ALU.subtract))
                rt(env, v, v.tensor_tensor(wk[:, 48:64], zz[:, 0:16], zz[:, 16:32],
                                           op=ALU.max))
                pr = v.tensor_scalar(out=zz[:, 0:16], in0=wk[:, 48:64], scalar1=1e-5,
                                scalar2=None, op0=ALU.max)
                rr = safe_recip(env, v, zz[:, 16:32], zz[:, 0:16], pr)
                rt(env, v, rr)
                v.tensor_scalar(out=sb["parks"][:, P + 48:P + 64], in0=zz[:, 16:32],
                                scalar1=127.0, scalar2=None,
                                op0=ALU.mult).then_inc(env.sem_cV, 1)
        cV_b1 = env.cV.add(1)

        with nc.Block() as blk:
            @blk.scalar
            def _(s):
                s.wait_ge(env.sem_cV, cV_b1)
                s.activation(wk[:, 32:48], wk[:, 32:48], AF.Sqrt, bias=0.0,
                             scale=1.0).then_inc(env.sem_cS, 1)
        cS_g = env.cS.add(1)

        with nc.Block() as blk:
            @blk.vector
            def _(v):
                v.wait_ge(env.sem_cS, cS_g)
                rt(env, v, v.tensor_tensor(wk[:, 64:80], wk[:, 32:48], wk[:, 48:64],
                                           op=ALU.mult))
                rt(env, v, v.tensor_scalar(out=wk[:, 64:80], in0=wk[:, 64:80], scalar1=1e-5,
                                           scalar2=None, op0=ALU.max))
                rt(env, v, v.tensor_scalar(out=wk[:, 64:80], in0=wk[:, 64:80],
                                           scalar1=sb["wsc"][:, 2:3], scalar2=None,
                                           op0=ALU.mult))
                pr = v.tensor_scalar(out=sb["parks"][:, P + 32:P + 48], in0=wk[:, 64:80],
                                     scalar1=1.0 / 127.0, scalar2=None, op0=ALU.mult)
                rr = safe_recip(env, v, wk[:, 0:16], sb["parks"][:, P + 32:P + 48], pr)
                rt(env, v, rr)
                v.tensor_copy(cpark_bf[:, :], wk[:, 0:16]).then_inc(env.sem_cV, 1)
        cV_cp = env.cV.add(1)

        with nc.Block() as blk:
            @blk.tensor
            def _(t):
                t.wait_ge(env.sem_cV, cV_cp)
                t.matmul(ps_c[0:16, :], cpark_bf[:, :], sb["ident"][:, :],
                         start=True, stop=True).then_inc(env.sem_cT, 1)
        cT_cp = env.cT.add(1)

        with nc.Block() as blk:
            @blk.vector
            def _(v):
                v.wait_ge(env.sem_cT, cT_cp)
                v.tensor_copy(crow_sb[:, :], ps_c[0:16, :]).then_inc(env.sem_cV, 1)
        cV_cr = env.cV.add(1)
        dG_cr0 = env.dG.n
        with nc.Block() as blk:
            @blk.gpsimd
            def _(g):
                g.wait_ge(env.sem_cV, cV_cr)
                g.dma_start(out=sb["crows"][0:1, 0:2048],
                            in_=crow_sb[:, :]).then_inc(env.sem_dG, 16)
                g.wait_ge(env.sem_dG, 16 * (dG_cr0 + 1))
        env.dG.add(1)

        cV1 = env.cV.n
        cT1 = env.cT.n
        cS1 = env.cS.n
        ps_t = [e(_ps(nc, f"xc_pt{i}", [128, 1024], F32)) for i in range(2)]
        with nc.Block() as blk:
            @blk.vector
            def _(v):
                for t in range(16):
                    if stats:
                        v.wait_ge(gsems[t // 4], 16 * (g0[t // 4] + 4))
                    if t >= 2:
                        v.wait_ge(env.sem_cT, cT1 + t - 1)
                    xt = xa[:, t * 1024:(t + 1) * 1024]
                    xo = xq[:, (t % 2) * 1024:(t % 2 + 1) * 1024]
                    for hf in range(2):
                        v.tensor_scalar(out=zz[:, hf * 512:(hf + 1) * 512],
                                        in0=xt[:, hf * 512:(hf + 1) * 512],
                                        scalar1=sb["parks"][:, P + t:P + t + 1],
                                        scalar2=sb["parks"][:, P + 48 + t:P + 49 + t],
                                        op0=ALU.subtract, op1=ALU.mult)
                    for hf in range(2):
                        pr = v.tensor_scalar(out=xo[:, hf * 512:(hf + 1) * 512],
                                             in0=zz[:, hf * 512:(hf + 1) * 512],
                                             scalar1=MAGIC, scalar2=MAGIC,
                                             op0=ALU.add, op1=ALU.subtract)
                    pr.then_inc(env.sem_cV, 1)

            @blk.tensor
            def _(t_):
                for t in range(16):
                    t_.wait_ge(env.sem_cV, cV1 + t + 1)
                    if t >= 2:
                        t_.wait_ge(env.sem_cS, cS1 + t - 1)
                    mm = None
                    for fi in range(8):
                        mm = t_.matmul(ps_t[t % 2][:, fi * 128:(fi + 1) * 128],
                                       xq[:, (t % 2) * 1024 + fi * 128:
                                          (t % 2) * 1024 + (fi + 1) * 128],
                                       sb["ident"][:, :], start=True, stop=True)
                    mm.then_inc(env.sem_cT, 1)

            @blk.scalar
            def _(s):
                for t in range(16):
                    s.wait_ge(env.sem_cT, cT1 + t + 1)
                    cp = None
                    for fi in range(8):
                        cp = s.copy(sb["xqT"][:, fi * 2048 + t * 128:
                                              fi * 2048 + (t + 1) * 128],
                                    ps_t[t % 2][:, fi * 128:(fi + 1) * 128])
                    cp.then_inc(env.sem_cS, 1)
        env.cV.add(16)
        env.cT.add(16)
        env.cS.add(16)
        env.join()


def emit_ffn(nc, env, dram, sb, res_src_name, out_dram_name, park_off,
             ln_stats=False):
    P = park_off

    with ExitStack() as stA:
        ea = stA.enter_context
        w1T = ea(_sb(nc, "fA_w1T", [128, 8 * 4096], BF16))
        xqT = sb["xqT"]
        h_raw = ea(_sb(nc, "fA_hraw", [128, 4096], F32))
        h_gel = ea(_sb(nc, "fA_hgel", [128, 2 * 4096], F32))
        hq = ea(_sb(nc, "fA_hq", [128, 4096], BF16))
        hred = ea(_sb(nc, "fA_hred", [128, 48], F32))
        hz = ea(_sb(nc, "fA_hz", [128, 4096], F32))
        hw = ea(_sb(nc, "fA_hw", [128, 64], F32))
        cpark_bf = ea(_sb(nc, "fA_cpb", [128, 16], BF16))
        crow_sb = ea(_sb(nc, "fA_crsb", [16, 128], BF16))
        ps_h = [ea(_ps(nc, f"fA_ps{i}", [128, 2048], F32)) for i in range(2)]
        sb["hredA"] = hred
        sb["hgelA"] = h_gel

        dY0, dG0, cT0, cS0, cV0 = env.dY.n, env.dG.n, env.cT.n, env.cS.n, env.cV.n
        agw_cc = env.agw_ready_cc
        with nc.Block() as blk:
            @blk.sync
            def _(y):
                y.wait_ge(env.sem_cc, agw_cc)
                w1T_v = w1T[:, :].rearrange("p (f g) -> p f g", g=4096)
                for r in range(8):
                    y.dma_start(
                        out=w1T_v[:, :, r * 512:(r + 1) * 512],
                        in_=dram["agw1_out"][r].rearrange("(f p) j -> p f j", p=128),
                    ).then_inc(env.sem_dY, 16)

            @blk.tensor
            def _(t):
                t.wait_ge(env.sem_dY, 16 * (dY0 + 8))
                for ti in range(16):
                    for s2 in range(2):
                        k = 2 * ti + s2
                        if k >= 2:
                            t2, sx = divmod(k - 2, 2)
                            t.wait_ge(env.sem_cS, cS0 + 5 * t2 + (1 if sx == 0 else 3))
                        ps = ps_h[k % 2]
                        for fi in range(8):
                            for nb in range(4):
                                fo = s2 * 2048 + nb * 512
                                t.matmul(ps[:, nb * 512:(nb + 1) * 512],
                                         xqT[:, fi * 2048 + ti * 128: fi * 2048 + (ti + 1) * 128],
                                         w1T[:, fi * 4096 + fo: fi * 4096 + fo + 512],
                                         start=(fi == 0), stop=False)
                        crow = sb["crows"][0:1, ti * 128:(ti + 1) * 128]
                        mm = None
                        for nb in range(4):
                            fo = s2 * 2048 + nb * 512
                            mm = t.matmul(ps[:, nb * 512:(nb + 1) * 512], crow,
                                          sb["b1_bf"][0:1, fo:fo + 512],
                                          start=False, stop=True)
                        mm.then_inc(env.sem_cT, 1)

            @blk.scalar
            def _(s):
                for ti in range(16):
                    if ti >= 2:
                        s.wait_ge(env.sem_cV, cV0 + ti - 1)
                    for s2 in range(2):
                        k = 2 * ti + s2
                        s.wait_ge(env.sem_cT, cT0 + k + 1)
                        ps = ps_h[k % 2]
                        hrs = h_raw[:, s2 * 2048:(s2 + 1) * 2048]
                        s.copy(hrs, ps[:, :]).then_inc(env.sem_cS, 1)
                        hgs = h_gel[:, (ti % 2) * 4096 + s2 * 2048:(ti % 2) * 4096 + (s2 + 1) * 2048]
                        s.activation(hgs, hrs, AF.Gelu, bias=0.0,
                                     scale=sb["parks"][:, P + 32 + ti:P + 33 + ti],
                                     ).then_inc(env.sem_cS, 1)
                    hg = h_gel[:, (ti % 2) * 4096:(ti % 2 + 1) * 4096]
                    s.activation(h_raw[:, :], hg, AF.Square, bias=0.0, scale=1.0,
                                 accum_out=hred[:, ti:ti + 1]).then_inc(env.sem_cS, 1)

            @blk.vector
            def _(v):
                for ti in range(16):
                    v.wait_ge(env.sem_cS, cS0 + 5 * ti + 4)
                    hg = h_gel[:, (ti % 2) * 4096:(ti % 2 + 1) * 4096]
                    rt(env, v, v.tensor_reduce(out=hred[:, 16 + ti:17 + ti], in_=hg,
                                               axis=AX.X, op=ALU.max,
                                               apply_absolute_value=True))
                    pr = v.tensor_scalar(out=hw[:, 0:1], in0=hred[:, 16 + ti:17 + ti],
                                         scalar1=1e-5, scalar2=None, op0=ALU.max)
                    rr = safe_recip(env, v, hw[:, 1:2], hw[:, 0:1], pr)
                    rt(env, v, rr)
                    rt(env, v, v.tensor_scalar(out=hw[:, 2:3], in0=hw[:, 1:2], scalar1=127.0,
                                               scalar2=None, op0=ALU.mult))
                    if ti >= 1:
                        v.wait_ge(env.sem_dG, 16 * (dG0 + ti))
                    v.tensor_scalar(out=hz[:, 0:2048], in0=hg[:, 0:2048],
                                    scalar1=hw[:, 2:3], scalar2=None, op0=ALU.mult)
                    v.tensor_scalar(out=hz[:, 2048:4096], in0=hg[:, 2048:4096],
                                    scalar1=hw[:, 2:3], scalar2=None, op0=ALU.mult)
                    v.tensor_scalar(out=hq[:, 0:2048], in0=hz[:, 0:2048],
                                    scalar1=MAGIC, scalar2=MAGIC, op0=ALU.add,
                                    op1=ALU.subtract)
                    v.tensor_scalar(out=hq[:, 2048:4096], in0=hz[:, 2048:4096],
                                    scalar1=MAGIC, scalar2=MAGIC, op0=ALU.add,
                                    op1=ALU.subtract).then_inc(env.sem_cV, 1)

            @blk.gpsimd
            def _(g):
                for ti in range(16):
                    g.wait_ge(env.sem_cV, cV0 + ti + 1)
                    g.dma_start(out=dram["hq_sc"][ti * 128:(ti + 1) * 128, :],
                                in_=hq[:, :]).then_inc(env.sem_dG, 16)
        env.dY.add(8)
        env.cT.add(32)
        env.cS.add(80)
        env.cV.add(16)
        env.dG.add(16)

        cV1 = env.cV.n
        cS_allsq = env.cS.n
        with nc.Block() as blk:
            @blk.vector
            def _(v):
                v.wait_ge(env.sem_cV, cV1)
                v.wait_ge(env.sem_cS, cS_allsq)
                pr = v.tensor_scalar(out=hw[:, 0:16], in0=hred[:, 0:16],
                                     scalar1=1.0 / 4096.0, scalar2=1e-5,
                                     op0=ALU.mult, op1=ALU.add)
                safe_recip(env, v, hw[:, 16:32], hw[:, 0:16], pr).then_inc(env.sem_cV, 1)
        cV_r = env.cV.add(1)

        with nc.Block() as blk:
            @blk.scalar
            def _(s):
                s.wait_ge(env.sem_cV, cV_r)
                s.activation(hw[:, 16:32], hw[:, 16:32], AF.Sqrt, bias=0.0,
                             scale=1.0).then_inc(env.sem_cS, 1)
        cS_r = env.cS.add(1)

        with nc.Block() as blk:
            @blk.vector
            def _(v):
                v.wait_ge(env.sem_cS, cS_r)
                rt(env, v, v.tensor_tensor(hw[:, 32:48], hw[:, 16:32], hred[:, 16:32],
                                           op=ALU.mult))
                rt(env, v, v.tensor_scalar(out=hw[:, 32:48], in0=hw[:, 32:48], scalar1=1e-5,
                                           scalar2=None, op0=ALU.max))
                rt(env, v, v.tensor_scalar(out=hw[:, 32:48], in0=hw[:, 32:48],
                                           scalar1=sb["wsc"][:, 3:4], scalar2=None,
                                           op0=ALU.mult))
                rt(env, v, v.tensor_scalar(out=hw[:, 32:48], in0=hw[:, 32:48],
                                           scalar1=1.0 / 127.0, scalar2=None,
                                           op0=ALU.mult))
                v.tensor_scalar(out=sb["parks"][:, P + 64:P + 80], in0=hw[:, 32:48],
                                scalar1=0.5, scalar2=None, op0=ALU.mult)
                rr = safe_recip(env, v, hw[:, 0:16], hw[:, 32:48],
                                v.tensor_scalar(out=hw[:, 48:49], in0=hw[:, 0:1],
                                                scalar1=0.0, scalar2=None, op0=ALU.add))
                rt(env, v, rr)
                v.tensor_copy(cpark_bf[:, :], hw[:, 0:16]).then_inc(env.sem_cV, 1)
        cV_cp = env.cV.add(1)

        with nc.Block() as blk:
            @blk.tensor
            def _(t):
                t.wait_ge(env.sem_cV, cV_cp)
                t.matmul(ps_h[0][0:16, 0:128], cpark_bf[:, :], sb["ident"][:, :],
                         start=True, stop=True).then_inc(env.sem_cT, 1)
        cT_cp = env.cT.add(1)

        with nc.Block() as blk:
            @blk.vector
            def _(v):
                v.wait_ge(env.sem_cT, cT_cp)
                v.tensor_copy(crow_sb[:, :], ps_h[0][0:16, 0:128]).then_inc(env.sem_cV, 1)
        cV_cr = env.cV.add(1)
        dG_cr0 = env.dG.n
        with nc.Block() as blk:
            @blk.gpsimd
            def _(g):
                g.wait_ge(env.sem_cV, cV_cr)
                g.dma_start(out=sb["crows"][0:1, 2048:4096],
                            in_=crow_sb[:, :]).then_inc(env.sem_dG, 16)
                g.wait_ge(env.sem_dG, 16 * (dG_cr0 + 1))
        env.dG.add(1)
        env.join()
    close_xqT(sb)

    with ExitStack() as stB:
        eb = stB.enter_context
        w2T = eb(_sb(nc, "fB_w2T", [128, 32 * 1024], BF16))
        hqT = eb(_sb(nc, "fB_hqT", [128, 2 * 16384], BF16))
        resb = eb(_sb(nc, "fB_res", [128, 2 * 1024], F32))
        outb = eb(_sb(nc, "fB_out", [128, 2 * 1024], F32))
        if ln_stats:
            sqs = eb(_sb(nc, "fB_sqs", [128, 1024], F32))
        ps_y = [eb(_ps(nc, f"fB_ps{i}", [128, 1024], F32)) for i in range(2)]

        dY0, cT0, cV0 = env.dY.n, env.cT.n, env.cV.n
        cS_sq0 = env.cS.n
        dYp0 = [env.dYp[0].n, env.dYp[1].n]
        dGp0 = [env.dGp[0].n, env.dGp[1].n]
        dSp0 = [env.dSp[0].n, env.dSp[1].n]
        res_src = dram[res_src_name]
        out_d = dram[out_dram_name]

        with nc.Block() as blk:
            @blk.sync
            def _(y):
                w2T_v = w2T[:, :].rearrange("p (f g) -> p f g", g=1024)
                for r in range(8):
                    srcv = dram["agw2_out"][r].rearrange("(f p) j -> p f j", p=128)
                    for q in range(4):
                        y.dma_start(
                            out=w2T_v[:, q * 8:(q + 1) * 8, r * 128:(r + 1) * 128],
                            in_=srcv[:, q * 8:(q + 1) * 8, :],
                        ).then_inc(env.sem_dY, 16)
                for tg in range(4):
                    if tg >= 2:
                        y.wait_ge(env.sem_cT, cT0 + 4 * (tg - 1))
                    for fi in range(32):
                        y.dma_start_transpose(
                            hqT[:, (tg % 2) * 16384 + fi * 512:(tg % 2) * 16384 + (fi + 1) * 512],
                            dram["hq_sc"][tg * 512:(tg + 1) * 512, fi * 128:(fi + 1) * 128],
                        ).then_inc(env.sem_dYp[tg % 2], 16)
            @blk.gpsimd
            def _(g):
                for t128 in range(18):
                    if t128 >= 2:
                        g.wait_ge(env.sem_cV, cV0 + t128 - 1)
                    if t128 < 16:
                        g.dma_start(out=resb[:, (t128 % 2) * 1024:(t128 % 2 + 1) * 1024],
                                    in_=res_src[t128 * 128:(t128 + 1) * 128, :],
                                    ).then_inc(env.sem_dGp[t128 % 2], 16)
                    if t128 >= 2:
                        ts_ = t128 - 2
                        g.dma_start(out=out_d[ts_ * 128:(ts_ + 1) * 128, :],
                                    in_=outb[:, (ts_ % 2) * 1024:(ts_ % 2 + 1) * 1024],
                                    ).then_inc(env.sem_dSp[ts_ % 2], 16)

            @blk.tensor
            def _(t):
                for tg in range(4):
                    t.wait_ge(env.sem_dY, 16 * (dY0 + 32))
                    t.wait_ge(env.sem_dYp[tg % 2], 16 * (dYp0[tg % 2] + 32 * (tg // 2 + 1)))
                    for tt in range(4):
                        t128 = tg * 4 + tt
                        if t128 >= 2:
                            t.wait_ge(env.sem_cV, cV0 + t128 - 1)
                        ps = ps_y[t128 % 2]
                        for fi in range(32):
                            lhs = hqT[:, (tg % 2) * 16384 + fi * 512 + tt * 128:
                                      (tg % 2) * 16384 + fi * 512 + (tt + 1) * 128]
                            t.matmul(ps[:, 0:512], lhs, w2T[:, fi * 1024:fi * 1024 + 512],
                                     start=(fi == 0), stop=False)
                            t.matmul(ps[:, 512:1024], lhs,
                                     w2T[:, fi * 1024 + 512:(fi + 1) * 1024],
                                     start=(fi == 0), stop=False)
                        crow = sb["crows"][0:1, 2048 + t128 * 128:2048 + (t128 + 1) * 128]
                        t.matmul(ps[:, 0:512], crow, sb["b2_bf"][0:1, 0:512],
                                 start=False, stop=True)
                        t.matmul(ps[:, 512:1024], crow, sb["b2_bf"][0:1, 512:1024],
                                 start=False, stop=True).then_inc(env.sem_cT, 1)

            @blk.vector
            def _(v):
                def red1(t):
                    v.tensor_reduce(out=sb["lnred"][:, t:t + 1],
                                    in_=outb[:, (t % 2) * 1024:(t % 2 + 1) * 1024],
                                    axis=AX.X, op=ALU.add)
                for t128 in range(16):
                    v.wait_ge(env.sem_cT, cT0 + t128 + 1)
                    v.wait_ge(env.sem_dGp[t128 % 2], 16 * (dGp0[t128 % 2] + t128 // 2 + 1))
                    if t128 >= 2:
                        v.wait_ge(env.sem_dSp[t128 % 2],
                                  16 * (dSp0[t128 % 2] + (t128 - 2) // 2 + 1))
                        if ln_stats:
                            v.wait_ge(env.sem_cS, cS_sq0 + t128 - 1)
                    ps = ps_y[t128 % 2]
                    v.scalar_tensor_tensor(
                        out=outb[:, (t128 % 2) * 1024:(t128 % 2 + 1) * 1024],
                        in0=ps[:, :],
                        scalar=sb["parks"][:, P + 64 + t128:P + 65 + t128],
                        in1=resb[:, (t128 % 2) * 1024:(t128 % 2 + 1) * 1024],
                        op0=ALU.mult, op1=ALU.add).then_inc(env.sem_cV, 1)
                    if ln_stats and t128 >= 1:
                        red1(t128 - 1)
                if ln_stats:
                    red1(15)

            if ln_stats:
                @blk.scalar
                def _(s):
                    for t in range(16):
                        s.wait_ge(env.sem_cV, cV0 + t + 1)
                        s.activation(sqs[:, :], outb[:, (t % 2) * 1024:(t % 2 + 1) * 1024],
                                     AF.Square, bias=0.0, scale=1.0,
                                     accum_out=sb["lnred"][:, 48 + t:49 + t],
                                     ).then_inc(env.sem_cS, 1)
        env.dY.add(32)
        env.dYp[0].add(64)
        env.dYp[1].add(64)
        env.dSp[0].add(8)
        env.dSp[1].add(8)
        env.dGp[0].add(8)
        env.dGp[1].add(8)
        env.cT.add(16)
        env.cV.add(16)
        if ln_stats:
            env.cS.add(16)
        env.join()


def emit_ln_to_bf(nc, env, dram, sb, src_name, dst_name):
    src, dst = dram[src_name], dram[dst_name]
    with ExitStack() as stl:
        e = stl.enter_context
        xa = e(_sb(nc, "ln_a", [128, 16 * 1024], F32))
        red = e(_sb(nc, "ln_red", [128, 48], F32))
        sq = e(_sb(nc, "ln_sq", [128, 1024], F32))
        ob = e(_sb(nc, "ln_o", [128, 2 * 1024], BF16))

        gsems = [env.sem_dYp[0], env.sem_dYp[1], env.sem_dOp[0], env.sem_dOp[1]]
        gctr = [env.dYp[0], env.dYp[1], env.dOp[0], env.dOp[1]]
        g0 = [c.n for c in gctr]
        with nc.Block() as blk:
            @blk.sync
            def _(y):
                for t in range(16):
                    y.dma_start(out=xa[:, t * 1024:(t + 1) * 1024],
                                in_=src[t * 128:(t + 1) * 128, :]).then_inc(gsems[t // 4], 16)
        for c in gctr:
            c.add(4)

        with nc.Block() as blk:
            @blk.vector
            def _(v):
                v.tensor_scalar(out=red[:, 0:16], in0=sb["lnred"][:, 0:16],
                                scalar1=1.0 / 1024.0, scalar2=None, op0=ALU.mult)
                rt(env, v, v.tensor_scalar(out=red[:, 16:32], in0=sb["lnred"][:, 48:64],
                                           scalar1=1.0 / 1024.0, scalar2=None,
                                           op0=ALU.mult))
                rt(env, v, v.tensor_tensor(red[:, 32:48], red[:, 0:16], red[:, 0:16],
                                           op=ALU.mult))
                rt(env, v, v.tensor_tensor(red[:, 16:32], red[:, 16:32], red[:, 32:48],
                                           op=ALU.subtract))
                v.tensor_scalar(out=red[:, 16:32], in0=red[:, 16:32], scalar1=1e-5,
                                scalar2=None, op0=ALU.add).then_inc(env.sem_cV, 1)
        cV_v = env.cV.add(1)

        with nc.Block() as blk:
            @blk.scalar
            def _(s):
                s.wait_ge(env.sem_cV, cV_v)
                s.activation(red[:, 16:32], red[:, 16:32], AF.Sqrt, bias=0.0,
                             scale=1.0).then_inc(env.sem_cS, 1)
        cS_s = env.cS.add(1)

        with nc.Block() as blk:
            @blk.vector
            def _(v):
                v.wait_ge(env.sem_cS, cS_s)
                v.reciprocal(red[:, 32:48], red[:, 16:32]).then_inc(env.sem_cV, 1)
        env.cV.add(1)

        cV1 = env.cV.n
        cT1 = env.cT.n
        cS1 = env.cS.n
        ps_t = [e(_ps(nc, f"ln_pt{i}", [128, 1024], F32)) for i in range(2)]
        with nc.Block() as blk:
            @blk.vector
            def _(v):
                for t in range(16):
                    v.wait_ge(gsems[t // 4], 16 * (g0[t // 4] + 4))
                    if t >= 2:
                        v.wait_ge(env.sem_cT, cT1 + t - 1)
                    v.tensor_scalar(out=ob[:, (t % 2) * 1024:(t % 2 + 1) * 1024],
                                    in0=xa[:, t * 1024:(t + 1) * 1024],
                                    scalar1=red[:, t:t + 1],
                                    scalar2=red[:, 32 + t:33 + t],
                                    op0=ALU.subtract, op1=ALU.mult).then_inc(env.sem_cV, 1)

            @blk.tensor
            def _(t_):
                for t in range(16):
                    t_.wait_ge(env.sem_cV, cV1 + t + 1)
                    if t >= 2:
                        t_.wait_ge(env.sem_cS, cS1 + t - 1)
                    mm = None
                    for fi in range(8):
                        mm = t_.matmul(ps_t[t % 2][:, fi * 128:(fi + 1) * 128],
                                       ob[:, (t % 2) * 1024 + fi * 128:
                                          (t % 2) * 1024 + (fi + 1) * 128],
                                       sb["ident"][:, :], start=True, stop=True)
                    mm.then_inc(env.sem_cT, 1)

            @blk.scalar
            def _(s):
                for t in range(16):
                    s.wait_ge(env.sem_cT, cT1 + t + 1)
                    cp = None
                    for fi in range(8):
                        cp = s.copy(sb["xqT"][:, fi * 2048 + t * 128:
                                              fi * 2048 + (t + 1) * 128],
                                    ps_t[t % 2][:, fi * 128:(fi + 1) * 128])
                    cp.then_inc(env.sem_cS, 1)
        env.cV.add(16)
        env.cT.add(16)
        env.cS.add(16)
        env.ln_evac_cS = cS1
        env.ln_tp_cT = cT1 + 16


def emit_agx(nc, env, dram, sb):
    dY0 = env.dY.n
    base = env.ln_evac_cS
    with nc.Block() as blk:
        @blk.sync
        def _(y):
            y.wait_ge(env.sem_cS, base + 8)
            for fi in range(8):
                y.dma_start(out=dram["agx_inA"][fi * 128:(fi + 1) * 128, :],
                            in_=sb["xqT"][:, fi * 2048:fi * 2048 + 1024],
                            ).then_inc(env.sem_dY, 16)
            y.wait_ge(env.sem_cS, base + 16)
            for fi in range(8):
                y.dma_start(out=dram["agx_inB"][fi * 128:(fi + 1) * 128, :],
                            in_=sb["xqT"][:, fi * 2048 + 1024:(fi + 1) * 2048],
                            ).then_inc(env.sem_dY, 16)

        @blk.gpsimd
        def _(g):
            g.wait_ge(env.sem_dY, 16 * (dY0 + 8))
            g.collective_compute(
                "AllGather", ALU.bypass, replica_groups=[list(range(NCORES))],
                ins=[dram["agx_inA"][:, :]], outs=[dram["agx_outA"][:, :, :]],
            ).then_inc(env.sem_cc, 1)
            g.wait_ge(env.sem_dY, 16 * (dY0 + 16))
            g.collective_compute(
                "AllGather", ALU.bypass, replica_groups=[list(range(NCORES))],
                ins=[dram["agx_inB"][:, :]], outs=[dram["agx_outB"][:, :, :]],
            ).then_inc(env.sem_cc, 1)
    env.dY.add(16)
    env.agxA_cc = env.cc.add(1)
    env.agxB_cc = env.cc.add(1)


def qkv_tile_of(k):
    if k < 64:
        return (k // 8) * 16 + (k % 8)
    return ((k - 64) // 8) * 16 + 8 + ((k - 64) % 8)


def emit_qkv(nc, env, st_att, dram, sb):
    e = st_att.enter_context
    sb["V"] = e(_sb(nc, "Vbig", [128, 128 * 130], BF16))
    sb["woT"] = e(_sb(nc, "woT_sb", [128, 8 * 1024], BF16))
    with ExitStack() as stl:
        el = stl.enter_context
        xbuf = el(_sb(nc, "qk_x", [128, 4 * 8192], BF16))
        sqd = el(_sb(nc, "qk_sqd", [128, 64], F32))
        nrm4 = el(_sb(nc, "qk_nrm4", [128, 96], F32))
        qkb = el(_sb(nc, "qk_qkb", [128, 2 * 256], BF16))
        ps = [el(_ps(nc, f"qk_ps{i}", [128, 384], F32)) for i in range(6)]

        lnS = env.ln_evac_cS + 16
        dY0, dG0, cT0, cV0, cS0 = env.dY.n, env.dG.n, env.cT.n, env.cV.n, env.cS.n

        with nc.Block() as blk:
            @blk.vector
            def _(v):
                v.wait_ge(env.sem_cS, lnS)
                vv = sb["V"][:, :].rearrange("p (t c) -> p t c", c=130)
                v.memset(vv[:, :, 64:65], 1.0)
                v.memset(vv[:, :, 129:130], 1.0)
                v.nop().then_inc(env.sem_cV, 1)

            @blk.gpsimd
            def _(g):
                g.wait_ge(env.sem_cS, lnS)
                for j in range(8):
                    g.dma_start(out=sb["woT"][:, j * 1024:(j + 1) * 1024],
                                in_=dram["woT"][j * 128:(j + 1) * 128, :]).then_inc(env.sem_dG, 16)
        env.cV.add(1)
        env.dG.add(8)
        cV0 = env.cV.n
        dG0 = env.dG.n

        NK = 128
        dYp0 = [env.dYp[0].n, env.dYp[1].n]
        dOp0 = [env.dOp[0].n, env.dOp[1].n]
        dGp0 = [env.dGp[0].n, env.dGp[1].n]
        with nc.Block() as blk:
            @blk.sync
            def _(y):
                y.wait_ge(env.sem_cS, lnS)
                y.wait_ge(env.sem_cc, env.agxA_cc)
                for r in range(8):
                    if r >= 2:
                        y.wait_ge(env.sem_cT, cT0 + 8 * (r - 1))
                    for fi in range(8):
                        y.dma_start(out=xbuf[:, (r % 2) * 8192 + fi * 1024:
                                             (r % 2) * 8192 + (fi + 1) * 1024],
                                    in_=dram["agx_outA"][r, fi * 128:(fi + 1) * 128, :],
                                    ).then_inc(env.sem_dYp[r % 2], 16)
                y.wait_ge(env.sem_cc, env.agxB_cc)
                for r in range(8):
                    if r >= 2:
                        y.wait_ge(env.sem_cT, cT0 + 64 + 8 * (r - 1))
                    for fi in range(8):
                        y.dma_start(out=xbuf[:, 16384 + (r % 2) * 8192 + fi * 1024:
                                             16384 + (r % 2) * 8192 + (fi + 1) * 1024],
                                    in_=dram["agx_outB"][r, fi * 128:(fi + 1) * 128, :],
                                    ).then_inc(env.sem_dOp[r % 2], 16)

            @blk.tensor
            def _(t):
                for k in range(NK):
                    if k < 64:
                        r, tc = k // 8, k % 8
                        if tc == 0:
                            t.wait_ge(env.sem_dYp[r % 2],
                                      16 * (dYp0[r % 2] + 8 * (r // 2) + 8))
                        col = (r % 2) * 8192 + tc * 128
                    else:
                        r, tc = (k - 64) // 8, (k - 64) % 8
                        if tc == 0:
                            t.wait_ge(env.sem_dOp[r % 2],
                                      16 * (dOp0[r % 2] + 8 * (r // 2) + 8))
                        col = 16384 + (r % 2) * 8192 + tc * 128
                    if k >= 6:
                        t.wait_ge(env.sem_cV, cV0 + (k - 6) + 1)
                    p = ps[k % 6]
                    for fi in range(8):
                        t.matmul(p[:, :],
                                 xbuf[:, col + fi * 1024:col + fi * 1024 + 128],
                                 sb["wqkvT"][:, fi * 384:(fi + 1) * 384],
                                 start=(fi == 0), stop=False)
                    t.matmul(p[:, :], sb["ones_f32r"][0:1, :],
                             sb["bqkv_bc"][0:1, :], start=False,
                             stop=True).then_inc(env.sem_cT, 1)

            @blk.scalar
            def _(s):
                def lnexp(kk):
                    sl = (kk % 6) * 16
                    s.activation(nrm4[:, sl + 4:sl + 8], nrm4[:, sl:sl + 4],
                                 AF.Ln, bias=0.0, scale=1.0)
                    s.activation(nrm4[:, sl + 8:sl + 12], nrm4[:, sl + 4:sl + 8],
                                 AF.Exp, bias=0.0,
                                 scale=-0.5).then_inc(env.sem_cS, 1)
                for k in range(NK):
                    s.wait_ge(env.sem_cT, cT0 + k + 1)
                    p = ps[k % 6]
                    sl = (k % 6) * 16
                    for h in range(4):
                        s.activation(sqd[:, :], p[:, h * 64:(h + 1) * 64],
                                     AF.Square, bias=0.0, scale=1.0,
                                     accum_out=nrm4[:, sl + h:sl + h + 1])
                    if k >= 1:
                        lnexp(k - 1)
                lnexp(NK - 1)

            @blk.vector
            def _(v):
                for k in range(NK):
                    v.wait_ge(env.sem_cS, cS0 + k + 1)
                    if k >= 2:
                        v.wait_ge(env.sem_dGp[k % 2], 16 * (dGp0[k % 2] + 2 * (k // 2)))
                    p = ps[k % 6]
                    sl = (k % 6) * 16
                    qb = qkb[:, (k % 2) * 256:(k % 2) * 256 + 256]
                    for h in range(4):
                        v.tensor_scalar(out=qb[:, h * 64:(h + 1) * 64],
                                        in0=p[:, h * 64:(h + 1) * 64],
                                        scalar1=nrm4[:, sl + 8 + h:sl + 9 + h],
                                        scalar2=None, op0=ALU.mult)
                    base = qkv_tile_of(k) * 130
                    v.tensor_copy(sb["V"][:, base:base + 64], p[:, 256:320])
                    v.tensor_copy(sb["V"][:, base + 65:base + 129],
                                  p[:, 320:384]).then_inc(env.sem_cV, 1)

            @blk.gpsimd
            def _(g):
                for k in range(NK):
                    g.wait_ge(env.sem_cV, cV0 + k + 1)
                    qb = qkb[:, (k % 2) * 256:(k % 2) * 256 + 256]
                    gt = qkv_tile_of(k)
                    g.dma_start(out=dram["qh_sc"][gt * 128:(gt + 1) * 128, :],
                                in_=qb[:, 0:128]).then_inc(env.sem_dGp[k % 2], 16)
                    g.dma_start(out=dram["kh_sc"][gt * 128:(gt + 1) * 128, :],
                                in_=qb[:, 128:256]).then_inc(env.sem_dGp[k % 2], 16)
        env.dYp[0].add(32)
        env.dYp[1].add(32)
        env.dOp[0].add(32)
        env.dOp[1].add(32)
        env.cT.add(NK)
        env.cV.add(NK)
        env.cS.add(NK)
        env.dGp[0].add(NK)
        env.dGp[1].add(NK)
        env.join()


def emit_attention(nc, env, st_att, dram, sb):
    stl = ExitStack()
    e = stl.enter_context
    qT = e(_sb(nc, "at_qT", [128, 2 * 4096], BF16))
    kT = e(_sb(nc, "at_kT", [128, 2 * 4096], BF16))
    pbf = e(_sb(nc, "at_p", [128, 3 * 1024], BF16))
    lnr = e(_sb(nc, "at_ln", [1, 1024], F32))
    rbf = e(_sb(nc, "at_rb", [1, 1024], BF16))
    oev = e(_sb(nc, "at_oe", [128, 512], F32))
    outT = e(_sb(nc, "at_outT", [128, 4096], BF16))
    ps_s = [e(_ps(nc, f"at_ps{i}", [128, 1024], F32)) for i in range(2)]
    ps_pv = e(_ps(nc, "at_pv", [128, 1024], F32))
    ps_bc = e(_ps(nc, "at_bc", [128, 512], F32))

    waves = []
    for b in range(4):
        for qc in range(8):
            for kt in range(4 * qc + 4):
                waves.append((b, qc, kt))
    NW = len(waves)
    widx = {w: i for i, w in enumerate(waves)}

    dY0, dG0 = env.dY.n, env.dG.n
    cT = Counter(env.cT.n)
    cV = Counter(env.cV.n)
    cS = Counter(env.cS.n)

    score_done = {}
    pv_done = {}
    bcast_done = {}
    exp_done = {}
    mask_done = {}
    tail1_done = {}
    evac_done = {}

    LAG = 2
    BDELAY = 4
    tev = []
    pending = []
    pvn = 0

    def _push_pv(wp):
        tev.append(("pv", wp))
        if wp[2] == 4 * wp[1] + 3:
            bd = 0 if wp[1] == 7 else BDELAY
            pending.append((pvn + 1 + bd, (wp[0], wp[1])))

    for i, w in enumerate(waves):
        tev.append(("score", w))
        if i >= LAG:
            _push_pv(waves[i - LAG])
            pvn += 1
            while pending and pending[0][0] <= pvn:
                tev.append(("bcast", pending.pop(0)[1]))
    for j in range(LAG, 0, -1):
        _push_pv(waves[-j])
        pvn += 1
        while pending and pending[0][0] <= pvn:
            tev.append(("bcast", pending.pop(0)[1]))
    while pending:
        tev.append(("bcast", pending.pop(0)[1]))

    for kind, x in tev:
        if kind == "score":
            score_done[x] = cT.add(1)
        elif kind == "pv":
            pv_done[x] = cT.add(1)
        else:
            bcast_done[x] = cT.add(1)

    tailS_done = {}
    for w in waves:
        exp_done[w] = cS.add(1)
        if w[2] == 4 * w[1] + 3:
            tailS_done[(w[0], w[1])] = cS.add(1)

    for w in waves:
        b, qc, kt = w
        if kt >= 4 * qc:
            mask_done[w] = cV.add(1)
        if kt == 4 * qc + 3:
            tail1_done[(b, qc)] = cV.add(1)
            evac_done[(b, qc)] = cV.add(1)

    def p_ready(w):
        b, qc, kt = w
        if kt >= 4 * qc:
            return (env.sem_cV, mask_done[w])
        return (env.sem_cS, exp_done[w])

    ag_dma_done = {b: 16 * (dG0 + b + 1) for b in range(4)}
    cc0 = env.cc.n
    ag_cc = {b: cc0 + b + 1 for b in range(4)}

    dYp0 = [env.dYp[0].n, env.dYp[1].n]
    trans_done = {b: (env.sem_dYp[b % 2], 16 * (dYp0[b % 2] + 2 * (b // 2 + 1)))
                  for b in range(4)}

    with nc.Block() as blk:
        @blk.sync
        def _(y):
            for b in range(4):
                if b >= 2:
                    last = (b - 2, 7, 31)
                    y.wait_ge(env.sem_cT, score_done[last])
                y.dma_start_transpose(qT[:, (b % 2) * 4096:(b % 2 + 1) * 4096],
                                      dram["qh_sc"][b * 4096:(b + 1) * 4096, :],
                                      ).then_inc(env.sem_dYp[b % 2], 16)
                y.dma_start_transpose(kT[:, (b % 2) * 4096:(b % 2 + 1) * 4096],
                                      dram["kh_sc"][b * 4096:(b + 1) * 4096, :],
                                      ).then_inc(env.sem_dYp[b % 2], 16)

        @blk.tensor
        def _(t):
            def emit_score(w):
                b, qc, kt = w
                i = widx[w]
                if i == 0 or waves[i - 1][0] != b:
                    t.wait_ge(*trans_done[b])
                if i >= 2:
                    t.wait_ge(env.sem_cS, exp_done[waves[i - 2]])
                sl = i % 2
                koff = (b % 2) * 4096 + kt * 128
                qoff = (b % 2) * 4096 + qc * 512
                t.matmul(ps_s[sl][:, 0:512], kT[0:64, koff:koff + 128],
                         qT[0:64, qoff:qoff + 512], start=True, stop=True)
                t.matmul(ps_s[sl][:, 512:1024], kT[64:128, koff:koff + 128],
                         qT[64:128, qoff:qoff + 512], start=True,
                         stop=True).then_inc(env.sem_cT, 1)

            def emit_pv(w):
                b, qc, kt = w
                i = widx[w]
                sem, val = p_ready(w)
                t.wait_ge(sem, val)
                if kt == 0 and (b, qc) != (0, 0):
                    pb, pqc = (b, qc - 1) if qc > 0 else (b - 1, 7)
                    t.wait_ge(env.sem_cV, tail1_done[(pb, pqc)])
                    t.wait_ge(env.sem_cS, tailS_done[(pb, pqc)])
                sl = i % 3
                gt = b * 32 + kt
                st_ = (kt == 0)
                sp = (kt == 4 * qc + 3)
                t.matmul(ps_pv[0:65, 0:512], sb["V"][:, gt * 130:gt * 130 + 65],
                         pbf[:, sl * 1024:sl * 1024 + 512], start=st_, stop=sp)
                t.matmul(ps_pv[0:65, 512:1024], sb["V"][:, gt * 130 + 65:gt * 130 + 130],
                         pbf[:, sl * 1024 + 512:sl * 1024 + 1024], start=st_,
                         stop=sp).then_inc(env.sem_cT, 1)

            def emit_bcast(b, qc):
                t.wait_ge(env.sem_cS, tailS_done[(b, qc)])
                t.wait_ge(env.sem_cV, tail1_done[(b, qc)])
                t.matmul(ps_bc[:, :], sb["mrow_lo"][0:1, :], rbf[0:1, 0:512],
                         start=True, stop=False)
                t.matmul(ps_bc[:, :], sb["mrow_hi"][0:1, :], rbf[0:1, 512:1024],
                         start=False, stop=True).then_inc(env.sem_cT, 1)

            for kind, x in tev:
                if kind == "score":
                    emit_score(x)
                elif kind == "pv":
                    emit_pv(x)
                else:
                    emit_bcast(*x)

        @blk.scalar
        def _(s):
            for i, w in enumerate(waves):
                s.wait_ge(env.sem_cT, score_done[w])
                if i >= 3:
                    s.wait_ge(env.sem_cT, pv_done[waves[i - 3]])
                s.activation(pbf[:, (i % 3) * 1024:(i % 3 + 1) * 1024],
                             ps_s[i % 2][:, :], AF.Exp, bias=0.0,
                             scale=COEF).then_inc(env.sem_cS, 1)
                if w[2] == 4 * w[1] + 3:
                    s.wait_ge(env.sem_cT, pv_done[w])
                    s.activation(lnr[0:1, :], ps_pv[64:65, :], AF.Ln,
                                 bias=0.0, scale=1.0)
                    s.activation(rbf[0:1, :], lnr[0:1, :], AF.Exp, bias=0.0,
                                 scale=-1.0).then_inc(env.sem_cS, 1)

        @blk.vector
        def _(v):
            for i, w in enumerate(waves):
                b, qc, kt = w
                sl = i % 3
                if kt >= 4 * qc:
                    o = kt - 4 * qc
                    v.wait_ge(env.sem_cS, exp_done[w])
                    v.tensor_tensor(pbf[:, sl * 1024:sl * 1024 + 512],
                                    pbf[:, sl * 1024:sl * 1024 + 512],
                                    sb["masks"][:, o * 512:(o + 1) * 512], op=ALU.mult)
                    v.tensor_tensor(pbf[:, sl * 1024 + 512:(sl + 1) * 1024],
                                    pbf[:, sl * 1024 + 512:(sl + 1) * 1024],
                                    sb["masks"][:, o * 512:(o + 1) * 512],
                                    op=ALU.mult).then_inc(env.sem_cV, 1)
                if kt == 4 * qc + 3:
                    v.wait_ge(env.sem_cT, pv_done[w])
                    if qc == 0 and b >= 1:
                        v.wait_ge(env.sem_dG, ag_dma_done[b - 1])
                    v.tensor_copy(oev[0:64, :], ps_pv[0:64, 0:512])
                    v.tensor_copy(oev[64:128, :],
                                  ps_pv[0:64, 512:1024]).then_inc(env.sem_cV, 1)
                    v.wait_ge(env.sem_cT, bcast_done[(b, qc)])
                    v.tensor_tensor(outT[:, qc * 512:(qc + 1) * 512], oev[:, :],
                                    ps_bc[:, :], op=ALU.mult).then_inc(env.sem_cV, 1)

        @blk.gpsimd
        def _(g):
            for b in range(4):
                g.wait_ge(env.sem_cV, evac_done[(b, 7)])
                g.dma_start(out=dram["ago_in"][b, :, :], in_=outT[:, :],
                            ).then_inc(env.sem_dG, 16)
                g.wait_ge(env.sem_dG, ag_dma_done[b])
                g.collective_compute(
                    "AllGather", ALU.bypass, replica_groups=[list(range(NCORES))],
                    ins=[dram["ago_in"][b, :, :]], outs=[dram["ago_out"][b, :, :, :]],
                ).then_inc(env.sem_cc, 1)

    env.dYp[0].add(4)
    env.dYp[1].add(4)
    env.dG.add(4)
    env.cT.n = cT.n
    env.cS.n = cS.n
    env.cV.n = cV.n
    env.cc.add(4)
    env.ago_ready_cc = env.cc.n
    env.join()
    stl.close()


def emit_wo(nc, env, st_att, dram, sb):
    with ExitStack() as stl:
        e = stl.enter_context
        om = e(_sb(nc, "wo_om", [128, 8 * 2048], BF16))
        x1b = e(_sb(nc, "wo_x1", [128, 2 * 1024], F32))
        x2b = e(_sb(nc, "wo_x2", [128, 2 * 1024], F32))
        sqs = e(_sb(nc, "wo_sqs", [128, 1024], F32))
        ps_y = [e(_ps(nc, f"wo_ps{i}", [128, 1024], F32)) for i in range(4)]

        dY0, dG0, cT0, cV0 = env.dY.n, env.dG.n, env.cT.n, env.cV.n
        cS_sq0 = env.cS.n
        dYp0 = [env.dYp[0].n, env.dYp[1].n]
        dGp0 = [env.dGp[0].n, env.dGp[1].n]
        ago_cc = env.ago_ready_cc
        with nc.Block() as blk:
            @blk.gpsimd
            def _(g):
                g.wait_ge(env.sem_cc, ago_cc)
                with g.register("rk") as rk, g.register("r2") as r2, \
                        g.register("rh") as rh, g.register("ro") as ro:
                    g.cc_rank_ld(rk, replica_groups=[list(range(NCORES))])
                    g.reg_div(r2, rk, 2)
                    g.reg_mul(r2, r2, 8 * 128 * 4096)
                    g.reg_mod(rh, rk, 2)
                    g.reg_mul(rh, rh, 2048)
                    g.reg_add(r2, r2, rh)
                    for r in range(8):
                        g.reg_mov(ro, r * 128 * 4096)
                        g.reg_add(ro, ro, r2)
                        g.dma_start(
                            out=om[:, r * 2048:(r + 1) * 2048],
                            in_=bass.AP(dram["ago_out"], ro, [[4096, 128], [1, 2048]]),
                        ).then_inc(env.sem_dG, 16)
                for t128 in range(16):
                    g.wait_ge(env.sem_cV, cV0 + t128 + 1)
                    g.dma_start(out=dram["x2_sp"][t128 * 128:(t128 + 1) * 128, :],
                                in_=x2b[:, (t128 % 2) * 1024:(t128 % 2 + 1) * 1024],
                                ).then_inc(env.sem_dGp[t128 % 2], 16)

            @blk.sync
            def _(y):
                for t128 in range(16):
                    if t128 >= 2:
                        y.wait_ge(env.sem_cV, cV0 + t128 - 1)
                    y.dma_start(out=x1b[:, (t128 % 2) * 1024:(t128 % 2 + 1) * 1024],
                                in_=dram["x1_sp"][t128 * 128:(t128 + 1) * 128, :],
                                ).then_inc(env.sem_dYp[t128 % 2], 16)

            @blk.tensor
            def _(t):
                t.wait_ge(env.sem_dG, 16 * (dG0 + 8))
                for t128 in range(16):
                    if t128 >= 4:
                        t.wait_ge(env.sem_cV, cV0 + t128 - 3)
                    ps = ps_y[t128 % 4]
                    for r in range(8):
                        lhs = om[:, r * 2048 + t128 * 128: r * 2048 + (t128 + 1) * 128]
                        t.matmul(ps[:, 0:512], lhs, sb["woT"][:, r * 1024:r * 1024 + 512],
                                 start=(r == 0), stop=False)
                        t.matmul(ps[:, 512:1024], lhs,
                                 sb["woT"][:, r * 1024 + 512:(r + 1) * 1024],
                                 start=(r == 0), stop=False)
                    t.matmul(ps[:, 0:512], sb["ones_row"][0:1, :],
                             sb["halfbo_bf"][0:1, 0:512], start=False, stop=True)
                    t.matmul(ps[:, 512:1024], sb["ones_row"][0:1, :],
                             sb["halfbo_bf"][0:1, 512:1024], start=False,
                             stop=True).then_inc(env.sem_cT, 1)

            @blk.vector
            def _(v):
                def red3(t):
                    xs = x2b[:, (t % 2) * 1024:(t % 2 + 1) * 1024]
                    v.tensor_reduce(out=sb["lnred"][:, t:t + 1], in_=xs,
                                    axis=AX.X, op=ALU.add)
                    v.tensor_reduce(out=sb["lnred"][:, 16 + t:17 + t], in_=xs,
                                    axis=AX.X, op=ALU.max)
                    v.tensor_reduce(out=sb["lnred"][:, 32 + t:33 + t], in_=xs,
                                    axis=AX.X, op=ALU.min)
                for t128 in range(16):
                    v.wait_ge(env.sem_cT, cT0 + t128 + 1)
                    v.wait_ge(env.sem_dYp[t128 % 2], 16 * (dYp0[t128 % 2] + t128 // 2 + 1))
                    if t128 >= 2:
                        v.wait_ge(env.sem_dGp[t128 % 2],
                                  16 * (dGp0[t128 % 2] + (t128 - 2) // 2 + 1))
                        v.wait_ge(env.sem_cS, cS_sq0 + t128 - 1)
                    v.scalar_tensor_tensor(
                        out=x2b[:, (t128 % 2) * 1024:(t128 % 2 + 1) * 1024],
                        in0=ps_y[t128 % 4][:, :], scalar=0.5,
                        in1=x1b[:, (t128 % 2) * 1024:(t128 % 2 + 1) * 1024],
                        op0=ALU.mult, op1=ALU.add).then_inc(env.sem_cV, 1)
                    if t128 >= 1:
                        red3(t128 - 1)
                red3(15)

            @blk.scalar
            def _(s):
                for t in range(16):
                    s.wait_ge(env.sem_cV, cV0 + t + 1)
                    s.activation(sqs[:, :], x2b[:, (t % 2) * 1024:(t % 2 + 1) * 1024],
                                 AF.Square, bias=0.0, scale=1.0,
                                 accum_out=sb["lnred"][:, 48 + t:49 + t],
                                 ).then_inc(env.sem_cS, 1)
        env.dYp[0].add(8)
        env.dYp[1].add(8)
        env.dG.add(8)
        env.dGp[0].add(8)
        env.dGp[1].add(8)
        env.cT.add(16)
        env.cV.add(16)
        env.cS.add(16)
        env.join()


def build_program(stop_after=None, debug_dump=None):
    nc = bass.Bass("TRN2", target_bir_lowering=False, detect_race_conditions=False)
    dram = {}
    dram["x0"] = nc.dram_tensor("x0", [2048, 1024], F32, kind="ExternalInput")
    dram["wqkvT"] = nc.dram_tensor("wqkvT", [1024, 384], F32, kind="ExternalInput")
    dram["bqkv"] = nc.dram_tensor("bqkv", [1, 384], F32, kind="ExternalInput")
    dram["woT"] = nc.dram_tensor("woT", [1024, 1024], F32, kind="ExternalInput")
    dram["bo_h"] = nc.dram_tensor("bo_h", [1, 1024], F32, kind="ExternalInput")
    dram["w1Tsh"] = nc.dram_tensor("w1Tsh", [1024, 512], F32, kind="ExternalInput")
    dram["b1r"] = nc.dram_tensor("b1r", [1, 4096], F32, kind="ExternalInput")
    dram["w2Tsh"] = nc.dram_tensor("w2Tsh", [4096, 128], F32, kind="ExternalInput")
    dram["b2r"] = nc.dram_tensor("b2r", [1, 1024], F32, kind="ExternalInput")
    dram["out"] = nc.dram_tensor("out", [2048, 1024], F32, kind="ExternalOutput")

    dram["arw_in"] = nc.dram_tensor("arw_in", [1, 8], F32)
    dram["arw_out"] = nc.dram_tensor("arw_out", [1, 8], F32)
    dram["agw1_in"] = nc.dram_tensor("agw1_in", [1024, 512], BF16)
    dram["agw1_out"] = nc.dram_tensor("agw1_out", [8, 1024, 512], BF16, addr_space="Shared")
    dram["agw2_in"] = nc.dram_tensor("agw2_in", [4096, 128], BF16)
    dram["agw2_out"] = nc.dram_tensor("agw2_out", [8, 4096, 128], BF16, addr_space="Shared")
    dram["xq_sc"] = nc.dram_tensor("xq_sc", [2048, 1024], BF16)
    dram["hq_sc"] = nc.dram_tensor("hq_sc", [2048, 4096], BF16)
    dram["agx_inA"] = nc.dram_tensor("agx_inA", [1024, 1024], BF16)
    dram["agx_inB"] = nc.dram_tensor("agx_inB", [1024, 1024], BF16)
    dram["agx_outA"] = nc.dram_tensor("agx_outA", [8, 1024, 1024], BF16, addr_space="Shared")
    dram["agx_outB"] = nc.dram_tensor("agx_outB", [8, 1024, 1024], BF16, addr_space="Shared")
    dram["qh_sc"] = nc.dram_tensor("qh_sc", [16384, 128], BF16)
    dram["kh_sc"] = nc.dram_tensor("kh_sc", [16384, 128], BF16)
    dram["ago_in"] = nc.dram_tensor("ago_in", [4, 128, 4096], BF16)
    dram["ago_out"] = nc.dram_tensor("ago_out", [4, 8, 128, 4096], BF16, addr_space="Shared")
    dram["x1_sp"] = nc.dram_tensor("x1_sp", [2048, 1024], F32)
    dram["x2_sp"] = nc.dram_tensor("x2_sp", [2048, 1024], F32)

    with ExitStack() as st:
        env = Env(nc, st)
        sb = {}
        sb["ones_row"] = st.enter_context(_sb(nc, "ones_row", [1, 128], BF16))
        emit_setup(nc, env, st, dram, sb)
        with nc.Block() as blk:
            @blk.vector
            def _(v):
                v.memset(sb["ones_row"][:, :], 1.0).then_inc(env.sem_cV, 1)
        env.cV.add(1)

        def _fin():
            if debug_dump is not None:
                env.join()
                with nc.Block() as blk:
                    @blk.gpsimd
                    def _(g):
                        if debug_dump.startswith("@"):
                            src_s = sb[debug_dump[1:]]
                            w_ = src_s.shape[1]
                            if w_ <= 1024:
                                g.dma_start(out=dram["out"][0:128, 0:w_],
                                            in_=src_s[:, :]).then_inc(env.sem_dG, 16)
                            else:
                                nrow = min(2048, (128 * w_) // 1024)
                                g.dma_start(
                                    out=dram["out"][0:nrow, :],
                                    in_=src_s[:, 0:(nrow * 1024) // 128],
                                    ).then_inc(env.sem_dG, 16)
                        else:
                            nd = dram[debug_dump]
                            tot = 1
                            for d_ in nd.shape:
                                tot *= d_
                            nr = min(2048, tot // 1024)
                            g.dma_start(out=dram["out"][0:nr, :],
                                        in_=bass.AP(nd, 0, [[1024, nr], [1, 1024]]),
                                        ).then_inc(env.sem_dG, 16)
                env.dG.add(1)
            with nc.Block() as blk:
                @blk.sync
                def _(y):
                    y.wait_ge(env.sem_dY, 16 * env.dY.n)

                @blk.gpsimd
                def _(g):
                    g.wait_ge(env.sem_dG, 16 * env.dG.n)
                    g.wait_ge(env.sem_cc, env.cc.n)

        emit_xchain(nc, env, st, dram, sb, "x0", park_off=0,
                    mid_emit=lambda: emit_setup_tail(nc, env, dram, sb))
        close_wtern(sb)
        if stop_after == "xchain1":
            _fin(); close_xqT(sb); return nc
        emit_ffn(nc, env, dram, sb, "x0", "x1_sp", park_off=0, ln_stats=True)
        if stop_after == "ffn1":
            _fin(); return nc
        open_xqT(nc, sb)
        emit_ln_to_bf(nc, env, dram, sb, "x1_sp", "xq_sc")
        if stop_after == "ln2":
            _fin(); close_xqT(sb); return nc
        emit_agx(nc, env, dram, sb)
        if stop_after == "agx":
            _fin(); close_xqT(sb); return nc
        stop_now = False
        with ExitStack() as st_att:
            emit_qkv(nc, env, st_att, dram, sb)
            if stop_after == "qkv":
                _fin(); stop_now = True
            if not stop_now:
                emit_attention(nc, env, st_att, dram, sb)
                if stop_after == "attn":
                    _fin(); stop_now = True
            if not stop_now:
                emit_wo(nc, env, st_att, dram, sb)
                env.join()
                if stop_after == "wo":
                    _fin(); stop_now = True
        close_xqT(sb)
        if stop_now:
            return nc
        open_xqT(nc, sb)
        emit_xchain(nc, env, st, dram, sb, "x2_sp", park_off=0, stats=True)
        if stop_after == "xchain3":
            _fin(); close_xqT(sb); return nc
        emit_ffn(nc, env, dram, sb, "x2_sp", "out", park_off=0)

        with nc.Block() as blk:
            @blk.sync
            def _(y):
                y.wait_ge(env.sem_dY, 16 * env.dY.n)

            @blk.gpsimd
            def _(g):
                g.wait_ge(env.sem_dG, 16 * env.dG.n)
    return nc


_CACHE = {}


def kernel(L0_attn_x, L0_attn_Wq, L0_attn_bq, L0_attn_Wk, L0_attn_bk,
           L0_attn_Wv, L0_attn_bv, L0_attn_Wo, L0_attn_bo,
           L0_ffn_w1, L0_ffn_b1, L0_ffn_w2, L0_ffn_b2):
    x = np.asarray(L0_attn_x, dtype=np.float32)
    Wq = np.asarray(L0_attn_Wq, np.float32)
    Wk = np.asarray(L0_attn_Wk, np.float32)
    Wv = np.asarray(L0_attn_Wv, np.float32)
    Wo = np.asarray(L0_attn_Wo, np.float32)
    bq = np.asarray(L0_attn_bq, np.float32)
    bk = np.asarray(L0_attn_bk, np.float32)
    bv = np.asarray(L0_attn_bv, np.float32)
    bo = np.asarray(L0_attn_bo, np.float32)
    w1 = np.asarray(L0_ffn_w1, np.float32)
    b1 = np.asarray(L0_ffn_b1, np.float32)
    w2 = np.asarray(L0_ffn_w2, np.float32)
    b2 = np.asarray(L0_ffn_b2, np.float32)

    if "nc" not in _CACHE:
        _CACHE["nc"] = build_program()
    nc = _CACHE["nc"]
    in_maps = make_in_maps(x, Wq, bq, Wk, bk, Wv, bv, Wo, bo, w1, b1, w2, b2)
    res = run_bass_kernel_spmd(nc, in_maps, core_ids=list(range(NCORES)))
    out = np.empty((4, 4096, 1024), np.float32)
    for c in range(NCORES):
        out[c // 2, (c % 2) * 2048:((c % 2) + 1) * 2048, :] = res.results[c]["out"]
    return out


def make_in_maps(x, Wq, bq, Wk, bk, Wv, bv, Wo, bo, w1, b1, w2, b2):
    w1T = np.ascontiguousarray(w1.T)
    w2T = np.ascontiguousarray(w2.T)
    woT = np.ascontiguousarray(Wo.T)
    in_maps = []
    for c in range(NCORES):
        sbq, half = c // 2, c % 2
        hsl = slice(128 * c, 128 * (c + 1))
        in_maps.append({
            "x0": np.ascontiguousarray(x[sbq, half * 2048:(half + 1) * 2048, :]),
            "wqkvT": np.ascontiguousarray(
                np.concatenate([Wq[hsl], Wk[hsl], Wv[hsl]], axis=0).T),
            "bqkv": np.concatenate([bq[hsl], bk[hsl], bv[hsl]])[None, :].copy(),
            "woT": woT,
            "bo_h": bo[None, :].copy(),
            "w1Tsh": np.ascontiguousarray(w1T[:, 512 * c:512 * (c + 1)]),
            "b1r": b1[None, :].copy(),
            "w2Tsh": np.ascontiguousarray(w2T[:, 128 * c:128 * (c + 1)]),
            "b2r": b2[None, :].copy(),
        })
    return in_maps
